# version 36
# speedup vs baseline: 1.1672x; 1.0056x over previous
import math
import contextlib
import numpy as np
import ml_dtypes

import concourse.bass as bass
import concourse.mybir as mybir
from concourse.bass_utils import run_bass_kernel_spmd

F32 = mybir.dt.float32
BF16 = mybir.dt.bfloat16
AF = mybir.ActivationFunctionType
ALU = mybir.AluOpType
AX = mybir.AxisListType
BF = ml_dtypes.bfloat16

NCORES = 8
L0_STAGE = 6
T = 2048
D = 1024
NT = T // 128
ND = D // 128

ENGINES = ("sync", "tensor", "vector", "scalar", "gpsimd")


class Sem:
    def __init__(self, handle):
        self.h = handle
        self.n = 0

    def inc(self, k=1):
        self.n += k
        return self.n


class PPSem:

    def __init__(self, P, n=2):
        self.sems = [P.sem(dma=True) for _ in range(n)]
        self.n = n

    def tgt(self, u):
        return self.sems[u % self.n].h, 16 * (u // self.n + 1)

    def inc(self, u):
        self.sems[u % self.n].inc(16)
        return self.sems[u % self.n].h


class Prog:

    ARENA_BYTES = 204 * 1024

    def __init__(self, nc, stack):
        self.nc = nc
        self.stack = stack
        self.ops = {e: [] for e in ENGINES}
        self._semi = 0
        arena = nc.alloc_sbuf_tensor("arena", [128, self.ARENA_BYTES], mybir.dt.uint8)
        self.arena_base = nc.lookup_mloc(arena).addr
        self.sbuf_off = 0

    SEM_POOL = 64

    def sem(self, dma=False):
        if dma:
            self._semi += 1
            h = self.stack.enter_context(self.nc.semaphore(f"d{self._semi}"))
            return Sem(h)
        self._sems = getattr(self, "_sems", [])
        if len(self._sems) < self.SEM_POOL:
            self._semi += 1
            h = self.stack.enter_context(self.nc.semaphore(f"s{self._semi}"))
            s = Sem(h)
            self._sems.append(s)
            return s
        self._semi += 1
        return self._sems[self._semi % self.SEM_POOL]

    def emit(self, eng, fn):
        self.ops[eng].append(fn)

    def t(self, fn): self.emit("tensor", fn)
    def v(self, fn): self.emit("vector", fn)
    def s(self, fn): self.emit("scalar", fn)
    def g(self, fn): self.emit("gpsimd", fn)
    def y(self, fn): self.emit("sync", fn)

    def sb(self, name, shape, dtype, align=64):
        off = (self.sbuf_off + align - 1) // align * align
        per_part = int(np.prod(shape[1:])) * (2 if dtype == BF16 else 4)
        assert off + per_part <= self.ARENA_BYTES, (name, off, per_part)
        h = self.nc.alloc_sbuf_tensor_at(name, shape, dtype,
                                         offset=self.arena_base + off)
        self.offsets = getattr(self, "offsets", {})
        self.offsets[name] = off
        self.sbuf_off = off + per_part
        return h

    def sb_at(self, name, shape, dtype, off):
        per_part = int(np.prod(shape[1:])) * (2 if dtype == BF16 else 4)
        assert off + per_part <= self.ARENA_BYTES, (name, off, per_part)
        return self.nc.alloc_sbuf_tensor_at(name, shape, dtype,
                                            offset=self.arena_base + off)

    def mark(self):
        return self.sbuf_off

    def reset_to(self, off):
        self.sbuf_off = off

    def finish(self):
        with self.nc.Block() as blk:
            for e in ENGINES:
                def body(eng, e=e):
                    for fn in self.ops[e]:
                        fn(eng)
                getattr(blk, e)(body)



def alibi_slopes(n_heads=16):
    def pow2(n):
        start = 2 ** (-(2 ** -(math.log2(n) - 3)))
        return [start ** i for i in range(n)]
    assert math.log2(n_heads).is_integer()
    return np.asarray(pow2(n_heads), dtype=np.float32)


def host_constants():
    c = {}
    c["ident"] = np.eye(128, dtype=BF)
    c["identf"] = np.eye(128, dtype=np.float32)
    c["ones_col"] = np.ones((1, 128), dtype=BF)
    c["ones_colf"] = np.ones((1, 128), dtype=np.float32)
    sl = np.arange(64)
    tm = (sl[:, None] <= sl[None, :]).astype(BF)
    c["trimask"] = np.vstack([tm, tm])
    r = np.arange(128)[:, None]
    u = np.arange(4352)[None, :]
    c["toep"] = (-np.abs((u - 1920) - r)).astype(BF)
    sl16 = alibi_slopes(16)
    dg = np.zeros((128, 16, 128), dtype=np.float32)
    for h in range(16):
        dg[np.arange(128), h, np.arange(128)] = sl16[h]
    c["diag_slope"] = dg.astype(BF)
    return c


CONST_SPECS = {
    "ident": ([128, 128], BF16),
    "identf": ([128, 128], F32),
    "ones_col": ([1, 128], BF16),
    "ones_colf": ([1, 128], F32),
    "trimask": ([128, 64], BF16),
    "toep": ([128, 4352], BF16),
    "diag_slope": ([128, 16, 128], BF16),
}



def declare_common(P):
    bufs = {}
    for name, (shape, dt) in CONST_SPECS.items():
        bufs[name] = P.sb(name, shape, dt)
    bufs["xres"] = P.sb("xres", [128, NT, 1024], F32)
    bufs["lnT"] = P.sb("lnT", [128, ND, T], BF16)
    bufs["ln_tm"] = P.sb("ln_tm", [128, 2, 1024], BF16)
    bufs["scratch_sq"] = P.sb_at("scratch_sq", [128, 1024], F32, P.offsets["ln_tm"])
    bufs["stats"] = P.sb("stats", [128, 128], F32)
    bufs["bias_raw"] = P.sb("bias_raw", [32, 128], F32)
    bufs["psumT0"] = P.nc.alloc_psum_tensor("psumT0", [128, 1024], BF16)
    bufs["psumT1"] = P.nc.alloc_psum_tensor("psumT1", [128, 1024], BF16)
    bufs["psumW0"] = P.nc.alloc_psum_tensor("psumW0", [128, 1024], F32)
    bufs["psumW1"] = P.nc.alloc_psum_tensor("psumW1", [128, 1024], F32)
    bufs["psumS0"] = P.nc.alloc_psum_tensor("psumS0", [128, 512], F32)
    bufs["psumS1"] = P.nc.alloc_psum_tensor("psumS1", [128, 512], F32)
    bufs["ld_const"] = P.sem(dma=True)
    return bufs


def emit_load_consts(P, bufs, haps):
    sem = bufs["ld_const"]
    n = len(CONST_SPECS) + 1
    tgt = sem.inc(16 * n)

    def f(eng):
        for name in CONST_SPECS:
            eng.dma_start(out=bufs[name][:], in_=haps[name]).then_inc(sem.h, 16)
        eng.dma_start(out=bufs["xres"][:],
                      in_=haps["x"].rearrange("(a p) d -> p a d", p=128)).then_inc(sem.h, 16)
    P.y(f)


def emit_store_x(P, bufs, out_ap, x_dep):
    sem = P.sem(dma=True)
    tgt = sem.inc(16)

    def f(eng, dep=x_dep.n, tgt=tgt):
        eng.wait_ge(x_dep.h, dep)
        eng.dma_start(out=out_ap.rearrange("(a p) d -> p a d", p=128),
                      in_=bufs["xres"][:]).then_inc(sem.h, 16)
        eng.wait_ge(sem.h, tgt)
    P.y(f)
    return sem


def emit_colbias(P, bufs, src_ap, dst, n_tiles, ld_sem, ld_tgt, psum_dep=None):
    raw = bufs["bias_raw"]
    s_pe = P.sem()
    s_dve = P.sem()
    ldl = P.sem(dma=True)
    vl = ldl.inc(16)

    def f_ld(eng):
        eng.dma_start(out=raw[0:n_tiles, :], in_=src_ap.rearrange("(a p) -> a p", p=128)).then_inc(ldl.h, 16)
    P.g(f_ld)

    vpe = s_pe.inc()

    def f_pe(eng, wl=vl, cdep=bufs["ld_const"].n, pdep=psum_dep):
        eng.wait_ge(ldl.h, wl)
        eng.wait_ge(bufs["ld_const"].h, cdep)
        if pdep is not None:
            eng.wait_ge(pdep[0], pdep[1])
        eng.transpose(bufs["psumW1"][0:128, 0:n_tiles], raw[0:n_tiles, :],
                      bufs["identf"][0:n_tiles, 0:n_tiles]).then_inc(s_pe.h, 1)
    P.t(f_pe)

    vd = s_dve.inc()

    def f_ev(eng, wp=vpe):
        eng.wait_ge(s_pe.h, wp)
        eng.tensor_copy(dst[:, 0:n_tiles], bufs["psumW1"][0:128, 0:n_tiles]).then_inc(s_dve.h, 1)
    P.v(f_ev)
    return s_dve


def emit_ln_AB(P, bufs, x_dep, tile_dep=False):
    xres = bufs["xres"]
    sq = bufs["scratch_sq"]
    st = bufs["stats"]
    s1 = st[:, 0:NT]
    s2 = st[:, NT:2 * NT]
    mean = st[:, 2 * NT:3 * NT]
    var = st[:, 3 * NT:4 * NT]
    rstd = st[:, 4 * NT:5 * NT]
    nbias = st[:, 5 * NT:6 * NT]

    s_act = P.sem()
    s_dve = P.sem()
    s_pe = P.sem()

    a_base = s_act.n
    vA_act = s_act.inc(NT)

    def fa_act(eng, dep=x_dep.n, base=a_base):
        if not tile_dep:
            eng.wait_ge(x_dep.h, dep)
        for i in range(NT):
            if tile_dep:
                eng.wait_ge(x_dep.h, dep - NT + i + 1)
            if i >= 1:
                eng.wait_ge(s_act.h, base + i)
            ins = eng.activation(sq[:, :], xres[:, i, :], AF.Square,
                                 accum_out=s2[:, i:i + 1])
            ins.then_inc(s_act.h, 1)
    P.s(fa_act)

    vA_dve = s_dve.inc()

    def fa_dve(eng, dep=x_dep.n, tgt=vA_dve):
        if not tile_dep:
            eng.wait_ge(x_dep.h, dep)
        for i in range(NT):
            if tile_dep:
                eng.wait_ge(x_dep.h, dep - NT + i + 1)
            ins = eng.reduce_sum(s1[:, i:i + 1], xres[:, i, :], AX.X)
        ins.then_inc(s_dve.h, 1)
    P.v(fa_dve)
    a_done = [(s_act.h, vA_act), (s_dve.h, vA_dve)]

    vB1 = s_dve.inc()

    def fb_dve(eng, wa=vA_act, wd=vA_dve):
        eng.wait_ge(s_act.h, wa)
        eng.wait_ge(s_dve.h, wd)
        eng.tensor_scalar_mul(mean[:, :], s1, 1.0 / D)
        eng.drain()
        eng.tensor_tensor(var[:, :], mean, mean, ALU.mult)
        eng.drain()
        eng.scalar_tensor_tensor(var[:, :], s2, 1.0 / D, var, ALU.mult, ALU.subtract)
        eng.drain()
        eng.tensor_scalar_add(var[:, :], var, 1e-5)
        eng.drain().then_inc(s_dve.h, 1)
    P.v(fb_dve)

    vB2 = s_act.inc()

    def fb_act(eng, wd=vB1):
        eng.wait_ge(s_dve.h, wd)
        eng.activation(rstd[:, :], var, AF.Sqrt)
        eng.drain().then_inc(s_act.h, 1)
    P.s(fb_act)

    vB3 = s_dve.inc()

    def fb2_dve(eng, wa=vB2):
        eng.wait_ge(s_act.h, wa)
        eng.reciprocal(rstd[:, :], rstd)
        eng.drain()
        eng.tensor_tensor(nbias[:, :], mean, rstd, ALU.mult)
        eng.drain()
        eng.tensor_scalar_mul(nbias[:, :], nbias, -1.0)
        eng.drain().then_inc(s_dve.h, 1)
    P.v(fb2_dve)
    return dict(s_act=s_act, s_dve=s_dve, s_pe=s_pe, vB3=vB3,
                a_done=a_done)


def emit_ln_CD(P, bufs, ctx, psum_dep=None, cx_dep=None):
    xres = bufs["xres"]
    lnT = bufs["lnT"]
    ln_tm = bufs["ln_tm"]
    st = bufs["stats"]
    rstd = st[:, 4 * NT:5 * NT]
    nbias = st[:, 5 * NT:6 * NT]
    s_act, s_dve, s_pe = ctx["s_act"], ctx["s_dve"], ctx["s_pe"]
    vB3 = ctx["vB3"]

    actC = s_act.n
    s_act.inc(NT)
    peD = s_pe.n
    s_pe.inc(NT)
    dveD = s_dve.n
    s_dve.inc(NT)

    def fc_act(eng, wd=vB3, base=actC, peD=peD,
               wcx=(None if cx_dep is None else (cx_dep.h, cx_dep.n))):
        eng.wait_ge(s_dve.h, wd)
        if wcx is not None:
            eng.wait_ge(wcx[0], wcx[1])
        for i in range(NT):
            if i >= 2:
                eng.wait_ge(s_pe.h, peD + i - 1)
            ins = eng.activation(ln_tm[:, i % 2, :], xres[:, i, :], AF.Identity,
                                 bias=nbias[:, i:i + 1], scale=rstd[:, i:i + 1])
            ins.then_inc(s_act.h, 1)
    P.s(fc_act)

    def fd_pe(eng, base=actC, dveD=dveD, cdep=bufs["ld_const"].n, pdep=psum_dep):
        eng.wait_ge(bufs["ld_const"].h, cdep)
        if pdep is not None:
            eng.wait_ge(pdep[0], pdep[1])
        for i in range(NT):
            eng.wait_ge(s_act.h, base + i + 1)
            if i >= 2:
                eng.wait_ge(s_dve.h, dveD + i - 1)
            ps = bufs["psumT0"] if i % 2 == 0 else bufs["psumT1"]
            for k in range(ND):
                ins = eng.transpose(ps[:, k * 128:(k + 1) * 128],
                                    ln_tm[:, i % 2, k * 128:(k + 1) * 128],
                                    bufs["ident"][:])
            ins.then_inc(s_pe.h, 1)
    P.t(fd_pe)

    def fd_dve(eng, base=peD):
        for i in range(NT):
            eng.wait_ge(s_pe.h, base + i + 1)
            ps = bufs["psumT0"] if i % 2 == 0 else bufs["psumT1"]
            ins = eng.tensor_copy(
                lnT[:, :, i * 128:(i + 1) * 128],
                ps[:].rearrange("p (a b) -> p a b", a=ND))
            ins.then_inc(s_dve.h, 1)
    P.v(fd_dve)
    s_dve.a_done = ctx["a_done"]
    return s_dve


def emit_ln(P, bufs, x_dep, psum_dep=None, cx_dep=None, tile_dep=False):
    ctx = emit_ln_AB(P, bufs, x_dep, tile_dep=tile_dep)
    return emit_ln_CD(P, bufs, ctx, psum_dep=psum_dep, cx_dep=cx_dep)



def declare_l1ffn(P, bufs):
    bufs["ffn_U1"] = P.sb("ffn_U1", [128, 8, 256], BF16)
    bufs["ffn_V1"] = P.sb("ffn_V1", [128, 2, 4096], BF16)
    bufs["ffn_U2"] = P.sb("ffn_U2", [128, 32, 256], BF16)
    bufs["ffn_V2"] = P.sb("ffn_V2", [128, 2, 1024], BF16)
    bufs["ffn_b1"] = P.sb("ffn_b1", [128, 32], F32)
    bufs["ffn_b2"] = P.sb("ffn_b2", [1, 1024], F32)
    bufs["ffn_b2h"] = P.sb("ffn_b2h", [1, 1024], BF16)
    bufs["ffn_b2l"] = P.sb("ffn_b2l", [1, 1024], BF16)
    bufs["ffn_PT"] = P.sb("ffn_PT", [128, 2, 2048], BF16)
    bufs["ffn_QT"] = P.sb("ffn_QT", [128, 2, 2048], BF16)
    bufs["ffn_Hs"] = P.sb("ffn_Hs", [128, 2, 512], BF16)


def emit_l1ffn(P, bufs, waps, ln_dep, ld_dep=None):
    lnT = bufs["lnT"]
    xres = bufs["xres"]
    U1, V1, U2, V2 = bufs["ffn_U1"], bufs["ffn_V1"], bufs["ffn_U2"], bufs["ffn_V2"]
    b1, b2, b2h, b2l = bufs["ffn_b1"], bufs["ffn_b2"], bufs["ffn_b2h"], bufs["ffn_b2l"]
    PT, QT, Hs = bufs["ffn_PT"], bufs["ffn_QT"], bufs["ffn_Hs"]
    pS = [bufs["psumS0"], bufs["psumS1"]]
    pW = [bufs["psumW0"], bufs["psumW1"]]

    ld = P.sem(dma=True)
    vld = ld.inc(16 * 5)

    def f_ld(eng, wd=ld_dep):
        if wd is not None:
            eng.wait_ge(wd[0], wd[1])
        eng.dma_start(out=U1[:], in_=waps["U1"].rearrange("(a p) r -> p a r", p=128)).then_inc(ld.h, 16)
        eng.dma_start(out=V1[:], in_=waps["V1"].rearrange("(a p) f -> p a f", p=128)).then_inc(ld.h, 16)
        eng.dma_start(out=U2[:], in_=waps["U2"].rearrange("(a p) r -> p a r", p=128)).then_inc(ld.h, 16)
        eng.dma_start(out=V2[:], in_=waps["V2"].rearrange("(a p) d -> p a d", p=128)).then_inc(ld.h, 16)
        eng.dma_start(out=b2[:], in_=waps["b2"].unsqueeze(0)).then_inc(ld.h, 16)
    P.g(f_ld)

    b1_dep = emit_colbias(P, bufs, waps["b1"], b1, 32, ld, vld)

    s_misc = P.sem()
    v_b2 = s_misc.inc()

    def f_b2(eng, wld=vld):
        eng.wait_ge(ld.h, wld)
        eng.tensor_copy(b2h[:], b2[:])
        eng.drain()
        eng.tensor_tensor(b2l[:], b2[:], b2h[:], ALU.subtract)
        eng.drain().then_inc(s_misc.h, 1)
    P.v(f_b2)

    s_pe = P.sem()
    s_act = P.sem()

    u1_pe = s_pe.n
    s_pe.inc(8)
    u1_act = s_act.n
    s_act.inc(8)

    def f_u1(eng, wld=vld, wln=ln_dep.n, base=u1_act, wb1=(b1_dep.h, b1_dep.n)):
        eng.wait_ge(ld.h, wld)
        eng.wait_ge(ln_dep.h, wln)
        eng.wait_ge(wb1[0], wb1[1])
        for j in range(8):
            rc, tc = divmod(j, 4)
            if j >= 2:
                eng.wait_ge(s_act.h, base + j - 1)
            ps = pS[j % 2]
            for k in range(ND):
                ins = eng.matmul(ps[:, :], U1[:, k, rc * 128:(rc + 1) * 128],
                                 lnT[:, k, tc * 512:(tc + 1) * 512],
                                 start=(k == 0), stop=(k == ND - 1))
            ins.then_inc(s_pe.h, 1)
    P.t(f_u1)

    def f_u1_evac(eng, base=u1_pe):
        for j in range(8):
            rc, tc = divmod(j, 4)
            eng.wait_ge(s_pe.h, base + j + 1)
            ins = eng.activation(PT[:, rc, tc * 512:(tc + 1) * 512], pS[j % 2][:, :],
                                 AF.Identity)
            ins.then_inc(s_act.h, 1)
    P.s(f_u1_evac)

    s_v1 = P.sem()
    s_u2 = P.sem()
    s_gel = P.sem()
    s_qev = P.sem()
    v1_0 = s_v1.n
    s_v1.inc(128)
    u2_0 = s_u2.n
    s_u2.inc(128)
    gel_0 = s_gel.n
    s_gel.inc(128)
    qev_0 = s_qev.n
    s_qev.inc(4)

    def f_v1u2(eng, u1a=u1_act, gel=gel_0, qev=qev_0):
        eng.wait_ge(s_act.h, u1a + 8)
        for j in range(129):
            if j < 128:
                tc, fs = divmod(j, 32)
                if j >= 2:
                    eng.wait_ge(s_gel.h, gel + j - 1)
                ps = pS[j % 2]
                for rt in range(2):
                    ins = eng.matmul(ps[:, :], V1[:, rt, fs * 128:(fs + 1) * 128],
                                     PT[:, rt, tc * 512:(tc + 1) * 512],
                                     start=(rt == 0), stop=(rt == 1))
                ins.then_inc(s_v1.h, 1)
            if j >= 1:
                jj = j - 1
                tc, fs = divmod(jj, 32)
                eng.wait_ge(s_gel.h, gel + jj + 1)
                if fs == 0 and tc >= 1:
                    eng.wait_ge(s_qev.h, qev + tc)
                ins = eng.matmul(pW[0][:, 0:512], U2[:, fs, 0:128], Hs[:, jj % 2, :],
                                 start=(fs == 0), stop=(fs == 31))
                ins2 = eng.matmul(pW[0][:, 512:1024], U2[:, fs, 128:256], Hs[:, jj % 2, :],
                                  start=(fs == 0), stop=(fs == 31))
                ins2.then_inc(s_u2.h, 1)
    P.t(f_v1u2)

    def f_gelu(eng, v1b=v1_0, u2b=u2_0):
        for j in range(128):
            tc, fs = divmod(j, 32)
            eng.wait_ge(s_v1.h, v1b + j + 1)
            ins = eng.activation(Hs[:, j % 2, :], pS[j % 2][:, :], AF.Gelu_apprx_tanh,
                                 bias=b1[:, fs:fs + 1])
            ins.then_inc(s_gel.h, 1)
            if fs == 31:
                eng.wait_ge(s_u2.h, u2b + (tc + 1) * 32)
                eng.activation(QT[:, 0, tc * 512:(tc + 1) * 512], pW[0][:, 0:512],
                               AF.Identity)
                ins = eng.activation(QT[:, 1, tc * 512:(tc + 1) * 512], pW[0][:, 512:1024],
                                     AF.Identity)
                ins.then_inc(s_qev.h, 1)
    P.s(f_gelu)

    v2_0 = s_pe.n
    s_pe.inc(NT)
    s_res = P.sem()
    res_0 = s_res.n
    s_res.inc(NT)

    def f_v2(eng, qev=qev_0, wb2=v_b2, res0=res_0):
        eng.wait_ge(s_qev.h, qev + 4)
        eng.wait_ge(s_misc.h, wb2)
        for tt in range(NT):
            if tt >= 2:
                eng.wait_ge(s_res.h, res0 + tt - 1)
            ps = pW[tt % 2]
            for half in range(2):
                sl = slice(half * 512, (half + 1) * 512)
                for rt in range(2):
                    eng.matmul(ps[:, sl], QT[:, rt, tt * 128:(tt + 1) * 128],
                               V2[:, rt, sl], start=(rt == 0), stop=False)
                eng.matmul(ps[:, sl], bufs["ones_col"][:, 0:128], b2h[0:1, sl],
                           start=False, stop=False)
                ins = eng.matmul(ps[:, sl], bufs["ones_col"][:, 0:128], b2l[0:1, sl],
                                 start=False, stop=True)
            ins.then_inc(s_pe.h, 1)
    P.t(f_v2)

    def f_res(eng, base=v2_0):
        for tt in range(NT):
            eng.wait_ge(s_pe.h, base + tt + 1)
            ins = eng.tensor_tensor(xres[:, tt, :], pW[tt % 2][:, :], xres[:, tt, :],
                                    ALU.add)
            ins.then_inc(s_res.h, 1)
    P.v(f_res)
    return s_res



def declare_l1attn(P, bufs):
    off_x = P.offsets["xres"]
    bufs["at_QT"] = P.sb_at("at_QT", [128, 8, 2048], BF16, off_x)
    bufs["at_KT"] = P.sb_at("at_KT", [128, 8, 2048], BF16, off_x + 32 * 1024)
    bufs["at_OT"] = P.sb_at("at_OT", [128, 8, 2048], BF16, P.offsets["lnT"])
    bufs["at_Vtm"] = P.sb("at_Vtm", [128, 16, 16, 65], BF16)
    bufs["at_Wq"] = P.sb("at_Wq", [128, 2, 8, 512], BF16)
    bufs["at_Wout"] = P.sb("at_Wout", [128, 8, 512], BF16)
    bufs["at_bqk"] = P.sb("at_bqk", [128, 16], F32)
    bufs["at_bqkq"] = P.sb("at_bqkq", [128, 8], F32)
    bufs["at_bvr"] = P.sb("at_bvr", [1, 1024], F32)
    bufs["at_bvh"] = P.sb("at_bvh", [1, 1024], BF16)
    bufs["at_bvl"] = P.sb("at_bvl", [1, 1024], BF16)
    bufs["at_bo"] = P.sb("at_bo", [1, 1024], F32)
    bufs["at_boh"] = P.sb("at_boh", [1, 1024], BF16)
    bufs["at_bol"] = P.sb("at_bol", [1, 1024], BF16)
    bufs["at_eS"] = P.sb("at_eS", [128, 2, 512], BF16)
    bufs["at_Osc"] = P.sb("at_Osc", [64, 2, 512], BF16)
    bufs["at_inv"] = P.sb("at_inv", [1, 2, 512], F32)
    bufs["at_QZ"] = P.sb("at_QZ", [128, 2, 512], BF16)


def emit_l1attn(P, bufs, waps, ln_dep, x_dep, spill_ap, dbg_stop=False):
    lnT = bufs["lnT"]
    xres = bufs["xres"]
    QT, KT, OT, Vtm = bufs["at_QT"], bufs["at_KT"], bufs["at_OT"], bufs["at_Vtm"]
    Wq, Wout = bufs["at_Wq"], bufs["at_Wout"]
    eS, Osc, inv = bufs["at_eS"], bufs["at_Osc"], bufs["at_inv"]
    pS = [bufs["psumS0"], bufs["psumS1"]]
    pW0, pW1 = bufs["psumW0"], bufs["psumW1"]
    banks4 = [lambda: pS[0][:, :], lambda: pS[1][:, :],
              lambda: pW0[:, 0:512], lambda: pW0[:, 512:1024]]

    s_spill = P.sem(dma=True)
    v_spill = s_spill.inc(16)

    def f_spill(eng, wx=x_dep.n):
        eng.wait_ge(x_dep.h, wx)
        eng.dma_start(out=spill_ap, in_=xres[:]).then_inc(s_spill.h, 16)
    P.y(f_spill)

    ld = P.sem(dma=True)
    v_ldb = ld.inc(16 * 2)

    def f_ldb(eng, wx=(x_dep.h, x_dep.n)):
        eng.wait_ge(wx[0], wx[1])
        eng.dma_start(out=bufs["at_bvr"][:], in_=waps["bqkv"][2048:3072].unsqueeze(0)).then_inc(ld.h, 16)
        eng.dma_start(out=bufs["at_bo"][:], in_=waps["bout"].unsqueeze(0)).then_inc(ld.h, 16)
    P.g(f_ldb)

    bqk_dep = emit_colbias(P, bufs, waps["bqkv"][0:2048], bufs["at_bqk"], 16, ld,
                           v_ldb, psum_dep=(x_dep.h, x_dep.n))

    s_misc = P.sem()
    v_hl = s_misc.inc()

    def f_hilo(eng, wld=v_ldb, wbq=(bqk_dep.h, bqk_dep.n)):
        eng.wait_ge(ld.h, wld)
        eng.wait_ge(wbq[0], wbq[1])
        eng.tensor_scalar_mul(bufs["at_bqkq"][:, :], bufs["at_bqk"][:, 0:8], 0.125)
        eng.tensor_copy(bufs["at_bvh"][:], bufs["at_bvr"][:])
        eng.tensor_copy(bufs["at_boh"][:], bufs["at_bo"][:])
        eng.drain()
        eng.tensor_tensor(bufs["at_bvl"][:], bufs["at_bvr"][:], bufs["at_bvh"][:], ALU.subtract)
        eng.tensor_tensor(bufs["at_bol"][:], bufs["at_bo"][:], bufs["at_boh"][:], ALU.subtract)
        eng.memset(Vtm[:, :, :, 64:65], 1.0)
        eng.drain().then_inc(s_misc.h, 1)
    P.v(f_hilo)

    pp_wld = PPSem(P)
    s_qku = P.sem()
    qku0 = s_qku.n
    s_qku.inc(16)

    def f_wld(eng, wsp=v_spill, base=qku0):
        eng.wait_ge(s_spill.h, wsp)
        for sc in range(6):
            if sc >= 2:
                eng.wait_ge(s_qku.h, base + (sc - 2) * 4 + 4)
            eng.dma_start(out=Wq[:, sc % 2],
                          in_=waps["Wqkv"].rearrange("(a p) (c j) -> p c a j",
                                                     p=128, j=512)[:, sc]
                          ).then_inc(pp_wld.inc(sc), 16)
    P.g(f_wld)

    s_qkev = P.sem()
    qkev0 = s_qkev.n
    s_qkev.inc(64)
    s_vpe = P.sem()
    vpe0 = s_vpe.n
    s_vpe.inc(32)
    s_vev = P.sem()
    vev0 = s_vev.n
    s_vev.inc(32)

    def f_qkv_pe(eng, wln=ln_dep.n, ev=qkev0, vev=vev0, whl=v_hl,
                 wbq=(bqk_dep.h, bqk_dep.n), base=qku0):
        eng.wait_ge(ln_dep.h, wln)
        eng.wait_ge(s_misc.h, whl)
        eng.wait_ge(wbq[0], wbq[1])
        for jc in range(16):
            sc = jc // 4
            eng.wait_ge(*pp_wld.tgt(sc))
            for tc in range(4):
                if jc >= 1:
                    eng.wait_ge(s_qkev.h, ev + (jc - 1) * 4 + tc + 1)
                ps = banks4[tc]()
                for dt in range(ND):
                    ins = eng.matmul(ps, Wq[:, sc % 2, dt, (jc % 4) * 128:(jc % 4) * 128 + 128],
                                     lnT[:, dt, tc * 512:(tc + 1) * 512],
                                     start=(dt == 0), stop=(dt == ND - 1))
            ins.then_inc(s_qku.h, 1)
        for u in range(32):
            kt, half = divmod(u, 2)
            sc = 4 + half
            if u < 2:
                eng.wait_ge(*pp_wld.tgt(sc))
            if u < 4:
                eng.wait_ge(s_qkev.h, ev + 64)
            else:
                eng.wait_ge(s_vev.h, vev + u - 3)
            ps = banks4[u % 4]()
            for dt in range(ND):
                eng.matmul(ps, lnT[:, dt, kt * 128:(kt + 1) * 128],
                           Wq[:, sc % 2, dt, :], start=(dt == 0), stop=False)
            eng.matmul(ps, bufs["ones_col"][:, 0:128],
                       bufs["at_bvh"][0:1, half * 512:(half + 1) * 512],
                       start=False, stop=False)
            ins = eng.matmul(ps, bufs["ones_col"][:, 0:128],
                             bufs["at_bvl"][0:1, half * 512:(half + 1) * 512],
                             start=False, stop=True)
            ins.then_inc(s_vpe.h, 1)
    P.t(f_qkv_pe)

    def f_qk_evac(eng, base=qku0, whl=v_hl):
        eng.wait_ge(s_misc.h, whl)
        for jc in range(16):
            eng.wait_ge(s_qku.h, base + jc + 1)
            for tc in range(4):
                if jc < 8:
                    ins = eng.activation(QT[:, jc, tc * 512:(tc + 1) * 512],
                                         banks4[tc](), AF.Identity,
                                         bias=bufs["at_bqkq"][:, jc:jc + 1],
                                         scale=0.125)
                else:
                    ins = eng.activation(KT[:, jc - 8, tc * 512:(tc + 1) * 512],
                                         banks4[tc](), AF.Identity,
                                         bias=bufs["at_bqk"][:, jc:jc + 1])
                ins.then_inc(s_qkev.h, 1)
    P.s(f_qk_evac)

    def f_v_evac(eng, base=vpe0):
        for u in range(32):
            kt, half = divmod(u, 2)
            eng.wait_ge(s_vpe.h, base + u + 1)
            ps = banks4[u % 4]()
            ins = eng.tensor_copy(Vtm[:, kt, half * 8:(half + 1) * 8, 0:64],
                                  ps.rearrange("p (a b) -> p a b", a=8))
            ins.then_inc(s_vev.h, 1)
    P.v(f_v_evac)

    _slopes = alibi_slopes(16)
    KEEP = {}
    for _q in range(64):
        _h, _qc = divmod(_q, 4)
        KEEP[_q] = [kt for kt in range(16)
                    if _slopes[_h] * max(0, kt * 128 - (_qc * 512 + 511),
                                         _qc * 512 - (kt * 128 + 127)) < 200.0]
    GIDX = {}
    PVB = [0] * 65
    CONS = {}
    _g = 0
    for _q in range(64):
        PVB[_q + 1] = PVB[_q] + len(KEEP[_q])
        for _i, _kt in enumerate(KEEP[_q]):
            GIDX[(_q, _kt)] = _g
            CONS[_g] = PVB[_q] + _i + 1
            _g += 1
    NKT = _g

    s_sc = P.sem()
    sc0 = s_sc.n
    s_sc.inc(NKT)
    s_ex = P.sem()
    ex0 = s_ex.n
    s_ex.inc(NKT)
    s_pv = P.sem()
    pv0 = s_pv.n
    s_pv.inc(NKT)
    s_rc = P.sem()
    rc0 = s_rc.n
    s_rc.inc(64)
    s_bc = P.sem()
    bc0 = s_bc.n
    s_bc.inc(64)
    s_oev = P.sem()
    oev0 = s_oev.n
    s_oev.inc(64)
    s_ot = P.sem()
    ot0 = s_ot.n
    s_ot.inc(64)
    s_qz = P.sem()
    qz0 = s_qz.n
    s_qz.inc(64)
    QZ = bufs["at_QZ"]

    def f_sc_pe(eng, e0=ex0, v0=vev0, q0=qkev0, r0=rc0, o0=oev0, t0=ot0,
                z0=qz0):
        eng.wait_ge(s_qkev.h, q0 + 64)
        eng.wait_ge(s_vev.h, v0 + 32)
        eng.wait_ge(bufs["ld_const"].h, bufs["ld_const"].n)
        for q in range(64):
            h, qc = divmod(q, 4)
            pv = pW0[0:65, (q % 2) * 512:(q % 2) * 512 + 512]
            if q >= 2:
                eng.wait_ge(s_rc.h, r0 + q - 1)
                if q >= 4:
                    eng.wait_ge(s_ot.h, t0 + q - 3)
                pbc = pW1[0:64, ((q - 2) % 2) * 512:((q - 2) % 2) * 512 + 512]
                ins = eng.matmul(pbc, bufs["ones_colf"][0:1, 0:64],
                                 inv[0:1, (q - 2) % 2, :], start=True, stop=True)
                ins.then_inc(s_bc.h, 1)
                eng.wait_ge(s_oev.h, o0 + q - 1)
            eng.wait_ge(s_qz.h, z0 + q + 1)
            KQ = KEEP[q]
            for i, kt in enumerate(KQ):
                g = GIDX[(q, kt)]
                if g >= 2:
                    eng.wait_ge(s_ex.h, e0 + g - 1)
                ps = pS[g % 2]
                eng.matmul(ps[:, :], KT[:, h // 2, kt * 128:(kt + 1) * 128],
                           QZ[:, q % 2, :],
                           start=True, stop=False)
                off = qc * 512 - kt * 128 + 1920
                ins = eng.matmul(ps[:, :], bufs["diag_slope"][:, h, :],
                                 bufs["toep"][:, off:off + 512],
                                 start=False, stop=True)
                ins.then_inc(s_sc.h, 1)
                if i >= 1:
                    eng.wait_ge(s_ex.h, e0 + g)
                    ins = eng.matmul(pv, Vtm[:, KQ[i - 1], h, 0:65],
                                     eS[:, (g - 1) % 2, :],
                                     start=(i == 1), stop=False)
                    ins.then_inc(s_pv.h, 1)
            eng.wait_ge(s_ex.h, e0 + GIDX[(q, KQ[-1])] + 1)
            ins = eng.matmul(pv, Vtm[:, KQ[-1], h, 0:65],
                             eS[:, GIDX[(q, KQ[-1])] % 2, :],
                             start=False, stop=True)
            ins.then_inc(s_pv.h, 1)
        for qq in (62, 63):
            eng.wait_ge(s_rc.h, r0 + qq + 1)
            eng.wait_ge(s_ot.h, t0 + qq - 1)
            pbc = pW1[0:64, (qq % 2) * 512:(qq % 2) * 512 + 512]
            ins = eng.matmul(pbc, bufs["ones_colf"][0:1, 0:64],
                             inv[0:1, qq % 2, :], start=True, stop=True)
            ins.then_inc(s_bc.h, 1)
    P.t(f_sc_pe)

    def f_sm_act(eng, s0=sc0, p0=pv0, t0=ot0):
        for q in range(64):
            for kt in KEEP[q]:
                g = GIDX[(q, kt)]
                eng.wait_ge(s_sc.h, s0 + g + 1)
                if g >= 2:
                    eng.wait_ge(s_pv.h, p0 + CONS[g - 2])
                ps = pS[g % 2]
                ins = eng.activation(eS[:, g % 2, :], ps[:, :], AF.Exp)
                ins.then_inc(s_ex.h, 1)
            if q >= 1:
                eng.wait_ge(s_pv.h, p0 + PVB[q])
                if q >= 3:
                    eng.wait_ge(s_ot.h, t0 + q - 2)
                pvp = pW0[0:65, ((q - 1) % 2) * 512:((q - 1) % 2) * 512 + 512]
                ins = eng.activation(Osc[:, (q - 1) % 2, :], pvp[0:64, :], AF.Identity)
                ins.then_inc(s_oev.h, 1)
        eng.wait_ge(s_pv.h, p0 + PVB[64])
        eng.wait_ge(s_ot.h, t0 + 62)
        pvp = pW0[0:65, (63 % 2) * 512:(63 % 2) * 512 + 512]
        ins = eng.activation(Osc[:, 63 % 2, :], pvp[0:64, :], AF.Identity)
        ins.then_inc(s_oev.h, 1)
    P.s(f_sm_act)

    def _qz_fill(eng, u):
        h, qc = divmod(u, 4)
        lo = (h % 2) * 64
        sib = 64 - lo
        eng.tensor_copy(QZ[lo:lo + 64, u % 2, :],
                        QT[lo:lo + 64, h // 2, qc * 512:(qc + 1) * 512])
        eng.memset(QZ[sib:sib + 64, u % 2, :], 0.0).then_inc(s_qz.h, 1)

    def f_sm_dve(eng, p0=pv0, b0=bc0, o0=oev0, q0=qkev0):
        eng.wait_ge(s_qkev.h, q0 + 32)
        _qz_fill(eng, 0)
        _qz_fill(eng, 1)
        for q in range(64):
            h, qc = divmod(q, 4)
            eng.wait_ge(s_pv.h, p0 + PVB[q + 1])
            if q >= 2:
                eng.wait_ge(s_bc.h, b0 + q - 1)
            pv = pW0[0:65, (q % 2) * 512:(q % 2) * 512 + 512]
            ins = eng.reciprocal(inv[0:1, q % 2, :], pv[64:65, :])
            ins.then_inc(s_rc.h, 1)
            if q <= 61:
                _qz_fill(eng, q + 2)
            eng.wait_ge(s_bc.h, b0 + q + 1)
            eng.wait_ge(s_oev.h, o0 + q + 1)
            pbc = pW1[0:64, (q % 2) * 512:(q % 2) * 512 + 512]
            ins = eng.tensor_tensor(OT[(h % 2) * 64:(h % 2) * 64 + 64, h // 2,
                                       qc * 512:(qc + 1) * 512],
                                    Osc[:, q % 2, :], pbc, ALU.mult)
            ins.then_inc(s_ot.h, 1)
    P.v(f_sm_dve)

    s_rest = P.sem(dma=True)
    v_rest = s_rest.inc(16)

    def f_rest(eng, w=sc0, wsp=v_spill):
        eng.wait_ge(s_sc.h, w + NKT)
        eng.dma_start(out=xres[:], in_=spill_ap).then_inc(s_rest.h, 16)
    P.y(f_rest)

    s_wol = P.sem(dma=True)
    wol0 = s_wol.n
    s_wol.inc(16 * 2)
    s_wo = P.sem()
    wo0 = s_wo.n
    s_wo.inc(2 * NT)
    s_res = P.sem()
    res0 = s_res.n
    s_res.inc(2 * NT)

    def f_wo_ld(eng, t0=ot0, w0=wo0):
        eng.wait_ge(s_ot.h, t0 + 64)
        for half in range(2):
            if half == 1:
                eng.wait_ge(s_wo.h, w0 + NT)
            eng.dma_start(out=bufs["at_Wout"][:],
                          in_=waps["Wout"].rearrange("(a p) d -> p a d", p=128)
                          [:, :, half * 512:(half + 1) * 512]).then_inc(s_wol.h, 16)
    P.g(f_wo_ld)

    def f_wo_pe(eng, t0=ot0, whl=v_hl, res0=res0, w0=wol0):
        eng.wait_ge(s_ot.h, t0 + 64)
        eng.wait_ge(s_misc.h, whl)
        for half in range(2):
            eng.wait_ge(s_wol.h, w0 + 16 * (half + 1))
            sl = slice(half * 512, (half + 1) * 512)
            for tt in range(NT):
                u = half * NT + tt
                if u >= 4:
                    eng.wait_ge(s_res.h, res0 + u - 3)
                ps = [pW1[:, 0:512], pW1[:, 512:1024],
                      pW0[:, 0:512], pW0[:, 512:1024]][u % 4]
                for dt in range(ND):
                    eng.matmul(ps, OT[:, dt, tt * 128:(tt + 1) * 128],
                               Wout[:, dt, :], start=(dt == 0), stop=False)
                eng.matmul(ps, bufs["ones_col"][:, 0:128],
                           bufs["at_boh"][0:1, sl], start=False, stop=False)
                ins = eng.matmul(ps, bufs["ones_col"][:, 0:128],
                                 bufs["at_bol"][0:1, sl], start=False, stop=True)
                ins.then_inc(s_wo.h, 1)
    P.t(f_wo_pe)

    def f_wo_res(eng, w0=wo0, wr=v_rest):
        eng.wait_ge(s_rest.h, wr)
        for half in range(2):
            sl = slice(half * 512, (half + 1) * 512)
            for tt in range(NT):
                u = half * NT + tt
                eng.wait_ge(s_wo.h, w0 + u + 1)
                ps = [pW1[:, 0:512], pW1[:, 512:1024],
                      pW0[:, 0:512], pW0[:, 512:1024]][u % 4]
                ins = eng.tensor_tensor(xres[:, tt, sl], ps, xres[:, tt, sl], ALU.add)
                ins.then_inc(s_res.h, 1)
    P.v(f_wo_res)
    return s_res



def declare_l0(P, bufs):
    TBL = 512
    bufs["l0_R"] = P.sb("l0_R", [128, 8, 64], F32)
    bufs["l0_Rbf"] = P.sb("l0_Rbf", [128, 8, 64], BF16)
    bufs["l0_atm"] = P.sb("l0_atm", [128, 16, 16], F32)
    bufs["l0_btm"] = P.sb("l0_btm", [128, 16, 16], F32)
    bufs["l0_etm"] = P.sb("l0_etm", [128, 16, 16], F32)
    bufs["l0_dtm"] = P.sb("l0_dtm", [128, 16, 16], F32)
    bufs["l0_cw"] = P.sb("l0_cw", [128, 24, 4], F32)
    bufs["l0_cb"] = P.sb("l0_cb", [128, 24], F32)
    bufs["l0_wbase"] = P.sb("l0_wbase", [16, 1], F32)
    bufs["l0_wA"] = P.sb("l0_wA", [128, 8, 16], BF16)
    bufs["l0_diag"] = P.sb("l0_diag", [128, 4, 128], BF16)
    bufs["l0_opsc"] = P.sb("l0_opsc", [128, 2, 512], F32)
    bufs["l0_small"] = P.sb("l0_small", [16, 8, 33], F32)
    bufs["l0_mt"] = P.sb("l0_mt", [128, 2, 8, 64], BF16)
    bufs["l0_yst"] = P.sb("l0_yst", [128, 2, 512], F32)
    bufs["l0_msk"] = P.sb_at("l0_msk", [128, 2, 8, 64], BF16, P.offsets["l0_yst"])
    bufs["l0_ipw"] = P.sb("l0_ipw", [128, 2, 8, 128], BF16)
    bufs["l0_ipw2"] = P.sb_at("l0_ipw2", [128, 2, 8, 128], BF16,
                              P.offsets["ln_tm"])
    bufs["l0_opw"] = P.sb("l0_opw", [128, 2, 8, 512], BF16)
    base = P.mark()
    bufs["l0_B1"] = P.sb_at("l0_B1", [16, 2048], F32, base)
    bufs["l0_B2"] = P.sb_at("l0_B2", [16, 2048], F32, base + 8 * 1024)
    bufs["l0_B3"] = P.sb_at("l0_B3", [16, 2048], F32, base + 16 * 1024)
    bufs["l0_B4"] = P.sb_at("l0_B4", [16, 2048], F32, base + 24 * 1024)
    bufs["l0_wraw"] = P.sb_at("l0_wraw", [128, 4, 1024], F32, base)
    bufs["l0_wev"] = P.sb_at("l0_wev", [128, 2, 8, 128], BF16, base + 16 * 1024)
    off = base
    bufs["l0_xio"] = P.sb_at("l0_xio", [128, 24, TBL + 3], BF16, off)
    off += ((24 * (TBL + 3) * 2 + 63) // 64) * 64
    bufs["l0_Itm"] = P.sb_at("l0_Itm", [128, 4, 1024], BF16, off)
    off += 8 * 1024
    bufs["l0_Otm"] = P.sb_at("l0_Otm", [128, 4, 1024], BF16, off)
    off += 8 * 1024
    bufs["l0_Ytm"] = P.sb_at("l0_Ytm", [128, 4, 1024], BF16, off)
    bufs["l0_diag2"] = P.sb_at("l0_diag2", [128, 4, 128], BF16, off)
    bufs["l0_YT"] = P.sb_at("l0_YT", [128, 4, 8, 128], BF16, off)
    off += 8 * 1024
    bufs["l0_halo"] = P.sb_at("l0_halo", [128, 24, 3], BF16, off)
    off += ((24 * 3 * 2 + 63) // 64) * 64
    P.reset_to(max(off, base + 32 * 1024))


def emit_l0_weightprep(P, bufs, waps, ipT_dram, opT_dram):
    wraw, wev = bufs["l0_wraw"], bufs["l0_wev"]
    pWl = [bufs["psumW0"], bufs["psumW1"]]
    ld4 = [P.sem(dma=True) for _ in range(4)]
    s_pe = P.sem()
    pe0 = s_pe.n
    s_pe.inc(33)
    s_ev = P.sem()
    ev0 = s_ev.n
    s_ev.inc(33)
    pk1 = P.sem(dma=True)
    s_pk = P.sem()
    pk0 = s_pk.n
    s_pk.inc(33)

    WP_ORDER = [24] + list(range(24)) + list(range(25, 33))

    def f_ld(eng):
        for p, u in enumerate(WP_ORDER):
            if p >= 4:
                eng.wait_ge(s_pe.h, pe0 + p - 3)
            if u < 25:
                jw = 128 if u < 24 else 16
                src = waps["in_proj_w"][u * 128:u * 128 + jw, :]
            else:
                jw = 128
                src = waps["out_proj_w"][(u - 25) * 128:(u - 24) * 128, :]
            eng.dma_start(out=wraw[0:jw, p % 4, :],
                          in_=src).then_inc(ld4[p % 4].h, 16)
    P.s(f_ld)

    def f_pe(eng, e0=ev0):
        eng.wait_ge(bufs["ld_const"].h, bufs["ld_const"].n)
        for p, u in enumerate(WP_ORDER):
            jw = 128 if (u != 24) else 16
            eng.wait_ge(ld4[p % 4].h, 16 * (p // 4 + 1))
            if p >= 2:
                eng.wait_ge(s_ev.h, e0 + p - 1)
            for dt in range(8):
                ins = eng.transpose(pWl[p % 2][:, dt * 128:dt * 128 + jw],
                                    wraw[0:jw, p % 4, dt * 128:(dt + 1) * 128],
                                    bufs["identf"][0:jw, 0:jw])
            ins.then_inc(s_pe.h, 1)
    P.t(f_pe)

    def f_ev(eng, p0=pe0):
        for p, u in enumerate(WP_ORDER):
            jw = 128 if (u != 24) else 16
            if p % 2 == 0:
                eng.wait_ge(s_pe.h, p0 + min(p + 2, 33))
            if p >= 2:
                eng.wait_ge(pk1.h, 16 * (p - 1))
            ins = eng.tensor_copy(
                wev[:, p % 2, :, 0:jw],
                pWl[p % 2][:].rearrange("p (a b) -> p a b", a=8)[:, :, 0:jw])
            ins.then_inc(s_ev.h, 1)
    P.v(f_ev)

    s_l2 = P.sem(dma=True)
    l2_sem = Sem(s_l2.h)
    l2_sem.n = s_l2.inc(16 * 3)

    def f_pk(eng, e0=ev0):
        for p, u in enumerate(WP_ORDER):
            jw = 128 if (u != 24) else 16
            eng.wait_ge(s_ev.h, e0 + p + 1)
            if p >= 1:
                eng.wait_ge(pk1.h, 16 * p)
            if u < 25:
                dst = ipT_dram[:, :, u * 128:u * 128 + jw]
            else:
                dst = opT_dram[:, :, (u - 25) * 128:(u - 24) * 128]
            eng.dma_start(out=dst, in_=wev[:, p % 2, :, 0:jw]).then_inc(pk1.h, 16)
            eng.wait_ge(pk1.h, 16 * (p + 1))
            eng.sem_inc(s_pk.h, 1)
            if u == 24:
                eng.dma_start(out=bufs["l0_wA"][:],
                              in_=ipT_dram[:, :, 3072:3088]).then_inc(s_l2.h, 16)
                eng.dma_start(out=bufs["l0_wbase"][:],
                              in_=waps["w_base"].unsqueeze(1)).then_inc(s_l2.h, 16)
                eng.dma_start(out=bufs["l0_cw"][:],
                              in_=waps["conv_w"].rearrange(
                                  "(a p) one k -> p a (one k)", p=128)
                              ).then_inc(s_l2.h, 16)
    P.y(f_pk)
    ev_sem = Sem(s_ev.h)
    ev_sem.n = ev0 + 33
    return s_pk, l2_sem, l2_sem, ev_sem


def emit_l0attn(P, bufs, waps, ln_dep, x_dep, ipT_dram, opT_dram, prep_dep,
                cw_dep, l2_dep):
    TBL = 512
    NB = T // TBL
    NCH = TBL // 64
    xres, lnT = bufs["xres"], bufs["lnT"]
    xio = bufs["l0_xio"]
    Itm, Otm, Ytm, YT = bufs["l0_Itm"], bufs["l0_Otm"], bufs["l0_Ytm"], bufs["l0_YT"]
    B1, B2, B3, B4 = bufs["l0_B1"], bufs["l0_B2"], bufs["l0_B3"], bufs["l0_B4"]
    sm = bufs["l0_small"]
    AL = sm[:, 0, 0:32]
    cumAL = sm[:, 1, :]
    cminAL = sm[:, 2, :]
    encAL = sm[:, 3, :]
    expcmAL = sm[:, 4, :]
    ndp = sm[:, 5, :]
    negmax = sm[:, 6, 0:32]
    scr33 = sm[:, 7, :]
    atm, btm, etm, dtm = bufs["l0_atm"], bufs["l0_btm"], bufs["l0_etm"], bufs["l0_dtm"]
    R, Rbf = bufs["l0_R"], bufs["l0_Rbf"]
    mt, msk, yst = bufs["l0_mt"], bufs["l0_msk"], bufs["l0_yst"]
    pT = [bufs["psumT0"], bufs["psumT1"]]
    pW0, pW1 = bufs["psumW0"], bufs["psumW1"]
    pS0, pS1 = bufs["psumS0"], bufs["psumS1"]

    s_l2 = l2_dep
    v_l2 = l2_dep.n

    cb_dep = emit_colbias(P, bufs, waps["conv_b"], bufs["l0_cb"], 24,
                          bufs["ld_const"], 0)

    s_app = P.sem()
    app0 = s_app.n
    s_app.inc(4)
    s_apa = P.sem()
    s_apd = P.sem()

    ape0 = s_apa.n
    v_araw = s_apa.inc(4)

    def f_ap_pe(eng, wl=(ln_dep.h, ln_dep.n), w2=v_l2, e0=ape0):
        eng.wait_ge(wl[0], wl[1])
        eng.wait_ge(s_l2.h, w2)
        for tc in range(4):
            if tc >= 2:
                eng.wait_ge(s_apa.h, e0 + tc - 1)
            ps = pS0 if tc % 2 == 0 else pS1
            for dt in range(ND):
                ins = eng.matmul(ps[0:16, 0:512], bufs["l0_wA"][:, dt, :],
                                 lnT[:, dt, tc * 512:(tc + 1) * 512],
                                 start=(dt == 0), stop=(dt == ND - 1))
            ins.then_inc(s_app.h, 1)
    P.t(f_ap_pe)

    def f_ap_evac(eng, a0=app0, wpk=(prep_dep.h, prep_dep.n)):
        eng.wait_ge(wpk[0], wpk[1])
        for tc in range(4):
            eng.wait_ge(s_app.h, a0 + tc + 1)
            ps = pS0 if tc % 2 == 0 else pS1
            ins = eng.activation(B1[:, tc * 512:(tc + 1) * 512], ps[0:16, 0:512],
                                 AF.Identity)
            ins.then_inc(s_apa.h, 1)
    P.s(f_ap_evac)

    v_d1 = s_apd.inc()

    def f_ap_d1(eng, wa=v_araw, w2=v_l2):
        eng.wait_ge(s_apa.h, wa)
        eng.wait_ge(s_l2.h, w2)
        eng.tensor_scalar_mul(B1[:, :], B1[:, :], bufs["l0_wbase"][:, 0:1])
        eng.drain()
        for c in range(32):
            sl = slice(c * 64, (c + 1) * 64)
            eng.tensor_tensor_scan(B2[:, sl], B1[:, sl], B1[:, sl], 0.0,
                                   ALU.add, ALU.bypass)
        eng.drain()
        for c in range(32):
            sl = slice(c * 64, (c + 1) * 64)
            eng.tensor_tensor_scan(B3[:, sl], B2[:, sl], B2[:, sl], 1e30,
                                   ALU.min, ALU.bypass)
        eng.drain()
        eng.tensor_copy(AL, B2[:].rearrange("p (c l) -> p c l", l=64)[:, :, 63])
        for c in range(32):
            eng.reduce_max(negmax[:, c:c + 1], B2[:, c * 64:(c + 1) * 64], AX.X,
                           negate=True)
        eng.drain()
        eng.memset(ndp[:, 0:1], 0.0)
        eng.memset(scr33[:, 0:1], 0.0)
        eng.drain()
        eng.tensor_copy(scr33[:, 1:33], AL)
        eng.drain()
        eng.tensor_tensor_scan(cumAL[:, :], scr33[:, :], scr33[:, :], 0.0,
                               ALU.add, ALU.bypass)
        eng.drain()
        eng.tensor_tensor_scan(cminAL[:, :], cumAL[:, :], cumAL[:, :], 1e30,
                               ALU.min, ALU.bypass)
        eng.drain().then_inc(s_apd.h, 1)
    P.v(f_ap_d1)

    v_a2 = s_apa.inc()

    def f_ap_a2(eng, wd=v_d1):
        eng.wait_ge(s_apd.h, wd)
        for c in range(32):
            sl = slice(c * 64, (c + 1) * 64)
            eng.activation(B4[:, sl], B2[:, sl], AF.Exp, scale=-1.0,
                           bias=B3[:, c * 64 + 63:c * 64 + 64])
        eng.activation(encAL[:, :], cumAL[:, :], AF.Exp, scale=-1.0)
        eng.activation(expcmAL[:, :], cminAL[:, :], AF.Exp)
        eng.drain().then_inc(s_apa.h, 1)
    P.s(f_ap_a2)

    v_d2 = s_apd.inc()

    def f_ap_d2(eng, wa=v_a2):
        eng.wait_ge(s_apa.h, wa)
        for c in range(32):
            eng.reduce_sum(ndp[:, c + 1:c + 2], B4[:, c * 64:(c + 1) * 64], AX.X)
        eng.drain()
        for c in range(32):
            eng.tensor_scalar_mul(B4[:, c * 64:(c + 1) * 64],
                                  B4[:, c * 64:(c + 1) * 64], encAL[:, c + 1:c + 2])
        eng.drain()
        eng.tensor_tensor(scr33[:, :], ndp[:, :], encAL[:, :], ALU.mult)
        eng.drain()
        eng.tensor_tensor_scan(cumAL[:, :], scr33[:, :], scr33[:, :], 0.0,
                               ALU.add, ALU.bypass)
        eng.drain()
        eng.tensor_tensor(scr33[:, :], cumAL[:, :], expcmAL[:, :], ALU.mult)
        eng.drain().then_inc(s_apd.h, 1)
    P.v(f_ap_d2)

    v_a3 = s_apa.inc()

    def f_ap_a3(eng, wd=v_d2):
        eng.wait_ge(s_apd.h, wd)
        eng.activation(B1[:, :], B2[:, :], AF.Exp, scale=-1.0)
        eng.activation(B3[:, :], B3[:, :], AF.Exp)
        eng.drain().then_inc(s_apa.h, 1)
    P.s(f_ap_a3)

    s_tmp = P.sem()
    tmp0 = s_tmp.n
    s_tmp.inc(2)
    s_tmd = P.sem()
    tmd0 = s_tmd.n
    s_tmd.inc(2)

    def f_tm_pe1(eng, wa=v_a3, d0=tmd0):
        eng.wait_ge(s_apa.h, wa)
        for i, src in enumerate([B4, B1]):
            if i >= 1:
                eng.wait_ge(s_tmd.h, d0 + i)
            for tt in range(16):
                ins = eng.transpose(pW0[:, tt * 16:(tt + 1) * 16],
                                    src[:, tt * 128:(tt + 1) * 128],
                                    bufs["identf"][0:16, 0:16])
            ins.then_inc(s_tmp.h, 1)
    P.t(f_tm_pe1)

    def f_tm_d1(eng, p0=tmp0):
        for i, dst in enumerate([dtm, etm]):
            eng.wait_ge(s_tmp.h, p0 + i + 1)
            ins = eng.tensor_copy(dst[:],
                                  pW0[:, 0:256].rearrange("p (a b) -> p a b", a=16))
            ins.then_inc(s_tmd.h, 1)
    P.v(f_tm_d1)

    v_d4 = s_apd.inc()

    def f_ap_d4(eng, d0=tmd0):
        eng.wait_ge(s_tmd.h, d0 + 2)
        for c in range(32):
            sl = slice(c * 64, (c + 1) * 64)
            eng.tensor_tensor_scan(B4[:, sl], B1[:, sl], B1[:, sl], 0.0,
                                   ALU.add, ALU.bypass)
        eng.drain()
        eng.tensor_tensor(B4[:, :], B4[:, :], B3[:, :], ALU.mult)
        eng.drain().then_inc(s_apd.h, 1)
    P.v(f_ap_d4)

    v_a4 = s_apa.inc()

    def f_ap_a4(eng, wd=v_d4):
        eng.wait_ge(s_apd.h, wd)
        for c in range(32):
            sl = slice(c * 64, (c + 1) * 64)
            eng.activation(B1[:, sl], B2[:, sl], AF.Exp,
                           bias=negmax[:, c:c + 1])
        eng.drain().then_inc(s_apa.h, 1)
    P.s(f_ap_a4)

    v_d5 = s_apd.inc()

    def f_ap_d5(eng, wa=v_a4):
        eng.wait_ge(s_apa.h, wa)
        for c in range(32):
            sl = slice(c * 64, (c + 1) * 64)
            eng.scalar_tensor_tensor(B4[:, sl], B1[:, sl], scr33[:, c:c + 1],
                                     B4[:, sl], ALU.mult, ALU.add)
        eng.drain()
        eng.reciprocal(B4[:, :], B4[:, :])
        eng.drain()
        eng.tensor_tensor(B3[:, :], B3[:, :], B4[:, :], ALU.mult)
        eng.drain()
        for c in range(32):
            sl = slice(c * 64, (c + 1) * 64)
            eng.scalar_tensor_tensor(B2[:, sl], B1[:, sl], expcmAL[:, c:c + 1],
                                     B4[:, sl], ALU.mult, ALU.mult)
        eng.drain().then_inc(s_apd.h, 1)
    P.v(f_ap_d5)

    s_tp2 = P.sem()
    tp20 = s_tp2.n
    s_tp2.inc(2)
    s_td2 = P.sem()
    td20 = s_td2.n
    s_td2.inc(2)

    def f_tm_pe2(eng, wd=v_d5, d0=td20):
        eng.wait_ge(s_apd.h, wd)
        for i, src in enumerate([B3, B2]):
            if i >= 1:
                eng.wait_ge(s_td2.h, d0 + i)
            for tt in range(16):
                ins = eng.transpose(pW0[:, tt * 16:(tt + 1) * 16],
                                    src[:, tt * 128:(tt + 1) * 128],
                                    bufs["identf"][0:16, 0:16])
            ins.then_inc(s_tp2.h, 1)
    P.t(f_tm_pe2)

    def f_tm_d2(eng, p0=tp20):
        for i, dst in enumerate([atm, btm]):
            eng.wait_ge(s_tp2.h, p0 + i + 1)
            ins = eng.tensor_copy(dst[:],
                                  pW0[:, 0:256].rearrange("p (a b) -> p a b", a=16))
            ins.then_inc(s_td2.h, 1)
    P.v(f_tm_d2)

    s_r0 = P.sem()
    v_r0 = s_r0.inc()

    def f_r0(eng):
        eng.memset(R[:], 0.0)
        eng.memset(Rbf[:], 0.0)
        eng.drain().then_inc(s_r0.h, 1)
    P.v(f_r0)

    ipl4 = [P.sem(dma=True) for _ in range(4)]
    s_opl = P.sem(dma=True)
    opl0 = s_opl.n
    s_opl.inc(16 * NB * 2)
    s_ip = P.sem()
    ip0 = s_ip.n
    s_ip.inc(NB * 24)
    s_ipe = P.sem()
    ipe0 = s_ipe.n
    s_ipe.inc(NB * 24)
    P.l0_pace = (s_ipe.h, ipe0, NB * 24)
    s_hl = P.sem()
    hl0 = s_hl.n
    s_hl.inc(NB)
    s_dg = P.sem()
    dg0 = s_dg.n
    s_dg.inc(NB * 24)
    s_cv = P.sem()
    cv0 = s_cv.n
    s_cv.inc(NB * 24)
    s_cve = P.sem()
    cve0 = s_cve.n
    s_cve.inc(NB * 24)
    s_tr = P.sem()
    tr0 = s_tr.n
    s_tr.inc(NB * 32)
    s_tre = P.sem()
    tre0 = s_tre.n
    s_tre.inc(NB * 32)
    s_gt = P.sem()
    gt0 = s_gt.n
    s_gt.inc(NB * NCH * 2)
    s_mte = P.sem()
    mte0 = s_mte.n
    s_mte.inc(NB * NCH * 2)
    s_yd = P.sem()
    yd0 = s_yd.n
    s_yd.inc(NB * NCH)
    s_st = P.sem()
    st0 = s_st.n
    s_st.inc(NB * NCH)
    s_rup = P.sem()
    rup0 = s_rup.n
    s_rup.inc(NB * NCH)
    s_ye = P.sem()
    ye0 = s_ye.n
    s_ye.inc(NB * (NCH // 2) * 2)
    s_yt = P.sem()
    yt0 = s_yt.n
    s_yt.inc(NB * 4)
    s_yte = P.sem()
    yte0 = s_yte.n
    s_yte.inc(NB * 4)
    s_op = P.sem()
    op0 = s_op.n
    s_op.inc(NB * 8)
    s_ope = P.sem()
    ope0 = s_ope.n
    s_ope.inc(NB * 8)
    s_res = P.sem()
    res0 = s_res.n
    s_res.inc(NB * 8)

    def f_blk_dma(eng, wpk=(prep_dep.h, prep_dep.n), wln2=(ln_dep.h, ln_dep.n)):
        if L0_STAGE < 2:
            return
        eng.wait_ge(wpk[0], wpk[1])
        for b in range(NB):
            for jc in range(24):
                u = b * 24 + jc
                if u >= 4:
                    eng.wait_ge(s_ip.h, ip0 + u - 3)
                elif u == 2:
                    eng.wait_ge(wln2[0], wln2[1])
                dbuf = bufs["l0_ipw"] if u % 4 < 2 else bufs["l0_ipw2"]
                eng.dma_start(out=dbuf[:, u % 2],
                              in_=ipT_dram[:, :, jc * 128:(jc + 1) * 128]
                              ).then_inc(ipl4[u % 4].h, 16)
            for half in range(2 if L0_STAGE >= 6 else 0):
                u8 = b * 2 + half
                if u8 >= 2:
                    eng.wait_ge(s_op.h, op0 + (u8 - 1) * 4)
                if u8 >= 1:
                    eng.wait_ge(s_opl.h, opl0 + 16 * u8)
                eng.dma_start(out=bufs["l0_opw"][:, u8 % 2],
                              in_=opT_dram[:, :, half * 512:(half + 1) * 512]).then_inc(s_opl.h, 16)
    P.y(f_blk_dma)

    def f_blk_pe(eng, wln=(ln_dep.h, ln_dep.n), wr0=v_r0, wd2=td20):
        if L0_STAGE < 2:
            return
        eng.wait_ge(bufs["ld_const"].h, bufs["ld_const"].n)
        eng.wait_ge(wln[0], wln[1])
        eng.wait_ge(s_r0.h, wr0)
        eng.wait_ge(s_td2.h, wd2 + 2)
        for b in range(NB):
            for jc in range(24):
                u = b * 24 + jc
                eng.wait_ge(ipl4[u % 4].h, 16 * (u // 4 + 1))
                if u >= 4:
                    eng.wait_ge(s_ipe.h, ipe0 + u - 3)
                if b >= 1 and jc == 0:
                    if L0_STAGE >= 3:
                        eng.wait_ge(s_hl.h, hl0 + b)
                    if L0_STAGE >= 5:
                        eng.wait_ge(s_rup.h, rup0 + b * NCH)
                        eng.wait_ge(s_mte.h, mte0 + b * NCH * 2)
                if jc == 2:
                    if b == 0:
                        eng.wait_ge(cb_dep.h, cb_dep.n)
                    elif L0_STAGE >= 6:
                        eng.wait_ge(s_ope.h, ope0 + b * 8 - 2)
                    elif L0_STAGE >= 5:
                        eng.wait_ge(s_ye.h, ye0 + b * 8)
                ps = [pS0[:, :], pS1[:, :], pW1[:, 0:512],
                      pW1[:, 512:1024]][u % 4]
                wbuf = bufs["l0_ipw"] if u % 4 < 2 else bufs["l0_ipw2"]
                for dt in range(ND):
                    ins = eng.matmul(ps, wbuf[:, u % 2, dt, :],
                                     lnT[:, dt, b * 512:(b + 1) * 512],
                                     start=(dt == 0), stop=(dt == ND - 1))
                ins.then_inc(s_ip.h, 1)
            for ct in range(24 if L0_STAGE >= 3 else 0):
                u = b * 24 + ct
                eng.wait_ge(s_dg.h, dg0 + u + 1)
                if u >= 2:
                    eng.wait_ge(s_cve.h, cve0 + u - 1)
                if b >= 1 and ct < 2 and L0_STAGE >= 6:
                    eng.wait_ge(s_ope.h, ope0 + b * 8)
                ps = pW0[:, 0:512] if ct % 2 == 0 else pW0[:, 512:1024]
                dgb = bufs["l0_diag"] if ct % 2 == 0 else bufs["l0_diag2"]
                for k in range(4):
                    ins = eng.matmul(ps, dgb[:, k, :],
                                     xio[:, ct, k:k + 512],
                                     start=(k == 0), stop=(k == 3))
                ins.then_inc(s_cv.h, 1)
            for i in range(2 if L0_STAGE >= 4 else 0):
                for g in range(16):
                    u = b * 32 + i * 16 + g
                    ctl, tc = divmod(g, 4)
                    eng.wait_ge(s_cve.h, cve0 + b * 24 + 24)
                    if u >= 2:
                        eng.wait_ge(s_tre.h, tre0 + u - 1)
                    ps = pT[u % 2]
                    for sub in range(2):
                        ct = 8 + i * 8 + ctl * 2 + sub
                        ins = eng.transpose(ps[:, sub * 128:(sub + 1) * 128],
                                            xio[:, ct, 3 + tc * 128:3 + (tc + 1) * 128],
                                            bufs["ident"][:])
                    ins.then_inc(s_tr.h, 1)
            for cl in range(NCH if L0_STAGE >= 5 else 0):
                c = b * NCH + cl
                par = c % 2
                tt = c // 2
                eng.wait_ge(s_tre.h, tre0 + b * 32 + 32)
                if c >= 1:
                    eng.wait_ge(s_mte.h, mte0 + c * 2)
                for hg in range(2):
                    eng.drain()
                    for hh in [0, 2, 4, 6, -1, 1, 3, 5, 7]:
                        if hh < 0:
                            eng.drain()
                            continue
                        h = hg * 8 + hh
                        hb = (h % 2) * 64
                        ins = eng.matmul(
                            pS0[hg * 64:hg * 64 + 64, hh * 64:(hh + 1) * 64],
                            xio[hb:hb + 64, 8 + h // 2, 3 + cl * 64:3 + (cl + 1) * 64],
                            xio[hb:hb + 64, h // 2, 3 + cl * 64:3 + (cl + 1) * 64],
                            start=(hh == 0), stop=True, skip_group_check=True,
                            tile_position=(hb, hg * 64))
                    ins.then_inc(s_gt.h, 1)
                eng.wait_ge(s_rup.h, rup0 + c)
                if cl == 0:
                    eng.wait_ge(s_ipe.h, ipe0 + (b + 1) * 24)
                if par == 0:
                    eng.wait_ge(s_ye.h, ye0 + (c // 2) * 2)
                eng.drain()
                for h in [0, 2, 4, 6, 8, 10, 12, 14, -1, 1, 3, 5, 7, 9, 11, 13, 15]:
                    if h < 0:
                        eng.drain()
                        continue
                    hb = (h % 2) * 64
                    half = (h // 8) * 512
                    eng.matmul(pW1[par * 64:par * 64 + 64,
                                   half + (h % 8) * 64:half + (h % 8) * 64 + 64],
                               xio[hb:hb + 64, h // 2, 3 + cl * 64:3 + (cl + 1) * 64],
                               Rbf[hb:hb + 64, h // 2, :],
                               start=(h in (0, 8)), stop=True,
                               skip_group_check=True,
                               tile_position=(hb, par * 64))
                eng.drain()
                for h in range(16):
                    hb = (h % 2) * 64
                    ins = eng.matmul(pS1[hb:hb + 64, (h // 2) * 64:(h // 2) * 64 + 64],
                                     Itm[par * 64:par * 64 + 64, cl // 2,
                                         h * 64:(h + 1) * 64],
                                     Otm[par * 64:par * 64 + 64, cl // 2,
                                         h * 64:(h + 1) * 64],
                                     start=(h < 2), stop=True, skip_group_check=True,
                                     tile_position=(par * 64, hb))
                ins.then_inc(s_st.h, 1)
                eng.drain()
                eng.wait_ge(s_mte.h, mte0 + c * 2 + 2)
                for h in range(16):
                    half = (h // 8) * 512
                    ins = eng.matmul(pW0[par * 64:par * 64 + 64,
                                         half + (h % 8) * 64:half + (h % 8) * 64 + 64],
                                     mt[par * 64:par * 64 + 64, h // 8, h % 8, :],
                                     Otm[par * 64:par * 64 + 64, cl // 2,
                                         h * 64:(h + 1) * 64],
                                     start=(h % 8 == 0), stop=True,
                                     skip_group_check=True,
                                     tile_position=(par * 64, par * 64))
                    ins.then_inc(s_yd.h, 1) if h == 15 else None
            if L0_STAGE < 6:
                continue
            for g in range(4):
                u = b * 4 + g
                eng.wait_ge(s_ye.h, ye0 + b * 8 + (g + 1) * 2)
                if u >= 2:
                    eng.wait_ge(s_yte.h, yte0 + u - 1)
                ps = pT[u % 2]
                for ct in range(8):
                    ins = eng.transpose(ps[:, ct * 128:(ct + 1) * 128],
                                        Ytm[:, g, ct * 128:(ct + 1) * 128],
                                        bufs["ident"][:])
                ins.then_inc(s_yt.h, 1)
            for half in range(2):
                u8 = b * 2 + half
                eng.wait_ge(s_opl.h, opl0 + 16 * (u8 + 1))
                for g in range(4):
                    u = u8 * 4 + g
                    eng.wait_ge(s_yte.h, yte0 + b * 4 + g + 1)
                    if u >= 4:
                        eng.wait_ge(s_ope.h, ope0 + u - 3)
                    ps = [pW1[:, 0:512], pW1[:, 512:1024],
                          pW0[:, 0:512], pW0[:, 512:1024]][u % 4]
                    sl = slice(half * 512, (half + 1) * 512)
                    for ctt in range(8):
                        ins = eng.matmul(ps, YT[:, g, ctt, :],
                                         bufs["l0_opw"][:, u8 % 2, ctt, :],
                                         start=(ctt == 0), stop=(ctt == 7))
                    ins.then_inc(s_op.h, 1)
    P.t(f_blk_pe)

    def f_blk_act(eng):
        if L0_STAGE < 2:
            return

        def ope_one(u):
            eng.wait_ge(s_op.h, op0 + u + 1)
            if u >= 2:
                eng.wait_ge(s_res.h, res0 + u - 1)
            ps = [pW1[:, 0:512], pW1[:, 512:1024],
                  pW0[:, 0:512], pW0[:, 512:1024]][u % 4]
            eng.activation(bufs["l0_opsc"][:, u % 2, :], ps,
                           AF.Identity).then_inc(s_ope.h, 1)

        for b in range(NB):
            for jc in range(24):
                u = b * 24 + jc
                if jc % 2 == 0:
                    eng.wait_ge(s_ip.h, ip0 + min(u + 2, b * 24 + 24))
                if b >= 1 and jc == 0 and L0_STAGE >= 3:
                    eng.wait_ge(s_hl.h, hl0 + b)
                ps = [pS0[:, :], pS1[:, :], pW1[:, 0:512],
                      pW1[:, 512:1024]][u % 4]
                ins = eng.activation(xio[:, jc, 3:515], ps, AF.Identity)
                ins.then_inc(s_ipe.h, 1)
                if b >= 1 and L0_STAGE >= 6:
                    if jc < 2:
                        ope_one((b - 1) * 8 + 4 + jc)
                    elif 16 <= jc < 18:
                        ope_one((b - 1) * 8 + 6 + jc - 16)
            for ct in range(24 if L0_STAGE >= 3 else 0):
                u = b * 24 + ct
                eng.wait_ge(s_cv.h, cv0 + u + 1)
                ps = pW0[:, 0:512] if ct % 2 == 0 else pW0[:, 512:1024]
                ins = eng.activation(xio[:, ct, 3:515], ps, AF.Identity,
                                     bias=bufs["l0_cb"][:, ct:ct + 1])
                ins.then_inc(s_cve.h, 1)
            if L0_STAGE < 6:
                continue
            for g in range(4):
                if g % 2 == 0:
                    eng.wait_ge(s_yt.h, yt0 + b * 4 + g + 2)
                ins = eng.activation(YT[:, g],
                                     pT[(b * 4 + g) % 2][:]
                                     .rearrange("p (a b) -> p a b", a=8),
                                     AF.Identity)
                ins.then_inc(s_yte.h, 1)
            for k in range(4):
                ope_one(b * 8 + k)
        if L0_STAGE >= 6:
            for u in range((NB - 1) * 8 + 4, NB * 8):
                ope_one(u)
    P.s(f_blk_act)

    def f_blk_dve(eng, wcb=(cb_dep.h, cb_dep.n), wr0=v_r0, wd2=td20,
                  wcw=(cw_dep.h, cw_dep.n)):
        if L0_STAGE < 2:
            return
        eng.wait_ge(wcb[0], wcb[1])
        eng.wait_ge(wcw[0], wcw[1])
        eng.wait_ge(s_r0.h, wr0)
        eng.wait_ge(s_td2.h, wd2 + 2)
        for b in range(NB):
            if L0_STAGE < 3:
                continue
            eng.wait_ge(s_ipe.h, ipe0 + b * 24 + 24)
            if b == 0:
                eng.memset(xio[:, :, 0:3], 0.0)
            else:
                eng.tensor_copy(xio[:, :, 0:3], bufs["l0_halo"][:])
            eng.drain()
            eng.tensor_copy(bufs["l0_halo"][:], xio[:, :, 512:515])
            eng.drain().then_inc(s_hl.h, 1)
            for ct in range(24):
                u = b * 24 + ct
                if ct % 2 == 0:
                    eng.wait_ge(s_cv.h, cv0 + u)
                dg = bufs["l0_diag"] if ct % 2 == 0 else bufs["l0_diag2"]
                ins = eng.tensor_tensor(
                    dg[:],
                    bufs["ident"][:, None, :].broadcast_to([128, 4, 128]),
                    bufs["l0_cw"][:, ct, :, None].broadcast_to([128, 4, 128]),
                    ALU.mult)
                ins.then_inc(s_dg.h, 1)
            for i in range(2 if L0_STAGE >= 4 else 0):
                for g in range(16):
                    u = b * 32 + i * 16 + g
                    ctl, tc = divmod(g, 4)
                    eng.wait_ge(s_tr.h, tr0 + u + 1)
                    ps = pT[u % 2]
                    tt = b * 4 + tc
                    if i == 0:
                        ins = eng.tensor_tensor(
                            Itm[:, tc, ctl * 256:(ctl + 1) * 256]
                            .rearrange("p (a b) -> p a b", a=4),
                            ps[:, 0:256].rearrange("p (a b) -> p a b", a=4),
                            dtm[:, tt, ctl * 4:(ctl + 1) * 4][:, :, None]
                            .broadcast_to([128, 4, 64]), ALU.mult)
                    else:
                        ins = eng.tensor_copy(Otm[:, tc, ctl * 256:(ctl + 1) * 256],
                                              ps[:, 0:256])
                    ins.then_inc(s_tre.h, 1)
            for cl in range(NCH if L0_STAGE >= 5 else 0):
                c = b * NCH + cl
                par = c % 2
                tt = c // 2
                for hg in range(2):
                    v = c * 2 + hg
                    eng.wait_ge(s_gt.h, gt0 + v + 1)
                    eng.tensor_tensor(
                        msk[par * 64:par * 64 + 64, hg],
                        bufs["trimask"][par * 64:par * 64 + 64, None, :]
                        .broadcast_to([64, 8, 64]),
                        etm[par * 64:par * 64 + 64, tt, hg * 8:(hg + 1) * 8]
                        [:, :, None].broadcast_to([64, 8, 64]), ALU.mult)
                    eng.drain()
                    ins = eng.tensor_tensor(
                        mt[par * 64:par * 64 + 64, hg],
                        pS0[hg * 64:hg * 64 + 64, :]
                        .rearrange("p (a b) -> p a b", a=8),
                        msk[par * 64:par * 64 + 64, hg], ALU.mult)
                    ins.then_inc(s_mte.h, 1)
                eng.wait_ge(s_st.h, st0 + c + 1)
                eng.tensor_tensor(R[:], pS1[:, :].rearrange("p (a b) -> p a b", a=8),
                                  R[:], ALU.add)
                eng.drain()
                eng.tensor_copy(Rbf[:], R[:])
                eng.drain().then_inc(s_rup.h, 1)
                if par == 1:
                    eng.wait_ge(s_yd.h, yd0 + c + 1)
                    if b >= 1:
                        eng.wait_ge(s_op.h, op0 + (b - 1) * 8 + 4 + (c // 2) % 4 + 1)
                    for hg in range(2):
                        npair = (c // 2) * 2 + hg
                        if npair >= 1:
                            eng.wait_ge(s_ye.h, ye0 + npair)
                        sl = slice(hg * 512, (hg + 1) * 512)
                        eng.tensor_tensor(
                            yst[:, 0, :].rearrange("p (a b) -> p a b", a=8),
                            pW0[:, sl].rearrange("p (a b) -> p a b", a=8),
                            atm[:, tt, hg * 8:(hg + 1) * 8][:, :, None]
                            .broadcast_to([128, 8, 64]), ALU.mult)
                        eng.tensor_tensor(
                            yst[:, 1, :].rearrange("p (a b) -> p a b", a=8),
                            pW1[:, sl].rearrange("p (a b) -> p a b", a=8),
                            btm[:, tt, hg * 8:(hg + 1) * 8][:, :, None]
                            .broadcast_to([128, 8, 64]), ALU.mult)
                        eng.drain()
                        ins = eng.tensor_tensor(Ytm[:, tt % 4, sl], yst[:, 0, :],
                                                yst[:, 1, :], ALU.add)
                        ins.then_inc(s_ye.h, 1)
                    eng.drain()
            if L0_STAGE < 6:
                continue
            for half in range(2):
                for g in range(4):
                    u = b * 8 + half * 4 + g
                    eng.wait_ge(s_ope.h, ope0 + u + 1)
                    sl = slice(half * 512, (half + 1) * 512)
                    ins = eng.tensor_tensor(xres[:, b * 4 + g, sl],
                                            bufs["l0_opsc"][:, u % 2, :],
                                            xres[:, b * 4 + g, sl], ALU.add)
                    ins.then_inc(s_res.h, 1)
    P.v(f_blk_dve)
    if L0_STAGE == 1:
        return s_td2
    if L0_STAGE == 2:
        return s_ipe
    if L0_STAGE == 3:
        return s_cve
    if L0_STAGE == 4:
        return s_tre
    if L0_STAGE == 5:
        return s_rup
    return s_res



def declare_moe(P, bufs):
    bufs["mo_wgT"] = P.sb("mo_wgT", [128, 8, 8], BF16)
    bufs["mo_wgraw"] = P.sb("mo_wgraw", [8, 1024], BF16)
    bufs["mo_lg"] = P.sb("mo_lg", [128, 16, 8], F32)
    bufs["mo_g1"] = P.sb("mo_g1", [128, 16, 8], F32)
    bufs["mo_g2"] = P.sb("mo_g2", [128, 16, 8], F32)
    bufs["mo_gs"] = P.sb("mo_gs", [128, 6, 16], F32)
    bufs["mo_gates"] = P.sb("mo_gates", [128, 16, 8], F32)
    bufs["mo_woT"] = P.sb("mo_woT", [128, 32, 1024], BF16)
    bufs["mo_win"] = P.sb_at("mo_win", [128, 32, 1024], BF16, P.offsets["xres"])
    bufs["mo_wev"] = P.sb("mo_wev", [128, 8, 1024], BF16)
    bufs["mo_Wacc"] = P.sb_at("mo_Wacc", [128, 8, 1024], F32, P.offsets["lnT"])
    bufs["mo_weT"] = P.sb_at("mo_weT", [128, 2, 8, 1024], BF16, P.offsets["mo_woT"])


def emit_moe_prelude(P, waps, wo_bf, pace=None):
    wob = [P.sem(dma=True), P.sem(dma=True)]
    n_chunks = 64

    def f_wob(eng):
        for k in range(n_chunks):
            e, r = divmod(k, 8)
            if pace is not None:
                h, base, total = pace
                eng.wait_ge(h, base + min((k * total) // n_chunks + 1, total))
            if k >= 2:
                eng.wait_ge(wob[k % 2].h, 16 * (k // 2))
            eng.dma_start(out=wo_bf[e][r * 128:(r + 1) * 128, :],
                          in_=waps["w_out"][e][r * 128:(r + 1) * 128, :]
                          ).then_inc(wob[k % 2].h, 16)
    P.g(f_wob)
    return wob


def emit_moe(P, bufs, waps, x_dep, weT_dram, wo_bf, spill_ap, pp_wob):
    xres = bufs["xres"]
    lnT = bufs["lnT"]
    woT, win, wev = bufs["mo_woT"], bufs["mo_win"], bufs["mo_wev"]
    Wacc = bufs["mo_Wacc"]
    pT0 = bufs["psumT0"]
    pW = [bufs["psumW0"], bufs["psumW1"]]
    pSb = [bufs["psumS0"], bufs["psumS1"]]

    s_wTh = [P.sem(dma=True), P.sem(dma=True)]
    wTh0 = [s.n for s in s_wTh]
    for s in s_wTh:
        s.inc(16 * 8 * 16)
    pp_winl = PPSem(P)
    s_msp = P.sem(dma=True)
    v_msp = s_msp.inc(16)
    s_mre = P.sem(dma=True)
    mre_sem = Sem(s_mre.h)
    v_mre = s_mre.inc(16)
    mre_sem.n = v_mre
    s_pmm = P.sem()
    pmm0 = s_pmm.n
    s_pmm.inc(8 * 16)
    s_pac = P.sem()
    pac0 = s_pac.n
    s_pac.inc(8 * 8)
    s_pdv = P.sem()
    pdv0 = s_pdv.n
    s_pdv.inc(8 * 8)
    pp_park = PPSem(P)

    def f_spill(eng, wx=(x_dep.h, x_dep.n)):
        eng.wait_ge(wx[0], wx[1])
        eng.dma_start(out=spill_ap, in_=xres[:]).then_inc(s_msp.h, 16)
    P.y(f_spill)

    ln_ctx = emit_ln_AB(P, bufs, x_dep)
    LN_A_BOX = ln_ctx["a_done"]

    def f_dma_pre(eng, wsp=v_msp, wx=(x_dep.h, x_dep.n), m0=pmm0,
                  wln=LN_A_BOX):
        eng.wait_ge(wx[0], wx[1])
        eng.wait_ge(s_msp.h, wsp)
        for h_, v_ in wln:
            eng.wait_ge(h_, v_)
        for s in range(16):
            e, half = divmod(s, 2)
            if s >= 2:
                eng.wait_ge(s_pmm.h, m0 + (s - 1) * 8)
            eng.dma_start(out=win[:, half * 16:(half + 1) * 16, :],
                          in_=waps["w_in"][e].rearrange("(a p) d -> p a d", p=128)
                          [:, half * 16:(half + 1) * 16, :]
                          ).then_inc(pp_winl.inc(s), 16)
    P.g(f_dma_pre)

    def f_trans(eng, wx=(x_dep.h, x_dep.n), m0=pmm0):
        eng.wait_ge(wx[0], wx[1])
        for e in range(8):
            eng.wait_ge(pp_wob[0].h, 16 * 4 * (e + 1))
            eng.wait_ge(pp_wob[1].h, 16 * 4 * (e + 1))
            for half in range(2):
                if e >= 1:
                    eng.wait_ge(s_pmm.h, m0 + (e - 1) * 16 + (half + 1) * 8)
                for fs in range(half * 16, half * 16 + 16):
                    if fs % 4 == 0 and fs != half * 16:
                        eng.wait_ge(s_wTh[half].h,
                                    wTh0[half] + 16 * (e * 16 + (fs - half * 16)))
                    eng.dma_start(out=woT[:, fs, :],
                                  in_=wo_bf[e][:, fs * 128:(fs + 1) * 128],
                                  transpose=True).then_inc(s_wTh[half].h, 16)
    P.y(f_trans)

    def f_restore(eng):
        eng.wait_ge(s_pmm.h, pmm0 + 128)
        eng.dma_start(out=xres[:], in_=spill_ap).then_inc(s_mre.h, 16)
    P.y(f_restore)

    def f_pre_pe(eng, a0=pac0, d0=pdv0, wx=(x_dep.h, x_dep.n)):
        eng.wait_ge(bufs["ld_const"].h, bufs["ld_const"].n)
        eng.wait_ge(wx[0], wx[1])
        for e in range(8):
            for half in range(2):
                sl = e * 2 + half
                eng.wait_ge(*pp_winl.tgt(sl))
                for dt in range(8):
                    v = sl * 8 + dt
                    if v >= 3:
                        pv = v - 3
                        pe_, pdt = pv // 16, pv % 8
                        if (pv % 16) // 8 == 0:
                            eng.wait_ge(s_pac.h, a0 + pe_ * 8 + pdt + 1)
                        else:
                            eng.wait_ge(s_pdv.h, d0 + pe_ * 8 + pdt + 1)
                    sel = v % 3
                    for fs in range(16):
                        if dt == 0 and fs % 4 == 0:
                            eng.wait_ge(s_wTh[half].h,
                                        wTh0[half] + 16 * (e * 16 + fs + 4))
                        fg = half * 16 + fs
                        for nh in range(2):
                            out = (pW[sel][:, nh * 512:(nh + 1) * 512]
                                   if sel < 2 else pSb[nh][:, :])
                            ins = eng.matmul(out,
                                             win[:, fg, dt * 128:(dt + 1) * 128],
                                             woT[:, fg, nh * 512:(nh + 1) * 512],
                                             start=(fs == 0), stop=(fs == 15))
                    ins.then_inc(s_pmm.h, 1)
    P.t(f_pre_pe)

    def f_pre_act(eng, m0=pmm0, d0=pdv0, wx=(x_dep.h, x_dep.n)):
        eng.wait_ge(wx[0], wx[1])
        for e in range(8):
            for dt in range(8):
                v = e * 16 + dt
                eng.wait_ge(s_pmm.h, m0 + v + 1)
                if e >= 1:
                    eng.wait_ge(s_pdv.h, d0 + (e - 1) * 8 + dt + 1)
                if v % 3 < 2:
                    ins = eng.activation(Wacc[:, dt, :], pW[v % 3][:, :],
                                         AF.Identity)
                else:
                    eng.activation(Wacc[:, dt, 0:512], pSb[0][:, :], AF.Identity)
                    ins = eng.activation(Wacc[:, dt, 512:1024], pSb[1][:, :],
                                         AF.Identity)
                ins.then_inc(s_pac.h, 1)
            if e >= 1:
                for dt in range(8):
                    u = (e - 1) * 8 + dt
                    eng.wait_ge(s_pdv.h, d0 + u + 1)
                    if u >= 2:
                        eng.wait_ge(pp_park.sems[u % 2].h, 16 * (u // 2))
                    eng.dma_start(out=weT_dram[e - 1, :, dt, :],
                                  in_=wev[:, dt, :]).then_inc(pp_park.inc(u), 16)
        for dt in range(8):
            u = 7 * 8 + dt
            eng.wait_ge(s_pdv.h, d0 + u + 1)
            eng.wait_ge(pp_park.sems[u % 2].h, 16 * (u // 2))
            eng.dma_start(out=weT_dram[7, :, dt, :],
                          in_=wev[:, dt, :]).then_inc(pp_park.inc(u), 16)
    P.s(f_pre_act)

    def f_pre_dve(eng, m0=pmm0, a0=pac0):
        for e in range(8):
            for dt in range(8):
                v = e * 16 + 8 + dt
                u = e * 8 + dt
                eng.wait_ge(s_pmm.h, m0 + v + 1)
                eng.wait_ge(s_pac.h, a0 + u + 1)
                if u >= 8:
                    eng.wait_ge(*pp_park.tgt(u - 8))
                if v % 3 < 2:
                    ins = eng.tensor_tensor(wev[:, dt, :], pW[v % 3][:, :],
                                            Wacc[:, dt, :], ALU.add)
                else:
                    eng.tensor_tensor(wev[:, dt, 0:512], pSb[0][:, :],
                                      Wacc[:, dt, 0:512], ALU.add)
                    ins = eng.tensor_tensor(wev[:, dt, 512:1024], pSb[1][:, :],
                                            Wacc[:, dt, 512:1024], ALU.add)
                ins.then_inc(s_pdv.h, 1)
    P.v(f_pre_dve)

    ln_dep = emit_ln_CD(P, bufs, ln_ctx, psum_dep=(x_dep.h, x_dep.n),
                        cx_dep=mre_sem)

    s_wg = P.sem(dma=True)
    v_wg = s_wg.inc(16)

    def f_ldwg(eng):
        eng.dma_start(out=bufs["mo_wgraw"][:], in_=waps["w_gate"]).then_inc(s_wg.h, 16)
    P.g(f_ldwg)

    s_gpe = P.sem()
    gpe0 = s_gpe.n
    s_gpe.inc(17)
    s_gev = P.sem()
    gev0 = s_gev.n
    s_gev.inc(17)

    def f_g_pe(eng, w=v_wg, wl=ln_dep.n, g0=gev0, a0=pac0, d0=pdv0):
        eng.wait_ge(s_wg.h, w)
        eng.wait_ge(ln_dep.h, wl)
        eng.wait_ge(s_pac.h, a0 + 64)
        eng.wait_ge(s_pdv.h, d0 + 64)
        for a in range(8):
            ins = eng.transpose(pT0[:, a * 8:(a + 1) * 8], bufs["mo_wgraw"][:, a * 128:(a + 1) * 128],
                                bufs["ident"][0:8, 0:8])
        ins.then_inc(s_gpe.h, 1)
        eng.wait_ge(s_gev.h, g0 + 1)
        for tt in range(NT):
            if tt >= 2 and tt % 2 == 0:
                eng.wait_ge(s_gev.h, g0 + min(tt + 1, 17))
            ps = pSb[tt % 2]
            for dt in range(ND):
                ins = eng.matmul(ps[:, 0:8], lnT[:, dt, tt * 128:(tt + 1) * 128],
                                 bufs["mo_wgT"][:, dt, :],
                                 start=(dt == 0), stop=(dt == ND - 1))
            ins.then_inc(s_gpe.h, 1)
    P.t(f_g_pe)

    def f_g_ev(eng, g0=gpe0):
        eng.wait_ge(s_gpe.h, g0 + 1)
        ins = eng.tensor_copy(bufs["mo_wgT"][:, :, :],
                              pT0[:, 0:64].rearrange("p (a b) -> p a b", a=8))
        ins.then_inc(s_gev.h, 1)
        for tt in range(NT):
            if tt % 2 == 0:
                eng.wait_ge(s_gpe.h, g0 + min(tt + 3, 17))
            ins = eng.tensor_copy(bufs["mo_lg"][:, tt, :], pSb[tt % 2][:, 0:8])
            ins.then_inc(s_gev.h, 1)
    P.v(f_g_ev)

    lg, eqm, tmp = bufs["mo_lg"], bufs["mo_g1"], bufs["mo_g2"]
    gs = bufs["mo_gs"]
    m1n, m2n, dd, qq, invd, glo = (gs[:, i, :] for i in range(6))
    s_gate = P.sem()
    v_gate = s_gate.inc()
    s_gq = P.sem()
    v_gq = s_gq.inc()

    def f_gates_a(eng, g0=gev0):
        eng.wait_ge(s_gev.h, g0 + 17)
        eng.reduce_max(m1n, lg[:], AX.X, negate=True)
        eng.drain()
        eng.tensor_tensor(eqm[:], lg[:], m1n[:, :, None].broadcast_to([128, 16, 8]), ALU.add)
        eng.drain()
        eng.tensor_single_scalar(eqm[:], eqm[:], 0.0, ALU.is_ge)
        eng.drain()
        eng.scalar_tensor_tensor(tmp[:], eqm[:], -1e30, lg[:], ALU.mult, ALU.add)
        eng.drain()
        eng.reduce_max(m2n, tmp[:], AX.X, negate=True)
        eng.drain()
        eng.tensor_tensor(tmp[:], tmp[:], m2n[:, :, None].broadcast_to([128, 16, 8]), ALU.add)
        eng.drain()
        eng.tensor_single_scalar(tmp[:], tmp[:], 0.0, ALU.is_ge)
        eng.drain()
        eng.tensor_tensor(dd, m1n, m2n, ALU.subtract)
        eng.drain().then_inc(s_gate.h, 1)
    P.v(f_gates_a)

    def f_gates_q(eng, w=v_gate):
        eng.wait_ge(s_gate.h, w)
        eng.activation(qq, dd, AF.Exp)
        eng.drain().then_inc(s_gq.h, 1)
    P.s(f_gates_q)

    s_gate2 = P.sem()
    v_gate2 = s_gate2.inc()

    def f_gates_b(eng, w=v_gq):
        eng.wait_ge(s_gq.h, w)
        eng.tensor_scalar_add(invd, qq, 1.0)
        eng.drain()
        eng.reciprocal(invd, invd)
        eng.drain()
        eng.tensor_tensor(glo, qq, invd, ALU.mult)
        eng.drain()
        eng.tensor_tensor(bufs["mo_gates"][:], eqm[:],
                          invd[:, :, None].broadcast_to([128, 16, 8]), ALU.mult)
        eng.drain()
        eng.tensor_tensor(tmp[:], tmp[:], glo[:, :, None].broadcast_to([128, 16, 8]),
                          ALU.mult)
        eng.drain()
        eng.tensor_tensor(bufs["mo_gates"][:], bufs["mo_gates"][:], tmp[:], ALU.add)
        eng.drain().then_inc(s_gate2.h, 1)
    P.v(f_gates_b)

    weT = bufs["mo_weT"]
    pp_awl = PPSem(P)
    s_amm = P.sem()
    amm0 = s_amm.n
    s_amm.inc(8 * NT)
    s_aev = P.sem()
    aev0 = s_aev.n
    s_aev.inc(8 * NT)

    def f_ald(eng, a0=amm0, pk=(pp_park.sems[0].n, pp_park.sems[1].n)):
        eng.wait_ge(pp_park.sems[0].h, pk[0] + 16 * 32)
        eng.wait_ge(pp_park.sems[1].h, pk[1] + 16 * 32)
        for e in range(8):
            if e >= 2:
                eng.wait_ge(s_amm.h, a0 + (e - 1) * NT)
            eng.dma_start(out=weT[:, e % 2], in_=weT_dram[e]).then_inc(pp_awl.inc(e), 16)
    P.y(f_ald)

    def f_apply_pe(eng, a0=aev0, wl=ln_dep.n, pa=pac0, pd=pdv0):
        eng.wait_ge(ln_dep.h, wl)
        eng.wait_ge(s_pac.h, pa + 64)
        eng.wait_ge(s_pdv.h, pd + 64)
        for e in range(8):
            eng.wait_ge(*pp_awl.tgt(e))
            for tt in range(NT):
                j = e * NT + tt
                if j == 2:
                    eng.wait_ge(s_gev.h, gev0 + 17)
                if j >= 3:
                    eng.wait_ge(s_aev.h, a0 + j - 2)
                sel = j % 3
                for dt in range(ND):
                    for half in range(2):
                        out = (pW[sel][:, half * 512:(half + 1) * 512]
                               if sel < 2 else pSb[half][:, :])
                        ins = eng.matmul(out,
                                         lnT[:, dt, tt * 128:(tt + 1) * 128],
                                         weT[:, e % 2, dt, half * 512:(half + 1) * 512],
                                         start=(dt == 0), stop=(dt == ND - 1))
                ins.then_inc(s_amm.h, 1)
    P.t(f_apply_pe)

    def f_apply_ev(eng, m0=amm0, g2=v_gate2):
        eng.wait_ge(s_gate2.h, g2)
        for e in range(8):
            for tt in range(NT):
                j = e * NT + tt
                eng.wait_ge(s_amm.h, m0 + j + 1)
                sel = j % 3
                if sel < 2:
                    ins = eng.scalar_tensor_tensor(
                        xres[:, tt, :], pW[sel][:, :],
                        bufs["mo_gates"][:, tt, e:e + 1],
                        xres[:, tt, :], ALU.mult, ALU.add)
                else:
                    eng.scalar_tensor_tensor(
                        xres[:, tt, 0:512], pSb[0][:, :],
                        bufs["mo_gates"][:, tt, e:e + 1],
                        xres[:, tt, 0:512], ALU.mult, ALU.add)
                    ins = eng.scalar_tensor_tensor(
                        xres[:, tt, 512:1024], pSb[1][:, :],
                        bufs["mo_gates"][:, tt, e:e + 1],
                        xres[:, tt, 512:1024], ALU.mult, ALU.add)
                ins.then_inc(s_aev.h, 1)
    P.v(f_apply_ev)
    return s_aev



def build_l1ffn_program():
    nc = bass.Bass("TRN2", target_bir_lowering=False)
    x_h = nc.dram_tensor("x", [T, D], F32, kind="ExternalInput").ap()
    out_h = nc.dram_tensor("out", [T, D], F32, kind="ExternalOutput").ap()
    haps = {"x": x_h}
    for name, (shape, dt) in CONST_SPECS.items():
        haps[name] = nc.dram_tensor(name, shape, dt, kind="ExternalInput").ap()
    waps = {}
    for nm, shape in [("U1", [1024, 256]), ("V1", [256, 4096]), ("b1", [4096]),
                      ("U2", [4096, 256]), ("V2", [256, 1024]), ("b2", [1024])]:
        waps[nm] = nc.dram_tensor(nm, shape, F32, kind="ExternalInput").ap()

    with contextlib.ExitStack() as stack:
        P = Prog(nc, stack)
        bufs = declare_common(P)
        declare_l1ffn(P, bufs)
        assert P.sbuf_off < 207 * 1024, P.sbuf_off
        emit_load_consts(P, bufs, haps)
        ln_dep = emit_ln(P, bufs, bufs["ld_const"])
        res = emit_l1ffn(P, bufs, waps, ln_dep)
        emit_store_x(P, bufs, out_h, res)
        P.finish()
    return nc


def build_l0_program():
    nc = bass.Bass("TRN2", target_bir_lowering=False)
    x_h = nc.dram_tensor("x", [T, D], F32, kind="ExternalInput").ap()
    out_h = nc.dram_tensor("out", [T, D], F32, kind="ExternalOutput").ap()
    haps = {"x": x_h}
    for name, (shape, dt) in CONST_SPECS.items():
        haps[name] = nc.dram_tensor(name, shape, dt, kind="ExternalInput").ap()
    waps = {
        "in_proj_w": nc.dram_tensor("in_proj_w", [3088, 1024], F32, kind="ExternalInput").ap(),
        "conv_w": nc.dram_tensor("conv_w", [3072, 1, 4], F32, kind="ExternalInput").ap(),
        "conv_b": nc.dram_tensor("conv_b", [3072], F32, kind="ExternalInput").ap(),
        "w_base": nc.dram_tensor("w_base", [16], F32, kind="ExternalInput").ap(),
        "out_proj_w": nc.dram_tensor("out_proj_w", [1024, 1024], F32, kind="ExternalInput").ap(),
    }
    ipT = nc.dram_tensor("ipT", [128, 8, 3088], BF16, kind="Internal").ap()
    opT = nc.dram_tensor("opT", [128, 8, 1024], BF16, kind="Internal").ap()

    with contextlib.ExitStack() as stack:
        P = Prog(nc, stack)
        bufs = declare_common(P)
        declare_l0(P, bufs)
        assert P.sbuf_off < Prog.ARENA_BYTES, P.sbuf_off
        emit_load_consts(P, bufs, haps)
        prep, cwl, l2d, evd = emit_l0_weightprep(P, bufs, waps, ipT, opT)
        ln_dep = emit_ln(P, bufs, bufs["ld_const"], psum_dep=(evd.h, evd.n))
        res = emit_l0attn(P, bufs, waps, ln_dep, bufs["ld_const"], ipT, opT, prep,
                          cwl, l2d)
        emit_store_x(P, bufs, out_h, res)
        P.finish()
    return nc


def build_moe_program():
    nc = bass.Bass("TRN2", target_bir_lowering=False)
    x_h = nc.dram_tensor("x", [T, D], F32, kind="ExternalInput").ap()
    out_h = nc.dram_tensor("out", [T, D], F32, kind="ExternalOutput").ap()
    haps = {"x": x_h}
    for name, (shape, dt) in CONST_SPECS.items():
        haps[name] = nc.dram_tensor(name, shape, dt, kind="ExternalInput").ap()
    waps = {
        "w_gate": nc.dram_tensor("w_gate", [8, 1024], F32, kind="ExternalInput").ap(),
        "w_in": nc.dram_tensor("w_in", [8, 4096, 1024], F32, kind="ExternalInput").ap(),
        "w_out": nc.dram_tensor("w_out", [8, 1024, 4096], F32, kind="ExternalInput").ap(),
    }
    weT_dram = nc.dram_tensor("weT", [8, 128, 8, 1024], BF16, kind="Internal").ap()
    wo_bf = nc.dram_tensor("wo_bf", [8, 1024, 4096], BF16, kind="Internal").ap()
    spill = nc.dram_tensor("spill", [128, NT, 1024], F32, kind="Internal").ap()

    with contextlib.ExitStack() as stack:
        P = Prog(nc, stack)
        bufs = declare_common(P)
        declare_moe(P, bufs)
        assert P.sbuf_off < Prog.ARENA_BYTES, P.sbuf_off
        emit_load_consts(P, bufs, haps)
        pp_wob = emit_moe_prelude(P, waps, wo_bf)
        res = emit_moe(P, bufs, waps, bufs["ld_const"], weT_dram, wo_bf, spill, pp_wob)
        emit_store_x(P, bufs, out_h, res)
        P.finish()
    return nc


def build_l1attn_program(dbg=False):
    nc = bass.Bass("TRN2", target_bir_lowering=False)
    x_h = nc.dram_tensor("x", [T, D], F32, kind="ExternalInput").ap()
    out_h = nc.dram_tensor("out", [T, D], F32, kind="ExternalOutput").ap()
    haps = {"x": x_h}
    for name, (shape, dt) in CONST_SPECS.items():
        haps[name] = nc.dram_tensor(name, shape, dt, kind="ExternalInput").ap()
    waps = {}
    for nm, shape in [("Wqkv", [1024, 3072]), ("bqkv", [3072]),
                      ("Wout", [1024, 1024]), ("bout", [1024])]:
        waps[nm] = nc.dram_tensor(nm, shape, F32, kind="ExternalInput").ap()
    spill = nc.dram_tensor("spill", [128, NT, 1024], F32, kind="Internal").ap()

    with contextlib.ExitStack() as stack:
        P = Prog(nc, stack)
        bufs = declare_common(P)
        declare_l1attn(P, bufs)
        assert P.sbuf_off < Prog.ARENA_BYTES, P.sbuf_off
        emit_load_consts(P, bufs, haps)
        ln_dep = emit_ln(P, bufs, bufs["ld_const"])
        res = emit_l1attn(P, bufs, waps, ln_dep, bufs["ld_const"], spill,
                          dbg_stop=dbg)
        if dbg:
            dsem = P.sem()
            names = [("at_QT", [128, 8, 2048]), ("at_KT", [128, 8, 2048]),
                     ("at_Vtm", [128, 16, 16, 65]), ("at_OT", [128, 8, 2048])]
            daps = {n: nc.dram_tensor("dbg_" + n, s, BF16, kind="ExternalOutput").ap()
                    for n, s in names}
            tgt = dsem.inc(16 * len(names))

            def f_dump(eng, w=res.n):
                eng.wait_ge(res.h, w)
                for n, s in names:
                    eng.dma_start(out=daps[n], in_=bufs[n][:]).then_inc(dsem.h, 16)
                eng.wait_ge(dsem.h, tgt)
            P.g(f_dump)
            res = dsem
        emit_store_x(P, bufs, out_h, res)
        P.finish()
    return nc


def build_fused_program(upto=4):
    nc = bass.Bass("TRN2", target_bir_lowering=False)
    x_h = nc.dram_tensor("x", [T, D], F32, kind="ExternalInput").ap()
    out_h = nc.dram_tensor("out", [T, D], F32, kind="ExternalOutput").ap()
    haps = {"x": x_h}
    for name, (shape, dt) in CONST_SPECS.items():
        haps[name] = nc.dram_tensor(name, shape, dt, kind="ExternalInput").ap()
    wl0 = {nm: nc.dram_tensor(nm, sh, F32, kind="ExternalInput").ap()
           for nm, sh in [("in_proj_w", [3088, 1024]), ("conv_w", [3072, 1, 4]),
                          ("conv_b", [3072]), ("w_base", [16]),
                          ("out_proj_w", [1024, 1024])]}
    wmo = {nm: nc.dram_tensor(nm, sh, F32, kind="ExternalInput").ap()
           for nm, sh in [("w_gate", [8, 1024]), ("w_in", [8, 4096, 1024]),
                          ("w_out", [8, 1024, 4096])]}
    wat = {nm: nc.dram_tensor(nm, sh, F32, kind="ExternalInput").ap()
           for nm, sh in [("Wqkv", [1024, 3072]), ("bqkv", [3072]),
                          ("Wout", [1024, 1024]), ("bout", [1024])]}
    wff = {nm: nc.dram_tensor(nm, sh, F32, kind="ExternalInput").ap()
           for nm, sh in [("U1", [1024, 256]), ("V1", [256, 4096]), ("b1", [4096]),
                          ("U2", [4096, 256]), ("V2", [256, 1024]), ("b2", [1024])]}
    ipT = nc.dram_tensor("ipT", [128, 8, 3088], BF16, kind="Internal").ap()
    opT = nc.dram_tensor("opT", [128, 8, 1024], BF16, kind="Internal").ap()
    weT_dram = nc.dram_tensor("weT", [8, 128, 8, 1024], BF16, kind="Internal").ap()
    wo_bf = nc.dram_tensor("wo_bf", [8, 1024, 4096], BF16, kind="Internal").ap()
    spill = nc.dram_tensor("spill", [128, NT, 1024], F32, kind="Internal").ap()

    with contextlib.ExitStack() as stack:
        P = Prog(nc, stack)
        bufs = declare_common(P)
        phase_base = P.mark()
        declare_l0(P, bufs)
        assert P.sbuf_off <= Prog.ARENA_BYTES
        P.reset_to(phase_base)
        declare_moe(P, bufs)
        assert P.sbuf_off <= Prog.ARENA_BYTES
        P.reset_to(phase_base)
        declare_l1attn(P, bufs)
        assert P.sbuf_off <= Prog.ARENA_BYTES
        P.reset_to(phase_base)
        declare_l1ffn(P, bufs)
        assert P.sbuf_off <= Prog.ARENA_BYTES

        emit_load_consts(P, bufs, haps)
        prep, cwl, l2d, evd = emit_l0_weightprep(P, bufs, wl0, ipT, opT)
        ln0 = emit_ln(P, bufs, bufs["ld_const"], psum_dep=(evd.h, evd.n))
        r = emit_l0attn(P, bufs, wl0, ln0, bufs["ld_const"], ipT, opT, prep,
                        cwl, l2d)
        pp_wob = emit_moe_prelude(P, wmo, wo_bf, pace=P.l0_pace) if upto >= 2 else None
        if upto >= 2:
            r = emit_moe(P, bufs, wmo, r, weT_dram, wo_bf, spill, pp_wob)
        if upto >= 3:
            ln2 = emit_ln(P, bufs, r, psum_dep=(r.h, r.n), tile_dep=True)
            r = emit_l1attn(P, bufs, wat, ln2, r, spill)
        if upto >= 4:
            ln3 = emit_ln(P, bufs, r, psum_dep=(r.h, r.n), tile_dep=True)
            r = emit_l1ffn(P, bufs, wff, ln3, ld_dep=(r.h, r.n))
        emit_store_x(P, bufs, out_h, r)
        P.finish()
    return nc


_PROG_CACHE = {}


def kernel(**inputs):
    consts = host_constants()
    if "fused" not in _PROG_CACHE:
        _PROG_CACHE["fused"] = build_fused_program()
    nc = _PROG_CACHE["fused"]
    x = np.ascontiguousarray(inputs["L0_attn_x"], dtype=np.float32)
    wmap = {
        "in_proj_w": inputs["L0_attn_in_proj_w"],
        "conv_w": inputs["L0_attn_conv_w"],
        "conv_b": inputs["L0_attn_conv_b"],
        "w_base": inputs["L0_attn_w_base"],
        "out_proj_w": inputs["L0_attn_out_proj_w"],
        "w_gate": inputs["L0_ffn_w_gate"],
        "w_in": inputs["L0_ffn_w_in"],
        "w_out": inputs["L0_ffn_w_out"],
        "Wqkv": inputs["L1_attn_Wqkv"],
        "bqkv": inputs["L1_attn_bqkv"],
        "Wout": inputs["L1_attn_Wout"],
        "bout": inputs["L1_attn_bout"],
        "U1": inputs["L1_ffn_U1"],
        "V1": inputs["L1_ffn_V1"],
        "b1": inputs["L1_ffn_b1"],
        "U2": inputs["L1_ffn_U2"],
        "V2": inputs["L1_ffn_V2"],
        "b2": inputs["L1_ffn_b2"],
    }
    wmap = {k: np.ascontiguousarray(v, dtype=np.float32) for k, v in wmap.items()}
    in_maps = []
    for c in range(NCORES):
        m = {"x": np.ascontiguousarray(x[c])}
        m.update(consts)
        m.update(wmap)
        in_maps.append(m)
    res = run_bass_kernel_spmd(nc, in_maps, core_ids=list(range(NCORES)))
    return np.stack([r["out"] for r in res.results], axis=0)


# version 37 (final)
# speedup vs baseline: 1.1858x; 1.0160x over previous
import math
import contextlib
import numpy as np
import ml_dtypes

import concourse.bass as bass
import concourse.mybir as mybir
from concourse.bass_utils import run_bass_kernel_spmd

F32 = mybir.dt.float32
BF16 = mybir.dt.bfloat16
AF = mybir.ActivationFunctionType
ALU = mybir.AluOpType
AX = mybir.AxisListType
BF = ml_dtypes.bfloat16

NCORES = 8
L0_STAGE = 6
T = 2048
D = 1024
NT = T // 128
ND = D // 128

ENGINES = ("sync", "tensor", "vector", "scalar", "gpsimd")


class Sem:
    def __init__(self, handle):
        self.h = handle
        self.n = 0

    def inc(self, k=1):
        self.n += k
        return self.n


class PPSem:

    def __init__(self, P, n=2):
        self.sems = [P.sem(dma=True) for _ in range(n)]
        self.n = n

    def tgt(self, u):
        return self.sems[u % self.n].h, 16 * (u // self.n + 1)

    def inc(self, u):
        self.sems[u % self.n].inc(16)
        return self.sems[u % self.n].h


class Prog:

    ARENA_BYTES = 204 * 1024

    def __init__(self, nc, stack):
        self.nc = nc
        self.stack = stack
        self.ops = {e: [] for e in ENGINES}
        self._semi = 0
        arena = nc.alloc_sbuf_tensor("arena", [128, self.ARENA_BYTES], mybir.dt.uint8)
        self.arena_base = nc.lookup_mloc(arena).addr
        self.sbuf_off = 0

    SEM_POOL = 64

    def sem(self, dma=False):
        if dma:
            self._semi += 1
            h = self.stack.enter_context(self.nc.semaphore(f"d{self._semi}"))
            return Sem(h)
        self._sems = getattr(self, "_sems", [])
        if len(self._sems) < self.SEM_POOL:
            self._semi += 1
            h = self.stack.enter_context(self.nc.semaphore(f"s{self._semi}"))
            s = Sem(h)
            self._sems.append(s)
            return s
        self._semi += 1
        return self._sems[self._semi % self.SEM_POOL]

    def emit(self, eng, fn):
        self.ops[eng].append(fn)

    def t(self, fn): self.emit("tensor", fn)
    def v(self, fn): self.emit("vector", fn)
    def s(self, fn): self.emit("scalar", fn)
    def g(self, fn): self.emit("gpsimd", fn)
    def y(self, fn): self.emit("sync", fn)

    def sb(self, name, shape, dtype, align=64):
        off = (self.sbuf_off + align - 1) // align * align
        per_part = int(np.prod(shape[1:])) * (2 if dtype == BF16 else 4)
        assert off + per_part <= self.ARENA_BYTES, (name, off, per_part)
        h = self.nc.alloc_sbuf_tensor_at(name, shape, dtype,
                                         offset=self.arena_base + off)
        self.offsets = getattr(self, "offsets", {})
        self.offsets[name] = off
        self.sbuf_off = off + per_part
        return h

    def sb_at(self, name, shape, dtype, off):
        per_part = int(np.prod(shape[1:])) * (2 if dtype == BF16 else 4)
        assert off + per_part <= self.ARENA_BYTES, (name, off, per_part)
        return self.nc.alloc_sbuf_tensor_at(name, shape, dtype,
                                            offset=self.arena_base + off)

    def mark(self):
        return self.sbuf_off

    def reset_to(self, off):
        self.sbuf_off = off

    def finish(self):
        with self.nc.Block() as blk:
            for e in ENGINES:
                def body(eng, e=e):
                    for fn in self.ops[e]:
                        fn(eng)
                getattr(blk, e)(body)



def alibi_slopes(n_heads=16):
    def pow2(n):
        start = 2 ** (-(2 ** -(math.log2(n) - 3)))
        return [start ** i for i in range(n)]
    assert math.log2(n_heads).is_integer()
    return np.asarray(pow2(n_heads), dtype=np.float32)


def host_constants():
    c = {}
    c["ident"] = np.eye(128, dtype=BF)
    c["identf"] = np.eye(128, dtype=np.float32)
    c["ones_col"] = np.ones((1, 128), dtype=BF)
    c["ones_colf"] = np.ones((1, 128), dtype=np.float32)
    sl = np.arange(64)
    tm = (sl[:, None] <= sl[None, :]).astype(BF)
    c["trimask"] = np.vstack([tm, tm])
    r = np.arange(128)[:, None]
    u = np.arange(4352)[None, :]
    c["toep"] = (-np.abs((u - 1920) - r)).astype(BF)
    sl16 = alibi_slopes(16)
    dg = np.zeros((128, 16, 128), dtype=np.float32)
    for h in range(16):
        dg[np.arange(128), h, np.arange(128)] = sl16[h]
    c["diag_slope"] = dg.astype(BF)
    return c


CONST_SPECS = {
    "ident": ([128, 128], BF16),
    "identf": ([128, 128], F32),
    "ones_col": ([1, 128], BF16),
    "ones_colf": ([1, 128], F32),
    "trimask": ([128, 64], BF16),
    "toep": ([128, 4352], BF16),
    "diag_slope": ([128, 16, 128], BF16),
}



def declare_common(P):
    bufs = {}
    for name, (shape, dt) in CONST_SPECS.items():
        bufs[name] = P.sb(name, shape, dt)
    bufs["xres"] = P.sb("xres", [128, NT, 1024], F32)
    bufs["lnT"] = P.sb("lnT", [128, ND, T], BF16)
    bufs["ln_tm"] = P.sb("ln_tm", [128, 2, 1024], BF16)
    bufs["scratch_sq"] = P.sb_at("scratch_sq", [128, 1024], F32, P.offsets["ln_tm"])
    bufs["stats"] = P.sb("stats", [128, 128], F32)
    bufs["bias_raw"] = P.sb("bias_raw", [32, 128], F32)
    bufs["psumT0"] = P.nc.alloc_psum_tensor("psumT0", [128, 1024], BF16)
    bufs["psumT1"] = P.nc.alloc_psum_tensor("psumT1", [128, 1024], BF16)
    bufs["psumW0"] = P.nc.alloc_psum_tensor("psumW0", [128, 1024], F32)
    bufs["psumW1"] = P.nc.alloc_psum_tensor("psumW1", [128, 1024], F32)
    bufs["psumS0"] = P.nc.alloc_psum_tensor("psumS0", [128, 512], F32)
    bufs["psumS1"] = P.nc.alloc_psum_tensor("psumS1", [128, 512], F32)
    bufs["ld_const"] = P.sem(dma=True)
    return bufs


def emit_load_consts(P, bufs, haps):
    sem = bufs["ld_const"]
    n = len(CONST_SPECS) + 1
    tgt = sem.inc(16 * n)

    def f(eng):
        for name in CONST_SPECS:
            eng.dma_start(out=bufs[name][:], in_=haps[name]).then_inc(sem.h, 16)
        eng.dma_start(out=bufs["xres"][:],
                      in_=haps["x"].rearrange("(a p) d -> p a d", p=128)).then_inc(sem.h, 16)
    P.y(f)


def emit_store_x(P, bufs, out_ap, x_dep):
    sem = P.sem(dma=True)
    tgt = sem.inc(16)

    def f(eng, dep=x_dep.n, tgt=tgt):
        eng.wait_ge(x_dep.h, dep)
        eng.dma_start(out=out_ap.rearrange("(a p) d -> p a d", p=128),
                      in_=bufs["xres"][:]).then_inc(sem.h, 16)
        eng.wait_ge(sem.h, tgt)
    P.y(f)
    return sem


def emit_colbias(P, bufs, src_ap, dst, n_tiles, ld_sem, ld_tgt, psum_dep=None):
    raw = bufs["bias_raw"]
    s_pe = P.sem()
    s_dve = P.sem()
    ldl = P.sem(dma=True)
    vl = ldl.inc(16)

    def f_ld(eng):
        eng.dma_start(out=raw[0:n_tiles, :], in_=src_ap.rearrange("(a p) -> a p", p=128)).then_inc(ldl.h, 16)
    P.g(f_ld)

    vpe = s_pe.inc()

    def f_pe(eng, wl=vl, cdep=bufs["ld_const"].n, pdep=psum_dep):
        eng.wait_ge(ldl.h, wl)
        eng.wait_ge(bufs["ld_const"].h, cdep)
        if pdep is not None:
            eng.wait_ge(pdep[0], pdep[1])
        eng.transpose(bufs["psumW1"][0:128, 0:n_tiles], raw[0:n_tiles, :],
                      bufs["identf"][0:n_tiles, 0:n_tiles]).then_inc(s_pe.h, 1)
    P.t(f_pe)

    vd = s_dve.inc()

    def f_ev(eng, wp=vpe):
        eng.wait_ge(s_pe.h, wp)
        eng.tensor_copy(dst[:, 0:n_tiles], bufs["psumW1"][0:128, 0:n_tiles]).then_inc(s_dve.h, 1)
    P.v(f_ev)
    return s_dve


def emit_ln_AB(P, bufs, x_dep, tile_dep=False):
    xres = bufs["xres"]
    sq = bufs["scratch_sq"]
    st = bufs["stats"]
    s1 = st[:, 0:NT]
    s2 = st[:, NT:2 * NT]
    mean = st[:, 2 * NT:3 * NT]
    var = st[:, 3 * NT:4 * NT]
    rstd = st[:, 4 * NT:5 * NT]
    nbias = st[:, 5 * NT:6 * NT]

    s_act = P.sem()
    s_dve = P.sem()
    s_pe = P.sem()

    a_base = s_act.n
    vA_act = s_act.inc(NT)

    def fa_act(eng, dep=x_dep.n, base=a_base):
        if not tile_dep:
            eng.wait_ge(x_dep.h, dep)
        for i in range(NT):
            if tile_dep:
                eng.wait_ge(x_dep.h, dep - NT + i + 1)
            if i >= 1:
                eng.wait_ge(s_act.h, base + i)
            ins = eng.activation(sq[:, :], xres[:, i, :], AF.Square,
                                 accum_out=s2[:, i:i + 1])
            ins.then_inc(s_act.h, 1)
    P.s(fa_act)

    vA_dve = s_dve.inc()

    def fa_dve(eng, dep=x_dep.n, tgt=vA_dve):
        if not tile_dep:
            eng.wait_ge(x_dep.h, dep)
        for i in range(NT):
            if tile_dep:
                eng.wait_ge(x_dep.h, dep - NT + i + 1)
            ins = eng.reduce_sum(s1[:, i:i + 1], xres[:, i, :], AX.X)
        ins.then_inc(s_dve.h, 1)
    P.v(fa_dve)
    a_done = [(s_act.h, vA_act), (s_dve.h, vA_dve)]

    vB1 = s_dve.inc()

    def fb_dve(eng, wa=vA_act, wd=vA_dve):
        eng.wait_ge(s_act.h, wa)
        eng.wait_ge(s_dve.h, wd)
        eng.tensor_scalar_mul(mean[:, :], s1, 1.0 / D)
        eng.drain()
        eng.tensor_tensor(var[:, :], mean, mean, ALU.mult)
        eng.drain()
        eng.scalar_tensor_tensor(var[:, :], s2, 1.0 / D, var, ALU.mult, ALU.subtract)
        eng.drain()
        eng.tensor_scalar_add(var[:, :], var, 1e-5)
        eng.drain().then_inc(s_dve.h, 1)
    P.v(fb_dve)

    vB2 = s_act.inc()

    def fb_act(eng, wd=vB1):
        eng.wait_ge(s_dve.h, wd)
        eng.activation(rstd[:, :], var, AF.Sqrt)
        eng.drain().then_inc(s_act.h, 1)
    P.s(fb_act)

    vB3 = s_dve.inc()

    def fb2_dve(eng, wa=vB2):
        eng.wait_ge(s_act.h, wa)
        eng.reciprocal(rstd[:, :], rstd)
        eng.drain()
        eng.tensor_tensor(nbias[:, :], mean, rstd, ALU.mult)
        eng.drain()
        eng.tensor_scalar_mul(nbias[:, :], nbias, -1.0)
        eng.drain().then_inc(s_dve.h, 1)
    P.v(fb2_dve)
    return dict(s_act=s_act, s_dve=s_dve, s_pe=s_pe, vB3=vB3,
                a_done=a_done)


def emit_ln_CD(P, bufs, ctx, psum_dep=None, cx_dep=None):
    xres = bufs["xres"]
    lnT = bufs["lnT"]
    ln_tm = bufs["ln_tm"]
    st = bufs["stats"]
    rstd = st[:, 4 * NT:5 * NT]
    nbias = st[:, 5 * NT:6 * NT]
    s_act, s_dve, s_pe = ctx["s_act"], ctx["s_dve"], ctx["s_pe"]
    vB3 = ctx["vB3"]

    actC = s_act.n
    s_act.inc(NT)
    peD = s_pe.n
    s_pe.inc(NT)
    dveD = s_dve.n
    s_dve.inc(NT)

    def fc_act(eng, wd=vB3, base=actC, peD=peD,
               wcx=(None if cx_dep is None else (cx_dep.h, cx_dep.n))):
        eng.wait_ge(s_dve.h, wd)
        if wcx is not None:
            eng.wait_ge(wcx[0], wcx[1])
        for i in range(NT):
            if i >= 2:
                eng.wait_ge(s_pe.h, peD + i - 1)
            ins = eng.activation(ln_tm[:, i % 2, :], xres[:, i, :], AF.Identity,
                                 bias=nbias[:, i:i + 1], scale=rstd[:, i:i + 1])
            ins.then_inc(s_act.h, 1)
    P.s(fc_act)

    def fd_pe(eng, base=actC, dveD=dveD, cdep=bufs["ld_const"].n, pdep=psum_dep):
        eng.wait_ge(bufs["ld_const"].h, cdep)
        if pdep is not None:
            eng.wait_ge(pdep[0], pdep[1])
        for i in range(NT):
            eng.wait_ge(s_act.h, base + i + 1)
            if i >= 2:
                eng.wait_ge(s_dve.h, dveD + i - 1)
            ps = bufs["psumT0"] if i % 2 == 0 else bufs["psumT1"]
            for k in range(ND):
                ins = eng.transpose(ps[:, k * 128:(k + 1) * 128],
                                    ln_tm[:, i % 2, k * 128:(k + 1) * 128],
                                    bufs["ident"][:])
            ins.then_inc(s_pe.h, 1)
    P.t(fd_pe)

    def fd_dve(eng, base=peD):
        for i in range(NT):
            eng.wait_ge(s_pe.h, base + i + 1)
            ps = bufs["psumT0"] if i % 2 == 0 else bufs["psumT1"]
            ins = eng.tensor_copy(
                lnT[:, :, i * 128:(i + 1) * 128],
                ps[:].rearrange("p (a b) -> p a b", a=ND))
            ins.then_inc(s_dve.h, 1)
    P.v(fd_dve)
    s_dve.a_done = ctx["a_done"]
    return s_dve


def emit_ln(P, bufs, x_dep, psum_dep=None, cx_dep=None, tile_dep=False):
    ctx = emit_ln_AB(P, bufs, x_dep, tile_dep=tile_dep)
    return emit_ln_CD(P, bufs, ctx, psum_dep=psum_dep, cx_dep=cx_dep)



def declare_l1ffn(P, bufs):
    bufs["ffn_U1"] = P.sb("ffn_U1", [128, 8, 256], BF16)
    bufs["ffn_V1"] = P.sb("ffn_V1", [128, 2, 4096], BF16)
    bufs["ffn_U2"] = P.sb("ffn_U2", [128, 32, 256], BF16)
    bufs["ffn_V2"] = P.sb("ffn_V2", [128, 2, 1024], BF16)
    bufs["ffn_b1"] = P.sb("ffn_b1", [128, 32], F32)
    bufs["ffn_b2"] = P.sb("ffn_b2", [1, 1024], F32)
    bufs["ffn_b2h"] = P.sb("ffn_b2h", [1, 1024], BF16)
    bufs["ffn_b2l"] = P.sb("ffn_b2l", [1, 1024], BF16)
    bufs["ffn_PT"] = P.sb("ffn_PT", [128, 2, 2048], BF16)
    bufs["ffn_QT"] = P.sb("ffn_QT", [128, 2, 2048], BF16)
    bufs["ffn_Hs"] = P.sb("ffn_Hs", [128, 2, 512], BF16)


def emit_l1ffn(P, bufs, waps, ln_dep, ld_dep=None):
    lnT = bufs["lnT"]
    xres = bufs["xres"]
    U1, V1, U2, V2 = bufs["ffn_U1"], bufs["ffn_V1"], bufs["ffn_U2"], bufs["ffn_V2"]
    b1, b2, b2h, b2l = bufs["ffn_b1"], bufs["ffn_b2"], bufs["ffn_b2h"], bufs["ffn_b2l"]
    PT, QT, Hs = bufs["ffn_PT"], bufs["ffn_QT"], bufs["ffn_Hs"]
    pS = [bufs["psumS0"], bufs["psumS1"]]
    pW = [bufs["psumW0"], bufs["psumW1"]]

    ld = P.sem(dma=True)
    vld = ld.inc(16 * 5)

    def f_ld(eng, wd=ld_dep):
        if wd is not None:
            eng.wait_ge(wd[0], wd[1])
        eng.dma_start(out=U1[:], in_=waps["U1"].rearrange("(a p) r -> p a r", p=128)).then_inc(ld.h, 16)
        eng.dma_start(out=V1[:], in_=waps["V1"].rearrange("(a p) f -> p a f", p=128)).then_inc(ld.h, 16)
        eng.dma_start(out=U2[:], in_=waps["U2"].rearrange("(a p) r -> p a r", p=128)).then_inc(ld.h, 16)
        eng.dma_start(out=V2[:], in_=waps["V2"].rearrange("(a p) d -> p a d", p=128)).then_inc(ld.h, 16)
        eng.dma_start(out=b2[:], in_=waps["b2"].unsqueeze(0)).then_inc(ld.h, 16)
    P.g(f_ld)

    b1_dep = emit_colbias(P, bufs, waps["b1"], b1, 32, ld, vld)

    s_misc = P.sem()
    v_b2 = s_misc.inc()

    def f_b2(eng, wld=vld):
        eng.wait_ge(ld.h, wld)
        eng.tensor_copy(b2h[:], b2[:])
        eng.drain()
        eng.tensor_tensor(b2l[:], b2[:], b2h[:], ALU.subtract)
        eng.drain().then_inc(s_misc.h, 1)
    P.v(f_b2)

    s_pe = P.sem()
    s_act = P.sem()

    u1_pe = s_pe.n
    s_pe.inc(8)
    u1_act = s_act.n
    s_act.inc(8)

    def f_u1(eng, wld=vld, wln=ln_dep.n, base=u1_act, wb1=(b1_dep.h, b1_dep.n)):
        eng.wait_ge(ld.h, wld)
        eng.wait_ge(ln_dep.h, wln)
        eng.wait_ge(wb1[0], wb1[1])
        for j in range(8):
            rc, tc = divmod(j, 4)
            if j >= 2:
                eng.wait_ge(s_act.h, base + j - 1)
            ps = pS[j % 2]
            for k in range(ND):
                ins = eng.matmul(ps[:, :], U1[:, k, rc * 128:(rc + 1) * 128],
                                 lnT[:, k, tc * 512:(tc + 1) * 512],
                                 start=(k == 0), stop=(k == ND - 1))
            ins.then_inc(s_pe.h, 1)
    P.t(f_u1)

    def f_u1_evac(eng, base=u1_pe):
        for j in range(8):
            rc, tc = divmod(j, 4)
            eng.wait_ge(s_pe.h, base + j + 1)
            ins = eng.activation(PT[:, rc, tc * 512:(tc + 1) * 512], pS[j % 2][:, :],
                                 AF.Identity)
            ins.then_inc(s_act.h, 1)
    P.s(f_u1_evac)

    s_v1 = P.sem()
    s_u2 = P.sem()
    s_gel = P.sem()
    s_qev = P.sem()
    v1_0 = s_v1.n
    s_v1.inc(128)
    u2_0 = s_u2.n
    s_u2.inc(128)
    gel_0 = s_gel.n
    s_gel.inc(128)
    qev_0 = s_qev.n
    s_qev.inc(4)

    def f_v1u2(eng, u1a=u1_act, gel=gel_0, qev=qev_0):
        eng.wait_ge(s_act.h, u1a + 8)
        for j in range(129):
            if j < 128:
                tc, fs = divmod(j, 32)
                if j >= 2:
                    eng.wait_ge(s_gel.h, gel + j - 1)
                ps = pS[j % 2]
                for rt in range(2):
                    ins = eng.matmul(ps[:, :], V1[:, rt, fs * 128:(fs + 1) * 128],
                                     PT[:, rt, tc * 512:(tc + 1) * 512],
                                     start=(rt == 0), stop=(rt == 1))
                ins.then_inc(s_v1.h, 1)
            if j >= 1:
                jj = j - 1
                tc, fs = divmod(jj, 32)
                eng.wait_ge(s_gel.h, gel + jj + 1)
                if fs == 0 and tc >= 1:
                    eng.wait_ge(s_qev.h, qev + tc)
                ins = eng.matmul(pW[0][:, 0:512], U2[:, fs, 0:128], Hs[:, jj % 2, :],
                                 start=(fs == 0), stop=(fs == 31))
                ins2 = eng.matmul(pW[0][:, 512:1024], U2[:, fs, 128:256], Hs[:, jj % 2, :],
                                  start=(fs == 0), stop=(fs == 31))
                ins2.then_inc(s_u2.h, 1)
    P.t(f_v1u2)

    def f_gelu(eng, v1b=v1_0, u2b=u2_0):
        for j in range(128):
            tc, fs = divmod(j, 32)
            eng.wait_ge(s_v1.h, v1b + j + 1)
            ins = eng.activation(Hs[:, j % 2, :], pS[j % 2][:, :], AF.Gelu_apprx_tanh,
                                 bias=b1[:, fs:fs + 1])
            ins.then_inc(s_gel.h, 1)
            if fs == 31:
                eng.wait_ge(s_u2.h, u2b + (tc + 1) * 32)
                eng.activation(QT[:, 0, tc * 512:(tc + 1) * 512], pW[0][:, 0:512],
                               AF.Identity)
                ins = eng.activation(QT[:, 1, tc * 512:(tc + 1) * 512], pW[0][:, 512:1024],
                                     AF.Identity)
                ins.then_inc(s_qev.h, 1)
    P.s(f_gelu)

    v2_0 = s_pe.n
    s_pe.inc(NT)
    s_res = P.sem()
    res_0 = s_res.n
    s_res.inc(NT)

    def f_v2(eng, qev=qev_0, wb2=v_b2, res0=res_0):
        eng.wait_ge(s_qev.h, qev + 4)
        eng.wait_ge(s_misc.h, wb2)
        for tt in range(NT):
            if tt >= 2:
                eng.wait_ge(s_res.h, res0 + tt - 1)
            ps = pW[tt % 2]
            for half in range(2):
                sl = slice(half * 512, (half + 1) * 512)
                for rt in range(2):
                    eng.matmul(ps[:, sl], QT[:, rt, tt * 128:(tt + 1) * 128],
                               V2[:, rt, sl], start=(rt == 0), stop=False)
                eng.matmul(ps[:, sl], bufs["ones_col"][:, 0:128], b2h[0:1, sl],
                           start=False, stop=False)
                ins = eng.matmul(ps[:, sl], bufs["ones_col"][:, 0:128], b2l[0:1, sl],
                                 start=False, stop=True)
            ins.then_inc(s_pe.h, 1)
    P.t(f_v2)

    def f_res(eng, base=v2_0):
        for tt in range(NT):
            eng.wait_ge(s_pe.h, base + tt + 1)
            ins = eng.tensor_tensor(xres[:, tt, :], pW[tt % 2][:, :], xres[:, tt, :],
                                    ALU.add)
            ins.then_inc(s_res.h, 1)
    P.v(f_res)
    return s_res



def declare_l1attn(P, bufs):
    off_x = P.offsets["xres"]
    bufs["at_QT"] = P.sb_at("at_QT", [128, 8, 2048], BF16, off_x)
    bufs["at_KT"] = P.sb_at("at_KT", [128, 8, 2048], BF16, off_x + 32 * 1024)
    bufs["at_OT"] = P.sb_at("at_OT", [128, 8, 2048], BF16, P.offsets["lnT"])
    bufs["at_Vtm"] = P.sb("at_Vtm", [128, 16, 16, 65], BF16)
    bufs["at_Wq"] = P.sb("at_Wq", [128, 2, 8, 512], BF16)
    bufs["at_Wout"] = P.sb("at_Wout", [128, 8, 512], BF16)
    bufs["at_bqk"] = P.sb("at_bqk", [128, 16], F32)
    bufs["at_bqkq"] = P.sb("at_bqkq", [128, 8], F32)
    bufs["at_bvr"] = P.sb("at_bvr", [1, 1024], F32)
    bufs["at_bvh"] = P.sb("at_bvh", [1, 1024], BF16)
    bufs["at_bvl"] = P.sb("at_bvl", [1, 1024], BF16)
    bufs["at_bo"] = P.sb("at_bo", [1, 1024], F32)
    bufs["at_boh"] = P.sb("at_boh", [1, 1024], BF16)
    bufs["at_bol"] = P.sb("at_bol", [1, 1024], BF16)
    bufs["at_eS"] = P.sb("at_eS", [128, 2, 512], BF16)
    bufs["at_Osc"] = P.sb("at_Osc", [64, 2, 512], BF16)
    bufs["at_inv"] = P.sb("at_inv", [1, 2, 512], F32)
    bufs["at_QZ"] = P.sb("at_QZ", [128, 2, 512], BF16)


def emit_l1attn(P, bufs, waps, ln_dep, x_dep, spill_ap, dbg_stop=False):
    lnT = bufs["lnT"]
    xres = bufs["xres"]
    QT, KT, OT, Vtm = bufs["at_QT"], bufs["at_KT"], bufs["at_OT"], bufs["at_Vtm"]
    Wq, Wout = bufs["at_Wq"], bufs["at_Wout"]
    eS, Osc, inv = bufs["at_eS"], bufs["at_Osc"], bufs["at_inv"]
    pS = [bufs["psumS0"], bufs["psumS1"]]
    pW0, pW1 = bufs["psumW0"], bufs["psumW1"]
    banks4 = [lambda: pS[0][:, :], lambda: pS[1][:, :],
              lambda: pW0[:, 0:512], lambda: pW0[:, 512:1024]]

    s_spill = P.sem(dma=True)
    v_spill = s_spill.inc(16)

    def f_spill(eng, wx=x_dep.n):
        eng.wait_ge(x_dep.h, wx)
        eng.dma_start(out=spill_ap, in_=xres[:]).then_inc(s_spill.h, 16)
    P.y(f_spill)

    ld = P.sem(dma=True)
    v_ldb = ld.inc(16 * 2)

    def f_ldb(eng, wx=(x_dep.h, x_dep.n)):
        eng.wait_ge(wx[0], wx[1])
        eng.dma_start(out=bufs["at_bvr"][:], in_=waps["bqkv"][2048:3072].unsqueeze(0)).then_inc(ld.h, 16)
        eng.dma_start(out=bufs["at_bo"][:], in_=waps["bout"].unsqueeze(0)).then_inc(ld.h, 16)
    P.g(f_ldb)

    bqk_dep = emit_colbias(P, bufs, waps["bqkv"][0:2048], bufs["at_bqk"], 16, ld,
                           v_ldb, psum_dep=(x_dep.h, x_dep.n))

    s_misc = P.sem()
    v_hl = s_misc.inc()

    def f_hilo(eng, wld=v_ldb, wbq=(bqk_dep.h, bqk_dep.n)):
        eng.wait_ge(ld.h, wld)
        eng.wait_ge(wbq[0], wbq[1])
        eng.tensor_scalar_mul(bufs["at_bqkq"][:, :], bufs["at_bqk"][:, 0:8], 0.125)
        eng.tensor_copy(bufs["at_bvh"][:], bufs["at_bvr"][:])
        eng.tensor_copy(bufs["at_boh"][:], bufs["at_bo"][:])
        eng.drain()
        eng.tensor_tensor(bufs["at_bvl"][:], bufs["at_bvr"][:], bufs["at_bvh"][:], ALU.subtract)
        eng.tensor_tensor(bufs["at_bol"][:], bufs["at_bo"][:], bufs["at_boh"][:], ALU.subtract)
        eng.memset(Vtm[:, :, :, 64:65], 1.0)
        eng.drain().then_inc(s_misc.h, 1)
    P.v(f_hilo)

    pp_wld = PPSem(P)
    s_qku = P.sem()
    qku0 = s_qku.n
    s_qku.inc(16)

    def f_wld(eng, wsp=v_spill, base=qku0):
        eng.wait_ge(s_spill.h, wsp)
        for sc in range(6):
            if sc >= 2:
                eng.wait_ge(s_qku.h, base + (sc - 2) * 4 + 4)
            eng.dma_start(out=Wq[:, sc % 2],
                          in_=waps["Wqkv"].rearrange("(a p) (c j) -> p c a j",
                                                     p=128, j=512)[:, sc]
                          ).then_inc(pp_wld.inc(sc), 16)
    P.g(f_wld)

    s_qkev = P.sem()
    qkev0 = s_qkev.n
    s_qkev.inc(64)
    s_vpe = P.sem()
    vpe0 = s_vpe.n
    s_vpe.inc(32)
    s_vev = P.sem()
    vev0 = s_vev.n
    s_vev.inc(32)

    def f_qkv_pe(eng, wln=ln_dep.n, ev=qkev0, vev=vev0, whl=v_hl,
                 wbq=(bqk_dep.h, bqk_dep.n), base=qku0):
        eng.wait_ge(ln_dep.h, wln)
        eng.wait_ge(s_misc.h, whl)
        eng.wait_ge(wbq[0], wbq[1])
        for jc in range(16):
            sc = jc // 4
            eng.wait_ge(*pp_wld.tgt(sc))
            for tc in range(4):
                if jc >= 1:
                    eng.wait_ge(s_qkev.h, ev + (jc - 1) * 4 + tc + 1)
                ps = banks4[tc]()
                for dt in range(ND):
                    ins = eng.matmul(ps, Wq[:, sc % 2, dt, (jc % 4) * 128:(jc % 4) * 128 + 128],
                                     lnT[:, dt, tc * 512:(tc + 1) * 512],
                                     start=(dt == 0), stop=(dt == ND - 1))
            ins.then_inc(s_qku.h, 1)
        for u in range(32):
            kt, half = divmod(u, 2)
            sc = 4 + half
            if u < 2:
                eng.wait_ge(*pp_wld.tgt(sc))
            if u < 4:
                eng.wait_ge(s_qkev.h, ev + 64)
            else:
                eng.wait_ge(s_vev.h, vev + u - 3)
            ps = banks4[u % 4]()
            for dt in range(ND):
                eng.matmul(ps, lnT[:, dt, kt * 128:(kt + 1) * 128],
                           Wq[:, sc % 2, dt, :], start=(dt == 0), stop=False)
            eng.matmul(ps, bufs["ones_col"][:, 0:128],
                       bufs["at_bvh"][0:1, half * 512:(half + 1) * 512],
                       start=False, stop=False)
            ins = eng.matmul(ps, bufs["ones_col"][:, 0:128],
                             bufs["at_bvl"][0:1, half * 512:(half + 1) * 512],
                             start=False, stop=True)
            ins.then_inc(s_vpe.h, 1)
    P.t(f_qkv_pe)

    def f_qk_evac(eng, base=qku0, whl=v_hl):
        eng.wait_ge(s_misc.h, whl)
        for jc in range(16):
            eng.wait_ge(s_qku.h, base + jc + 1)
            for tc in range(4):
                if jc < 8:
                    ins = eng.activation(QT[:, jc, tc * 512:(tc + 1) * 512],
                                         banks4[tc](), AF.Identity,
                                         bias=bufs["at_bqkq"][:, jc:jc + 1],
                                         scale=0.125)
                else:
                    ins = eng.activation(KT[:, jc - 8, tc * 512:(tc + 1) * 512],
                                         banks4[tc](), AF.Identity,
                                         bias=bufs["at_bqk"][:, jc:jc + 1])
                ins.then_inc(s_qkev.h, 1)
    P.s(f_qk_evac)

    def f_v_evac(eng, base=vpe0):
        for u in range(32):
            kt, half = divmod(u, 2)
            eng.wait_ge(s_vpe.h, base + u + 1)
            ps = banks4[u % 4]()
            ins = eng.tensor_copy(Vtm[:, kt, half * 8:(half + 1) * 8, 0:64],
                                  ps.rearrange("p (a b) -> p a b", a=8))
            ins.then_inc(s_vev.h, 1)
    P.v(f_v_evac)

    _slopes = alibi_slopes(16)
    KEEP = {}
    for _q in range(64):
        _h, _qc = divmod(_q, 4)
        KEEP[_q] = [kt for kt in range(16)
                    if _slopes[_h] * max(0, kt * 128 - (_qc * 512 + 511),
                                         _qc * 512 - (kt * 128 + 127)) < 200.0]
    GIDX = {}
    PVB = [0] * 65
    CONS = {}
    _g = 0
    for _q in range(64):
        PVB[_q + 1] = PVB[_q] + len(KEEP[_q])
        for _i, _kt in enumerate(KEEP[_q]):
            GIDX[(_q, _kt)] = _g
            CONS[_g] = PVB[_q] + _i + 1
            _g += 1
    NKT = _g

    s_sc = P.sem()
    sc0 = s_sc.n
    s_sc.inc(NKT)
    s_ex = P.sem()
    ex0 = s_ex.n
    s_ex.inc(NKT)
    s_pv = P.sem()
    pv0 = s_pv.n
    s_pv.inc(NKT)
    s_rc = P.sem()
    rc0 = s_rc.n
    s_rc.inc(64)
    s_bc = P.sem()
    bc0 = s_bc.n
    s_bc.inc(64)
    s_oev = P.sem()
    oev0 = s_oev.n
    s_oev.inc(64)
    s_ot = P.sem()
    ot0 = s_ot.n
    s_ot.inc(64)
    s_qz = P.sem()
    qz0 = s_qz.n
    s_qz.inc(64)
    QZ = bufs["at_QZ"]

    def f_sc_pe(eng, e0=ex0, v0=vev0, q0=qkev0, r0=rc0, o0=oev0, t0=ot0,
                z0=qz0):
        eng.wait_ge(s_qkev.h, q0 + 64)
        eng.wait_ge(s_vev.h, v0 + 32)
        eng.wait_ge(bufs["ld_const"].h, bufs["ld_const"].n)
        for q in range(64):
            h, qc = divmod(q, 4)
            pv = pW0[0:65, (q % 2) * 512:(q % 2) * 512 + 512]
            if q >= 2:
                eng.wait_ge(s_rc.h, r0 + q - 1)
                if q >= 4:
                    eng.wait_ge(s_ot.h, t0 + q - 3)
                pbc = pW1[0:64, ((q - 2) % 2) * 512:((q - 2) % 2) * 512 + 512]
                ins = eng.matmul(pbc, bufs["ones_colf"][0:1, 0:64],
                                 inv[0:1, (q - 2) % 2, :], start=True, stop=True)
                ins.then_inc(s_bc.h, 1)
                eng.wait_ge(s_oev.h, o0 + q - 1)
            eng.wait_ge(s_qz.h, z0 + q + 1)
            KQ = KEEP[q]
            for i, kt in enumerate(KQ):
                g = GIDX[(q, kt)]
                if g >= 2:
                    eng.wait_ge(s_ex.h, e0 + g - 1)
                ps = pS[g % 2]
                eng.matmul(ps[:, :], KT[:, h // 2, kt * 128:(kt + 1) * 128],
                           QZ[:, q % 2, :],
                           start=True, stop=False)
                off = qc * 512 - kt * 128 + 1920
                ins = eng.matmul(ps[:, :], bufs["diag_slope"][:, h, :],
                                 bufs["toep"][:, off:off + 512],
                                 start=False, stop=True)
                ins.then_inc(s_sc.h, 1)
                if i >= 1:
                    eng.wait_ge(s_ex.h, e0 + g)
                    ins = eng.matmul(pv, Vtm[:, KQ[i - 1], h, 0:65],
                                     eS[:, (g - 1) % 2, :],
                                     start=(i == 1), stop=False)
                    ins.then_inc(s_pv.h, 1)
            eng.wait_ge(s_ex.h, e0 + GIDX[(q, KQ[-1])] + 1)
            ins = eng.matmul(pv, Vtm[:, KQ[-1], h, 0:65],
                             eS[:, GIDX[(q, KQ[-1])] % 2, :],
                             start=False, stop=True)
            ins.then_inc(s_pv.h, 1)
        for qq in (62, 63):
            eng.wait_ge(s_rc.h, r0 + qq + 1)
            eng.wait_ge(s_ot.h, t0 + qq - 1)
            pbc = pW1[0:64, (qq % 2) * 512:(qq % 2) * 512 + 512]
            ins = eng.matmul(pbc, bufs["ones_colf"][0:1, 0:64],
                             inv[0:1, qq % 2, :], start=True, stop=True)
            ins.then_inc(s_bc.h, 1)
    P.t(f_sc_pe)

    def f_sm_act(eng, s0=sc0, p0=pv0, t0=ot0):
        for q in range(64):
            for kt in KEEP[q]:
                g = GIDX[(q, kt)]
                eng.wait_ge(s_sc.h, s0 + g + 1)
                if g >= 2:
                    eng.wait_ge(s_pv.h, p0 + CONS[g - 2])
                ps = pS[g % 2]
                ins = eng.activation(eS[:, g % 2, :], ps[:, :], AF.Exp)
                ins.then_inc(s_ex.h, 1)
            if q >= 1:
                eng.wait_ge(s_pv.h, p0 + PVB[q])
                if q >= 3:
                    eng.wait_ge(s_ot.h, t0 + q - 2)
                pvp = pW0[0:65, ((q - 1) % 2) * 512:((q - 1) % 2) * 512 + 512]
                ins = eng.activation(Osc[:, (q - 1) % 2, :], pvp[0:64, :], AF.Identity)
                ins.then_inc(s_oev.h, 1)
        eng.wait_ge(s_pv.h, p0 + PVB[64])
        eng.wait_ge(s_ot.h, t0 + 62)
        pvp = pW0[0:65, (63 % 2) * 512:(63 % 2) * 512 + 512]
        ins = eng.activation(Osc[:, 63 % 2, :], pvp[0:64, :], AF.Identity)
        ins.then_inc(s_oev.h, 1)
    P.s(f_sm_act)

    def _qz_fill(eng, u):
        h, qc = divmod(u, 4)
        lo = (h % 2) * 64
        sib = 64 - lo
        eng.tensor_copy(QZ[lo:lo + 64, u % 2, :],
                        QT[lo:lo + 64, h // 2, qc * 512:(qc + 1) * 512])
        eng.memset(QZ[sib:sib + 64, u % 2, :], 0.0).then_inc(s_qz.h, 1)

    def f_sm_dve(eng, p0=pv0, b0=bc0, o0=oev0, q0=qkev0):
        eng.wait_ge(s_qkev.h, q0 + 32)
        _qz_fill(eng, 0)
        _qz_fill(eng, 1)
        for q in range(64):
            h, qc = divmod(q, 4)
            eng.wait_ge(s_pv.h, p0 + PVB[q + 1])
            if q >= 2:
                eng.wait_ge(s_bc.h, b0 + q - 1)
            pv = pW0[0:65, (q % 2) * 512:(q % 2) * 512 + 512]
            ins = eng.reciprocal(inv[0:1, q % 2, :], pv[64:65, :])
            ins.then_inc(s_rc.h, 1)
            if q <= 61:
                _qz_fill(eng, q + 2)
            eng.wait_ge(s_bc.h, b0 + q + 1)
            eng.wait_ge(s_oev.h, o0 + q + 1)
            pbc = pW1[0:64, (q % 2) * 512:(q % 2) * 512 + 512]
            ins = eng.tensor_tensor(OT[(h % 2) * 64:(h % 2) * 64 + 64, h // 2,
                                       qc * 512:(qc + 1) * 512],
                                    Osc[:, q % 2, :], pbc, ALU.mult)
            ins.then_inc(s_ot.h, 1)
    P.v(f_sm_dve)

    s_rest = P.sem(dma=True)
    v_rest = s_rest.inc(16)

    def f_rest(eng, w=sc0, wsp=v_spill):
        eng.wait_ge(s_sc.h, w + NKT)
        eng.dma_start(out=xres[:], in_=spill_ap).then_inc(s_rest.h, 16)
    P.y(f_rest)

    s_wol = P.sem(dma=True)
    wol0 = s_wol.n
    s_wol.inc(16 * 2)
    s_wo = P.sem()
    wo0 = s_wo.n
    s_wo.inc(2 * NT)
    s_res = P.sem()
    res0 = s_res.n
    s_res.inc(2 * NT)

    def f_wo_ld(eng, t0=ot0, w0=wo0):
        for half in range(2):
            if half == 1:
                eng.wait_ge(s_wo.h, w0 + NT)
            eng.dma_start(out=bufs["at_Wout"][:],
                          in_=waps["Wout"].rearrange("(a p) d -> p a d", p=128)
                          [:, :, half * 512:(half + 1) * 512]).then_inc(s_wol.h, 16)
    P.g(f_wo_ld)

    def f_wo_pe(eng, t0=ot0, whl=v_hl, res0=res0, w0=wol0):
        eng.wait_ge(s_ot.h, t0 + 64)
        eng.wait_ge(s_misc.h, whl)
        for half in range(2):
            eng.wait_ge(s_wol.h, w0 + 16 * (half + 1))
            sl = slice(half * 512, (half + 1) * 512)
            for tt in range(NT):
                u = half * NT + tt
                if u >= 4:
                    eng.wait_ge(s_res.h, res0 + u - 3)
                ps = [pW1[:, 0:512], pW1[:, 512:1024],
                      pW0[:, 0:512], pW0[:, 512:1024]][u % 4]
                for dt in range(ND):
                    eng.matmul(ps, OT[:, dt, tt * 128:(tt + 1) * 128],
                               Wout[:, dt, :], start=(dt == 0), stop=False)
                eng.matmul(ps, bufs["ones_col"][:, 0:128],
                           bufs["at_boh"][0:1, sl], start=False, stop=False)
                ins = eng.matmul(ps, bufs["ones_col"][:, 0:128],
                                 bufs["at_bol"][0:1, sl], start=False, stop=True)
                ins.then_inc(s_wo.h, 1)
    P.t(f_wo_pe)

    def f_wo_res(eng, w0=wo0, wr=v_rest):
        eng.wait_ge(s_rest.h, wr)
        for half in range(2):
            sl = slice(half * 512, (half + 1) * 512)
            for tt in range(NT):
                u = half * NT + tt
                eng.wait_ge(s_wo.h, w0 + u + 1)
                ps = [pW1[:, 0:512], pW1[:, 512:1024],
                      pW0[:, 0:512], pW0[:, 512:1024]][u % 4]
                ins = eng.tensor_tensor(xres[:, tt, sl], ps, xres[:, tt, sl], ALU.add)
                ins.then_inc(s_res.h, 1)
    P.v(f_wo_res)
    return s_res



def declare_l0(P, bufs):
    TBL = 512
    bufs["l0_R"] = P.sb("l0_R", [128, 8, 64], F32)
    bufs["l0_Rbf"] = P.sb("l0_Rbf", [128, 8, 64], BF16)
    bufs["l0_atm"] = P.sb("l0_atm", [128, 16, 16], F32)
    bufs["l0_btm"] = P.sb("l0_btm", [128, 16, 16], F32)
    bufs["l0_etm"] = P.sb("l0_etm", [128, 16, 16], F32)
    bufs["l0_dtm"] = P.sb("l0_dtm", [128, 16, 16], F32)
    bufs["l0_cw"] = P.sb("l0_cw", [128, 24, 4], F32)
    bufs["l0_cb"] = P.sb("l0_cb", [128, 24], F32)
    bufs["l0_wbase"] = P.sb("l0_wbase", [16, 1], F32)
    bufs["l0_wA"] = P.sb("l0_wA", [128, 8, 16], BF16)
    bufs["l0_diag"] = P.sb("l0_diag", [128, 4, 128], BF16)
    bufs["l0_opsc"] = P.sb("l0_opsc", [128, 2, 512], F32)
    bufs["l0_small"] = P.sb("l0_small", [16, 8, 33], F32)
    bufs["l0_mt"] = P.sb("l0_mt", [128, 2, 8, 64], BF16)
    bufs["l0_yst"] = P.sb("l0_yst", [128, 2, 512], F32)
    bufs["l0_msk"] = P.sb_at("l0_msk", [128, 2, 8, 64], BF16, P.offsets["l0_yst"])
    bufs["l0_ipw"] = P.sb("l0_ipw", [128, 2, 8, 128], BF16)
    bufs["l0_ipw2"] = P.sb_at("l0_ipw2", [128, 2, 8, 128], BF16,
                              P.offsets["ln_tm"])
    bufs["l0_opw"] = P.sb("l0_opw", [128, 2, 8, 512], BF16)
    base = P.mark()
    bufs["l0_B1"] = P.sb_at("l0_B1", [16, 2048], F32, base)
    bufs["l0_B2"] = P.sb_at("l0_B2", [16, 2048], F32, base + 8 * 1024)
    bufs["l0_B3"] = P.sb_at("l0_B3", [16, 2048], F32, base + 16 * 1024)
    bufs["l0_B4"] = P.sb_at("l0_B4", [16, 2048], F32, base + 24 * 1024)
    bufs["l0_wraw"] = P.sb_at("l0_wraw", [128, 4, 1024], F32, base)
    bufs["l0_wev"] = P.sb_at("l0_wev", [128, 2, 8, 128], BF16, base + 16 * 1024)
    off = base
    bufs["l0_xio"] = P.sb_at("l0_xio", [128, 24, TBL + 3], BF16, off)
    off += ((24 * (TBL + 3) * 2 + 63) // 64) * 64
    bufs["l0_Itm"] = P.sb_at("l0_Itm", [128, 4, 1024], BF16, off)
    off += 8 * 1024
    bufs["l0_Otm"] = P.sb_at("l0_Otm", [128, 4, 1024], BF16, off)
    off += 8 * 1024
    bufs["l0_Ytm"] = P.sb_at("l0_Ytm", [128, 4, 1024], BF16, off)
    bufs["l0_diag2"] = P.sb_at("l0_diag2", [128, 4, 128], BF16, off)
    bufs["l0_YT"] = P.sb_at("l0_YT", [128, 4, 8, 128], BF16, off)
    off += 8 * 1024
    bufs["l0_halo"] = P.sb_at("l0_halo", [128, 24, 3], BF16, off)
    off += ((24 * 3 * 2 + 63) // 64) * 64
    P.reset_to(max(off, base + 32 * 1024))


def emit_l0_weightprep(P, bufs, waps, ipT_dram, opT_dram):
    wraw, wev = bufs["l0_wraw"], bufs["l0_wev"]
    pWl = [bufs["psumW0"], bufs["psumW1"]]
    ld4 = [P.sem(dma=True) for _ in range(4)]
    s_pe = P.sem()
    pe0 = s_pe.n
    s_pe.inc(33)
    s_ev = P.sem()
    ev0 = s_ev.n
    s_ev.inc(33)
    pk1 = P.sem(dma=True)
    s_pk = P.sem()
    pk0 = s_pk.n
    s_pk.inc(33)

    WP_ORDER = [24] + list(range(24)) + list(range(25, 33))

    def f_ld(eng):
        for p, u in enumerate(WP_ORDER):
            if p >= 4:
                eng.wait_ge(s_pe.h, pe0 + p - 3)
            if u < 25:
                jw = 128 if u < 24 else 16
                src = waps["in_proj_w"][u * 128:u * 128 + jw, :]
            else:
                jw = 128
                src = waps["out_proj_w"][(u - 25) * 128:(u - 24) * 128, :]
            eng.dma_start(out=wraw[0:jw, p % 4, :],
                          in_=src).then_inc(ld4[p % 4].h, 16)
    P.s(f_ld)

    def f_pe(eng, e0=ev0):
        eng.wait_ge(bufs["ld_const"].h, bufs["ld_const"].n)
        for p, u in enumerate(WP_ORDER):
            jw = 128 if (u != 24) else 16
            eng.wait_ge(ld4[p % 4].h, 16 * (p // 4 + 1))
            if p >= 2:
                eng.wait_ge(s_ev.h, e0 + p - 1)
            for dt in range(8):
                ins = eng.transpose(pWl[p % 2][:, dt * 128:dt * 128 + jw],
                                    wraw[0:jw, p % 4, dt * 128:(dt + 1) * 128],
                                    bufs["identf"][0:jw, 0:jw])
            ins.then_inc(s_pe.h, 1)
    P.t(f_pe)

    def f_ev(eng, p0=pe0):
        for p, u in enumerate(WP_ORDER):
            jw = 128 if (u != 24) else 16
            if p % 2 == 0:
                eng.wait_ge(s_pe.h, p0 + min(p + 2, 33))
            if p >= 2:
                eng.wait_ge(pk1.h, 16 * (p - 1))
            ins = eng.tensor_copy(
                wev[:, p % 2, :, 0:jw],
                pWl[p % 2][:].rearrange("p (a b) -> p a b", a=8)[:, :, 0:jw])
            ins.then_inc(s_ev.h, 1)
    P.v(f_ev)

    s_l2 = P.sem(dma=True)
    l2_sem = Sem(s_l2.h)
    l2_sem.n = s_l2.inc(16 * 3)

    def f_pk(eng, e0=ev0):
        for p, u in enumerate(WP_ORDER):
            jw = 128 if (u != 24) else 16
            eng.wait_ge(s_ev.h, e0 + p + 1)
            if p >= 1:
                eng.wait_ge(pk1.h, 16 * p)
            if u < 25:
                dst = ipT_dram[:, :, u * 128:u * 128 + jw]
            else:
                dst = opT_dram[:, :, (u - 25) * 128:(u - 24) * 128]
            eng.dma_start(out=dst, in_=wev[:, p % 2, :, 0:jw]).then_inc(pk1.h, 16)
            eng.wait_ge(pk1.h, 16 * (p + 1))
            eng.sem_inc(s_pk.h, 1)
            if u == 24:
                eng.dma_start(out=bufs["l0_wA"][:],
                              in_=ipT_dram[:, :, 3072:3088]).then_inc(s_l2.h, 16)
                eng.dma_start(out=bufs["l0_wbase"][:],
                              in_=waps["w_base"].unsqueeze(1)).then_inc(s_l2.h, 16)
                eng.dma_start(out=bufs["l0_cw"][:],
                              in_=waps["conv_w"].rearrange(
                                  "(a p) one k -> p a (one k)", p=128)
                              ).then_inc(s_l2.h, 16)
    P.y(f_pk)
    ev_sem = Sem(s_ev.h)
    ev_sem.n = ev0 + 33
    return s_pk, l2_sem, l2_sem, ev_sem


def emit_l0attn(P, bufs, waps, ln_dep, x_dep, ipT_dram, opT_dram, prep_dep,
                cw_dep, l2_dep):
    TBL = 512
    NB = T // TBL
    NCH = TBL // 64
    xres, lnT = bufs["xres"], bufs["lnT"]
    xio = bufs["l0_xio"]
    Itm, Otm, Ytm, YT = bufs["l0_Itm"], bufs["l0_Otm"], bufs["l0_Ytm"], bufs["l0_YT"]
    B1, B2, B3, B4 = bufs["l0_B1"], bufs["l0_B2"], bufs["l0_B3"], bufs["l0_B4"]
    sm = bufs["l0_small"]
    AL = sm[:, 0, 0:32]
    cumAL = sm[:, 1, :]
    cminAL = sm[:, 2, :]
    encAL = sm[:, 3, :]
    expcmAL = sm[:, 4, :]
    ndp = sm[:, 5, :]
    negmax = sm[:, 6, 0:32]
    scr33 = sm[:, 7, :]
    atm, btm, etm, dtm = bufs["l0_atm"], bufs["l0_btm"], bufs["l0_etm"], bufs["l0_dtm"]
    R, Rbf = bufs["l0_R"], bufs["l0_Rbf"]
    mt, msk, yst = bufs["l0_mt"], bufs["l0_msk"], bufs["l0_yst"]
    pT = [bufs["psumT0"], bufs["psumT1"]]
    pW0, pW1 = bufs["psumW0"], bufs["psumW1"]
    pS0, pS1 = bufs["psumS0"], bufs["psumS1"]

    s_l2 = l2_dep
    v_l2 = l2_dep.n

    cb_dep = emit_colbias(P, bufs, waps["conv_b"], bufs["l0_cb"], 24,
                          bufs["ld_const"], 0)

    s_app = P.sem()
    app0 = s_app.n
    s_app.inc(4)
    s_apa = P.sem()
    s_apd = P.sem()

    ape0 = s_apa.n
    v_araw = s_apa.inc(4)

    def f_ap_pe(eng, wl=(ln_dep.h, ln_dep.n), w2=v_l2, e0=ape0):
        eng.wait_ge(wl[0], wl[1])
        eng.wait_ge(s_l2.h, w2)
        for tc in range(4):
            if tc >= 2:
                eng.wait_ge(s_apa.h, e0 + tc - 1)
            ps = pS0 if tc % 2 == 0 else pS1
            for dt in range(ND):
                ins = eng.matmul(ps[0:16, 0:512], bufs["l0_wA"][:, dt, :],
                                 lnT[:, dt, tc * 512:(tc + 1) * 512],
                                 start=(dt == 0), stop=(dt == ND - 1))
            ins.then_inc(s_app.h, 1)
    P.t(f_ap_pe)

    def f_ap_evac(eng, a0=app0, wpk=(prep_dep.h, prep_dep.n)):
        eng.wait_ge(wpk[0], wpk[1])
        for tc in range(4):
            eng.wait_ge(s_app.h, a0 + tc + 1)
            ps = pS0 if tc % 2 == 0 else pS1
            ins = eng.activation(B1[:, tc * 512:(tc + 1) * 512], ps[0:16, 0:512],
                                 AF.Identity)
            ins.then_inc(s_apa.h, 1)
    P.s(f_ap_evac)

    v_d1 = s_apd.inc()

    def f_ap_d1(eng, wa=v_araw, w2=v_l2):
        eng.wait_ge(s_apa.h, wa)
        eng.wait_ge(s_l2.h, w2)
        eng.tensor_scalar_mul(B1[:, :], B1[:, :], bufs["l0_wbase"][:, 0:1])
        eng.drain()
        for c in range(32):
            sl = slice(c * 64, (c + 1) * 64)
            eng.tensor_tensor_scan(B2[:, sl], B1[:, sl], B1[:, sl], 0.0,
                                   ALU.add, ALU.bypass)
        eng.drain()
        for c in range(32):
            sl = slice(c * 64, (c + 1) * 64)
            eng.tensor_tensor_scan(B3[:, sl], B2[:, sl], B2[:, sl], 1e30,
                                   ALU.min, ALU.bypass)
        eng.drain()
        eng.tensor_copy(AL, B2[:].rearrange("p (c l) -> p c l", l=64)[:, :, 63])
        for c in range(32):
            eng.reduce_max(negmax[:, c:c + 1], B2[:, c * 64:(c + 1) * 64], AX.X,
                           negate=True)
        eng.drain()
        eng.memset(ndp[:, 0:1], 0.0)
        eng.memset(scr33[:, 0:1], 0.0)
        eng.drain()
        eng.tensor_copy(scr33[:, 1:33], AL)
        eng.drain()
        eng.tensor_tensor_scan(cumAL[:, :], scr33[:, :], scr33[:, :], 0.0,
                               ALU.add, ALU.bypass)
        eng.drain()
        eng.tensor_tensor_scan(cminAL[:, :], cumAL[:, :], cumAL[:, :], 1e30,
                               ALU.min, ALU.bypass)
        eng.drain().then_inc(s_apd.h, 1)
    P.v(f_ap_d1)

    v_a2 = s_apa.inc()

    def f_ap_a2(eng, wd=v_d1):
        eng.wait_ge(s_apd.h, wd)
        for c in range(32):
            sl = slice(c * 64, (c + 1) * 64)
            eng.activation(B4[:, sl], B2[:, sl], AF.Exp, scale=-1.0,
                           bias=B3[:, c * 64 + 63:c * 64 + 64])
        eng.activation(encAL[:, :], cumAL[:, :], AF.Exp, scale=-1.0)
        eng.activation(expcmAL[:, :], cminAL[:, :], AF.Exp)
        eng.drain().then_inc(s_apa.h, 1)
    P.s(f_ap_a2)

    v_d2 = s_apd.inc()

    def f_ap_d2(eng, wa=v_a2):
        eng.wait_ge(s_apa.h, wa)
        for c in range(32):
            eng.reduce_sum(ndp[:, c + 1:c + 2], B4[:, c * 64:(c + 1) * 64], AX.X)
        eng.drain()
        for c in range(32):
            eng.tensor_scalar_mul(B4[:, c * 64:(c + 1) * 64],
                                  B4[:, c * 64:(c + 1) * 64], encAL[:, c + 1:c + 2])
        eng.drain()
        eng.tensor_tensor(scr33[:, :], ndp[:, :], encAL[:, :], ALU.mult)
        eng.drain()
        eng.tensor_tensor_scan(cumAL[:, :], scr33[:, :], scr33[:, :], 0.0,
                               ALU.add, ALU.bypass)
        eng.drain()
        eng.tensor_tensor(scr33[:, :], cumAL[:, :], expcmAL[:, :], ALU.mult)
        eng.drain().then_inc(s_apd.h, 1)
    P.v(f_ap_d2)

    v_a3 = s_apa.inc()

    def f_ap_a3(eng, wd=v_d2):
        eng.wait_ge(s_apd.h, wd)
        eng.activation(B1[:, :], B2[:, :], AF.Exp, scale=-1.0)
        eng.activation(B3[:, :], B3[:, :], AF.Exp)
        eng.drain().then_inc(s_apa.h, 1)
    P.s(f_ap_a3)

    s_tmp = P.sem()
    tmp0 = s_tmp.n
    s_tmp.inc(2)
    s_tmd = P.sem()
    tmd0 = s_tmd.n
    s_tmd.inc(2)

    def f_tm_pe1(eng, wa=v_a3, d0=tmd0):
        eng.wait_ge(s_apa.h, wa)
        for i, src in enumerate([B4, B1]):
            if i >= 1:
                eng.wait_ge(s_tmd.h, d0 + i)
            for tt in range(16):
                ins = eng.transpose(pW0[:, tt * 16:(tt + 1) * 16],
                                    src[:, tt * 128:(tt + 1) * 128],
                                    bufs["identf"][0:16, 0:16])
            ins.then_inc(s_tmp.h, 1)
    P.t(f_tm_pe1)

    def f_tm_d1(eng, p0=tmp0):
        for i, dst in enumerate([dtm, etm]):
            eng.wait_ge(s_tmp.h, p0 + i + 1)
            ins = eng.tensor_copy(dst[:],
                                  pW0[:, 0:256].rearrange("p (a b) -> p a b", a=16))
            ins.then_inc(s_tmd.h, 1)
    P.v(f_tm_d1)

    v_d4 = s_apd.inc()

    def f_ap_d4(eng, d0=tmd0):
        eng.wait_ge(s_tmd.h, d0 + 2)
        for c in range(32):
            sl = slice(c * 64, (c + 1) * 64)
            eng.tensor_tensor_scan(B4[:, sl], B1[:, sl], B1[:, sl], 0.0,
                                   ALU.add, ALU.bypass)
        eng.drain()
        eng.tensor_tensor(B4[:, :], B4[:, :], B3[:, :], ALU.mult)
        eng.drain().then_inc(s_apd.h, 1)
    P.v(f_ap_d4)

    v_a4 = s_apa.inc()

    def f_ap_a4(eng, wd=v_d4):
        eng.wait_ge(s_apd.h, wd)
        for c in range(32):
            sl = slice(c * 64, (c + 1) * 64)
            eng.activation(B1[:, sl], B2[:, sl], AF.Exp,
                           bias=negmax[:, c:c + 1])
        eng.drain().then_inc(s_apa.h, 1)
    P.s(f_ap_a4)

    v_d5 = s_apd.inc()

    def f_ap_d5(eng, wa=v_a4):
        eng.wait_ge(s_apa.h, wa)
        for c in range(32):
            sl = slice(c * 64, (c + 1) * 64)
            eng.scalar_tensor_tensor(B4[:, sl], B1[:, sl], scr33[:, c:c + 1],
                                     B4[:, sl], ALU.mult, ALU.add)
        eng.drain()
        eng.reciprocal(B4[:, :], B4[:, :])
        eng.drain()
        eng.tensor_tensor(B3[:, :], B3[:, :], B4[:, :], ALU.mult)
        eng.drain()
        for c in range(32):
            sl = slice(c * 64, (c + 1) * 64)
            eng.scalar_tensor_tensor(B2[:, sl], B1[:, sl], expcmAL[:, c:c + 1],
                                     B4[:, sl], ALU.mult, ALU.mult)
        eng.drain().then_inc(s_apd.h, 1)
    P.v(f_ap_d5)

    s_tp2 = P.sem()
    tp20 = s_tp2.n
    s_tp2.inc(2)
    s_td2 = P.sem()
    td20 = s_td2.n
    s_td2.inc(2)

    def f_tm_pe2(eng, wd=v_d5, d0=td20):
        eng.wait_ge(s_apd.h, wd)
        for i, src in enumerate([B3, B2]):
            if i >= 1:
                eng.wait_ge(s_td2.h, d0 + i)
            for tt in range(16):
                ins = eng.transpose(pW0[:, tt * 16:(tt + 1) * 16],
                                    src[:, tt * 128:(tt + 1) * 128],
                                    bufs["identf"][0:16, 0:16])
            ins.then_inc(s_tp2.h, 1)
    P.t(f_tm_pe2)

    def f_tm_d2(eng, p0=tp20):
        for i, dst in enumerate([atm, btm]):
            eng.wait_ge(s_tp2.h, p0 + i + 1)
            ins = eng.tensor_copy(dst[:],
                                  pW0[:, 0:256].rearrange("p (a b) -> p a b", a=16))
            ins.then_inc(s_td2.h, 1)
    P.v(f_tm_d2)

    s_r0 = P.sem()
    v_r0 = s_r0.inc()

    def f_r0(eng):
        eng.memset(R[:], 0.0)
        eng.memset(Rbf[:], 0.0)
        eng.drain().then_inc(s_r0.h, 1)
    P.v(f_r0)

    ipl4 = [P.sem(dma=True) for _ in range(4)]
    s_opl = P.sem(dma=True)
    opl0 = s_opl.n
    s_opl.inc(16 * NB * 2)
    s_ip = P.sem()
    ip0 = s_ip.n
    s_ip.inc(NB * 24)
    s_ipe = P.sem()
    ipe0 = s_ipe.n
    s_ipe.inc(NB * 24)
    P.l0_pace = (s_ipe.h, ipe0, NB * 24)
    s_hl = P.sem()
    hl0 = s_hl.n
    s_hl.inc(NB)
    s_dg = P.sem()
    dg0 = s_dg.n
    s_dg.inc(NB * 24)
    s_cv = P.sem()
    cv0 = s_cv.n
    s_cv.inc(NB * 24)
    s_cve = P.sem()
    cve0 = s_cve.n
    s_cve.inc(NB * 24)
    s_tr = P.sem()
    tr0 = s_tr.n
    s_tr.inc(NB * 32)
    s_tre = P.sem()
    tre0 = s_tre.n
    s_tre.inc(NB * 32)
    s_gt = P.sem()
    gt0 = s_gt.n
    s_gt.inc(NB * NCH * 2)
    s_mte = P.sem()
    mte0 = s_mte.n
    s_mte.inc(NB * NCH * 2)
    s_yd = P.sem()
    yd0 = s_yd.n
    s_yd.inc(NB * NCH)
    s_st = P.sem()
    st0 = s_st.n
    s_st.inc(NB * NCH)
    s_rup = P.sem()
    rup0 = s_rup.n
    s_rup.inc(NB * NCH)
    s_ye = P.sem()
    ye0 = s_ye.n
    s_ye.inc(NB * (NCH // 2) * 2)
    s_yt = P.sem()
    yt0 = s_yt.n
    s_yt.inc(NB * 4)
    s_yte = P.sem()
    yte0 = s_yte.n
    s_yte.inc(NB * 4)
    s_op = P.sem()
    op0 = s_op.n
    s_op.inc(NB * 8)
    s_ope = P.sem()
    ope0 = s_ope.n
    s_ope.inc(NB * 8)
    s_res = P.sem()
    res0 = s_res.n
    s_res.inc(NB * 8)

    def f_blk_dma(eng, wpk=(prep_dep.h, prep_dep.n), wln2=(ln_dep.h, ln_dep.n)):
        if L0_STAGE < 2:
            return
        eng.wait_ge(wpk[0], wpk[1])
        for b in range(NB):
            for jc in range(24):
                u = b * 24 + jc
                if u >= 4:
                    eng.wait_ge(s_ip.h, ip0 + u - 3)
                elif u == 2:
                    eng.wait_ge(wln2[0], wln2[1])
                dbuf = bufs["l0_ipw"] if u % 4 < 2 else bufs["l0_ipw2"]
                eng.dma_start(out=dbuf[:, u % 2],
                              in_=ipT_dram[:, :, jc * 128:(jc + 1) * 128]
                              ).then_inc(ipl4[u % 4].h, 16)
            for half in range(2 if L0_STAGE >= 6 else 0):
                u8 = b * 2 + half
                if u8 >= 2:
                    eng.wait_ge(s_op.h, op0 + (u8 - 1) * 4)
                if u8 >= 1:
                    eng.wait_ge(s_opl.h, opl0 + 16 * u8)
                eng.dma_start(out=bufs["l0_opw"][:, u8 % 2],
                              in_=opT_dram[:, :, half * 512:(half + 1) * 512]).then_inc(s_opl.h, 16)
    P.y(f_blk_dma)

    def f_blk_pe(eng, wln=(ln_dep.h, ln_dep.n), wr0=v_r0, wd2=td20):
        if L0_STAGE < 2:
            return
        eng.wait_ge(bufs["ld_const"].h, bufs["ld_const"].n)
        eng.wait_ge(wln[0], wln[1])
        eng.wait_ge(s_r0.h, wr0)
        eng.wait_ge(s_td2.h, wd2 + 2)
        for b in range(NB):
            for jc in range(24):
                u = b * 24 + jc
                eng.wait_ge(ipl4[u % 4].h, 16 * (u // 4 + 1))
                if u >= 4:
                    eng.wait_ge(s_ipe.h, ipe0 + u - 3)
                if b >= 1 and jc == 0:
                    if L0_STAGE >= 3:
                        eng.wait_ge(s_hl.h, hl0 + b)
                    if L0_STAGE >= 5:
                        eng.wait_ge(s_rup.h, rup0 + b * NCH)
                        eng.wait_ge(s_mte.h, mte0 + b * NCH * 2)
                if jc == 2:
                    if b == 0:
                        eng.wait_ge(cb_dep.h, cb_dep.n)
                    elif L0_STAGE >= 6:
                        eng.wait_ge(s_ope.h, ope0 + b * 8 - 2)
                    elif L0_STAGE >= 5:
                        eng.wait_ge(s_ye.h, ye0 + b * 8)
                ps = [pS0[:, :], pS1[:, :], pW1[:, 0:512],
                      pW1[:, 512:1024]][u % 4]
                wbuf = bufs["l0_ipw"] if u % 4 < 2 else bufs["l0_ipw2"]
                for dt in range(ND):
                    ins = eng.matmul(ps, wbuf[:, u % 2, dt, :],
                                     lnT[:, dt, b * 512:(b + 1) * 512],
                                     start=(dt == 0), stop=(dt == ND - 1))
                ins.then_inc(s_ip.h, 1)
            for ct in range(24 if L0_STAGE >= 3 else 0):
                u = b * 24 + ct
                eng.wait_ge(s_dg.h, dg0 + u + 1)
                if u >= 2:
                    eng.wait_ge(s_cve.h, cve0 + u - 1)
                if b >= 1 and ct < 2 and L0_STAGE >= 6:
                    eng.wait_ge(s_ope.h, ope0 + b * 8)
                ps = pW0[:, 0:512] if ct % 2 == 0 else pW0[:, 512:1024]
                dgb = bufs["l0_diag"] if ct % 2 == 0 else bufs["l0_diag2"]
                for k in range(4):
                    ins = eng.matmul(ps, dgb[:, k, :],
                                     xio[:, ct, k:k + 512],
                                     start=(k == 0), stop=(k == 3))
                ins.then_inc(s_cv.h, 1)
            for i in range(2 if L0_STAGE >= 4 else 0):
                for g in range(16):
                    u = b * 32 + i * 16 + g
                    ctl, tc = divmod(g, 4)
                    eng.wait_ge(s_cve.h, cve0 + b * 24 + 24)
                    if u >= 2:
                        eng.wait_ge(s_tre.h, tre0 + u - 1)
                    ps = pT[u % 2]
                    for sub in range(2):
                        ct = 8 + i * 8 + ctl * 2 + sub
                        ins = eng.transpose(ps[:, sub * 128:(sub + 1) * 128],
                                            xio[:, ct, 3 + tc * 128:3 + (tc + 1) * 128],
                                            bufs["ident"][:])
                    ins.then_inc(s_tr.h, 1)
            for cl in range(NCH if L0_STAGE >= 5 else 0):
                c = b * NCH + cl
                par = c % 2
                tt = c // 2
                eng.wait_ge(s_tre.h, tre0 + b * 32 + 32)
                if c >= 1:
                    eng.wait_ge(s_mte.h, mte0 + c * 2)
                for hg in range(2):
                    eng.drain()
                    for hh in [0, 2, 4, 6, -1, 1, 3, 5, 7]:
                        if hh < 0:
                            eng.drain()
                            continue
                        h = hg * 8 + hh
                        hb = (h % 2) * 64
                        ins = eng.matmul(
                            pS0[hg * 64:hg * 64 + 64, hh * 64:(hh + 1) * 64],
                            xio[hb:hb + 64, 8 + h // 2, 3 + cl * 64:3 + (cl + 1) * 64],
                            xio[hb:hb + 64, h // 2, 3 + cl * 64:3 + (cl + 1) * 64],
                            start=(hh == 0), stop=True, skip_group_check=True,
                            tile_position=(hb, hg * 64))
                    ins.then_inc(s_gt.h, 1)
                eng.wait_ge(s_rup.h, rup0 + c)
                if cl == 0:
                    eng.wait_ge(s_ipe.h, ipe0 + (b + 1) * 24)
                if par == 0:
                    eng.wait_ge(s_ye.h, ye0 + (c // 2) * 2)
                eng.drain()
                for h in [0, 2, 4, 6, 8, 10, 12, 14, -1, 1, 3, 5, 7, 9, 11, 13, 15]:
                    if h < 0:
                        eng.drain()
                        continue
                    hb = (h % 2) * 64
                    half = (h // 8) * 512
                    eng.matmul(pW1[par * 64:par * 64 + 64,
                                   half + (h % 8) * 64:half + (h % 8) * 64 + 64],
                               xio[hb:hb + 64, h // 2, 3 + cl * 64:3 + (cl + 1) * 64],
                               Rbf[hb:hb + 64, h // 2, :],
                               start=(h in (0, 8)), stop=True,
                               skip_group_check=True,
                               tile_position=(hb, par * 64))
                eng.drain()
                for h in range(16):
                    hb = (h % 2) * 64
                    ins = eng.matmul(pS1[hb:hb + 64, (h // 2) * 64:(h // 2) * 64 + 64],
                                     Itm[par * 64:par * 64 + 64, cl // 2,
                                         h * 64:(h + 1) * 64],
                                     Otm[par * 64:par * 64 + 64, cl // 2,
                                         h * 64:(h + 1) * 64],
                                     start=(h < 2), stop=True, skip_group_check=True,
                                     tile_position=(par * 64, hb))
                ins.then_inc(s_st.h, 1)
                eng.drain()
                eng.wait_ge(s_mte.h, mte0 + c * 2 + 2)
                for h in range(16):
                    half = (h // 8) * 512
                    ins = eng.matmul(pW0[par * 64:par * 64 + 64,
                                         half + (h % 8) * 64:half + (h % 8) * 64 + 64],
                                     mt[par * 64:par * 64 + 64, h // 8, h % 8, :],
                                     Otm[par * 64:par * 64 + 64, cl // 2,
                                         h * 64:(h + 1) * 64],
                                     start=(h % 8 == 0), stop=True,
                                     skip_group_check=True,
                                     tile_position=(par * 64, par * 64))
                    ins.then_inc(s_yd.h, 1) if h == 15 else None
            if L0_STAGE < 6:
                continue
            for g in range(4):
                u = b * 4 + g
                eng.wait_ge(s_ye.h, ye0 + b * 8 + (g + 1) * 2)
                if u >= 2:
                    eng.wait_ge(s_yte.h, yte0 + u - 1)
                ps = pT[u % 2]
                for ct in range(8):
                    ins = eng.transpose(ps[:, ct * 128:(ct + 1) * 128],
                                        Ytm[:, g, ct * 128:(ct + 1) * 128],
                                        bufs["ident"][:])
                ins.then_inc(s_yt.h, 1)
            for half in range(2):
                u8 = b * 2 + half
                eng.wait_ge(s_opl.h, opl0 + 16 * (u8 + 1))
                for g in range(4):
                    u = u8 * 4 + g
                    eng.wait_ge(s_yte.h, yte0 + b * 4 + g + 1)
                    if u >= 4:
                        eng.wait_ge(s_ope.h, ope0 + u - 3)
                    ps = [pW1[:, 0:512], pW1[:, 512:1024],
                          pW0[:, 0:512], pW0[:, 512:1024]][u % 4]
                    sl = slice(half * 512, (half + 1) * 512)
                    for ctt in range(8):
                        ins = eng.matmul(ps, YT[:, g, ctt, :],
                                         bufs["l0_opw"][:, u8 % 2, ctt, :],
                                         start=(ctt == 0), stop=(ctt == 7))
                    ins.then_inc(s_op.h, 1)
    P.t(f_blk_pe)

    def f_blk_act(eng):
        if L0_STAGE < 2:
            return

        def ope_one(u):
            eng.wait_ge(s_op.h, op0 + u + 1)
            if u >= 2:
                eng.wait_ge(s_res.h, res0 + u - 1)
            ps = [pW1[:, 0:512], pW1[:, 512:1024],
                  pW0[:, 0:512], pW0[:, 512:1024]][u % 4]
            eng.activation(bufs["l0_opsc"][:, u % 2, :], ps,
                           AF.Identity).then_inc(s_ope.h, 1)

        for b in range(NB):
            for jc in range(24):
                u = b * 24 + jc
                if jc % 2 == 0:
                    eng.wait_ge(s_ip.h, ip0 + min(u + 2, b * 24 + 24))
                if b >= 1 and jc == 0 and L0_STAGE >= 3:
                    eng.wait_ge(s_hl.h, hl0 + b)
                ps = [pS0[:, :], pS1[:, :], pW1[:, 0:512],
                      pW1[:, 512:1024]][u % 4]
                ins = eng.activation(xio[:, jc, 3:515], ps, AF.Identity)
                ins.then_inc(s_ipe.h, 1)
                if b >= 1 and L0_STAGE >= 6:
                    if jc < 2:
                        ope_one((b - 1) * 8 + 4 + jc)
                    elif 16 <= jc < 18:
                        ope_one((b - 1) * 8 + 6 + jc - 16)
            for ct in range(24 if L0_STAGE >= 3 else 0):
                u = b * 24 + ct
                eng.wait_ge(s_cv.h, cv0 + u + 1)
                ps = pW0[:, 0:512] if ct % 2 == 0 else pW0[:, 512:1024]
                ins = eng.activation(xio[:, ct, 3:515], ps, AF.Identity,
                                     bias=bufs["l0_cb"][:, ct:ct + 1])
                ins.then_inc(s_cve.h, 1)
            if L0_STAGE < 6:
                continue
            for g in range(4):
                if g % 2 == 0:
                    eng.wait_ge(s_yt.h, yt0 + b * 4 + g + 2)
                ins = eng.activation(YT[:, g],
                                     pT[(b * 4 + g) % 2][:]
                                     .rearrange("p (a b) -> p a b", a=8),
                                     AF.Identity)
                ins.then_inc(s_yte.h, 1)
            for k in range(4):
                ope_one(b * 8 + k)
        if L0_STAGE >= 6:
            for u in range((NB - 1) * 8 + 4, NB * 8):
                ope_one(u)
    P.s(f_blk_act)

    def f_blk_dve(eng, wcb=(cb_dep.h, cb_dep.n), wr0=v_r0, wd2=td20,
                  wcw=(cw_dep.h, cw_dep.n)):
        if L0_STAGE < 2:
            return
        eng.wait_ge(wcb[0], wcb[1])
        eng.wait_ge(wcw[0], wcw[1])
        eng.wait_ge(s_r0.h, wr0)
        eng.wait_ge(s_td2.h, wd2 + 2)
        for b in range(NB):
            if L0_STAGE < 3:
                continue
            eng.wait_ge(s_ipe.h, ipe0 + b * 24 + 24)
            if b == 0:
                eng.memset(xio[:, :, 0:3], 0.0)
            else:
                eng.tensor_copy(xio[:, :, 0:3], bufs["l0_halo"][:])
            eng.drain()
            eng.tensor_copy(bufs["l0_halo"][:], xio[:, :, 512:515])
            eng.drain().then_inc(s_hl.h, 1)
            for ct in range(24):
                u = b * 24 + ct
                if ct % 2 == 0:
                    eng.wait_ge(s_cv.h, cv0 + u)
                dg = bufs["l0_diag"] if ct % 2 == 0 else bufs["l0_diag2"]
                ins = eng.tensor_tensor(
                    dg[:],
                    bufs["ident"][:, None, :].broadcast_to([128, 4, 128]),
                    bufs["l0_cw"][:, ct, :, None].broadcast_to([128, 4, 128]),
                    ALU.mult)
                ins.then_inc(s_dg.h, 1)
            for i in range(2 if L0_STAGE >= 4 else 0):
                for g in range(16):
                    u = b * 32 + i * 16 + g
                    ctl, tc = divmod(g, 4)
                    eng.wait_ge(s_tr.h, tr0 + u + 1)
                    ps = pT[u % 2]
                    tt = b * 4 + tc
                    if i == 0:
                        ins = eng.tensor_tensor(
                            Itm[:, tc, ctl * 256:(ctl + 1) * 256]
                            .rearrange("p (a b) -> p a b", a=4),
                            ps[:, 0:256].rearrange("p (a b) -> p a b", a=4),
                            dtm[:, tt, ctl * 4:(ctl + 1) * 4][:, :, None]
                            .broadcast_to([128, 4, 64]), ALU.mult)
                    else:
                        ins = eng.tensor_copy(Otm[:, tc, ctl * 256:(ctl + 1) * 256],
                                              ps[:, 0:256])
                    ins.then_inc(s_tre.h, 1)
            for cl in range(NCH if L0_STAGE >= 5 else 0):
                c = b * NCH + cl
                par = c % 2
                tt = c // 2
                for hg in range(2):
                    v = c * 2 + hg
                    eng.wait_ge(s_gt.h, gt0 + v + 1)
                    eng.tensor_tensor(
                        msk[par * 64:par * 64 + 64, hg],
                        bufs["trimask"][par * 64:par * 64 + 64, None, :]
                        .broadcast_to([64, 8, 64]),
                        etm[par * 64:par * 64 + 64, tt, hg * 8:(hg + 1) * 8]
                        [:, :, None].broadcast_to([64, 8, 64]), ALU.mult)
                    eng.drain()
                    ins = eng.tensor_tensor(
                        mt[par * 64:par * 64 + 64, hg],
                        pS0[hg * 64:hg * 64 + 64, :]
                        .rearrange("p (a b) -> p a b", a=8),
                        msk[par * 64:par * 64 + 64, hg], ALU.mult)
                    ins.then_inc(s_mte.h, 1)
                eng.wait_ge(s_st.h, st0 + c + 1)
                eng.tensor_tensor(R[:], pS1[:, :].rearrange("p (a b) -> p a b", a=8),
                                  R[:], ALU.add)
                eng.drain()
                eng.tensor_copy(Rbf[:], R[:])
                eng.drain().then_inc(s_rup.h, 1)
                if par == 1:
                    eng.wait_ge(s_yd.h, yd0 + c + 1)
                    if b >= 1:
                        eng.wait_ge(s_op.h, op0 + (b - 1) * 8 + 4 + (c // 2) % 4 + 1)
                    for hg in range(2):
                        npair = (c // 2) * 2 + hg
                        if npair >= 1:
                            eng.wait_ge(s_ye.h, ye0 + npair)
                        sl = slice(hg * 512, (hg + 1) * 512)
                        eng.tensor_tensor(
                            yst[:, 0, :].rearrange("p (a b) -> p a b", a=8),
                            pW0[:, sl].rearrange("p (a b) -> p a b", a=8),
                            atm[:, tt, hg * 8:(hg + 1) * 8][:, :, None]
                            .broadcast_to([128, 8, 64]), ALU.mult)
                        eng.tensor_tensor(
                            yst[:, 1, :].rearrange("p (a b) -> p a b", a=8),
                            pW1[:, sl].rearrange("p (a b) -> p a b", a=8),
                            btm[:, tt, hg * 8:(hg + 1) * 8][:, :, None]
                            .broadcast_to([128, 8, 64]), ALU.mult)
                        eng.drain()
                        ins = eng.tensor_tensor(Ytm[:, tt % 4, sl], yst[:, 0, :],
                                                yst[:, 1, :], ALU.add)
                        ins.then_inc(s_ye.h, 1)
                    eng.drain()
            if L0_STAGE < 6:
                continue
            for half in range(2):
                for g in range(4):
                    u = b * 8 + half * 4 + g
                    eng.wait_ge(s_ope.h, ope0 + u + 1)
                    sl = slice(half * 512, (half + 1) * 512)
                    ins = eng.tensor_tensor(xres[:, b * 4 + g, sl],
                                            bufs["l0_opsc"][:, u % 2, :],
                                            xres[:, b * 4 + g, sl], ALU.add)
                    ins.then_inc(s_res.h, 1)
    P.v(f_blk_dve)
    if L0_STAGE == 1:
        return s_td2
    if L0_STAGE == 2:
        return s_ipe
    if L0_STAGE == 3:
        return s_cve
    if L0_STAGE == 4:
        return s_tre
    if L0_STAGE == 5:
        return s_rup
    return s_res



def declare_moe(P, bufs):
    bufs["mo_wgT"] = P.sb("mo_wgT", [128, 8, 8], BF16)
    bufs["mo_wgraw"] = P.sb("mo_wgraw", [8, 1024], BF16)
    bufs["mo_lg"] = P.sb("mo_lg", [128, 16, 8], F32)
    bufs["mo_g1"] = P.sb("mo_g1", [128, 16, 8], F32)
    bufs["mo_g2"] = P.sb("mo_g2", [128, 16, 8], F32)
    bufs["mo_gs"] = P.sb("mo_gs", [128, 6, 16], F32)
    bufs["mo_gates"] = P.sb("mo_gates", [128, 16, 8], F32)
    bufs["mo_woT"] = P.sb("mo_woT", [128, 32, 1024], BF16)
    bufs["mo_win"] = P.sb_at("mo_win", [128, 32, 1024], BF16, P.offsets["xres"])
    bufs["mo_wev"] = P.sb("mo_wev", [128, 8, 1024], BF16)
    bufs["mo_Wacc"] = P.sb_at("mo_Wacc", [128, 8, 1024], F32, P.offsets["lnT"])
    bufs["mo_weT"] = P.sb_at("mo_weT", [128, 2, 8, 1024], BF16, P.offsets["mo_woT"])


def emit_moe_prelude(P, waps, wo_bf, pace=None):
    wob = [P.sem(dma=True), P.sem(dma=True)]
    n_chunks = 64

    def f_wob(eng):
        for k in range(n_chunks):
            e, r = divmod(k, 8)
            if pace is not None:
                h, base, total = pace
                eng.wait_ge(h, base + min((k * total) // n_chunks + 1, total))
            if k >= 2:
                eng.wait_ge(wob[k % 2].h, 16 * (k // 2))
            eng.dma_start(out=wo_bf[e][r * 128:(r + 1) * 128, :],
                          in_=waps["w_out"][e][r * 128:(r + 1) * 128, :]
                          ).then_inc(wob[k % 2].h, 16)
    P.g(f_wob)
    return wob


def emit_moe(P, bufs, waps, x_dep, weT_dram, wo_bf, spill_ap, pp_wob):
    xres = bufs["xres"]
    lnT = bufs["lnT"]
    woT, win, wev = bufs["mo_woT"], bufs["mo_win"], bufs["mo_wev"]
    Wacc = bufs["mo_Wacc"]
    pT0 = bufs["psumT0"]
    pW = [bufs["psumW0"], bufs["psumW1"]]
    pSb = [bufs["psumS0"], bufs["psumS1"]]

    s_wTh = [P.sem(dma=True), P.sem(dma=True)]
    wTh0 = [s.n for s in s_wTh]
    for s in s_wTh:
        s.inc(16 * 8 * 16)
    pp_winl = PPSem(P)
    s_msp = P.sem(dma=True)
    v_msp = s_msp.inc(16)
    s_mre = P.sem(dma=True)
    mre_sem = Sem(s_mre.h)
    v_mre = s_mre.inc(16)
    mre_sem.n = v_mre
    s_pmm = P.sem()
    pmm0 = s_pmm.n
    s_pmm.inc(8 * 16)
    s_pac = P.sem()
    pac0 = s_pac.n
    s_pac.inc(8 * 8)
    s_pdv = P.sem()
    pdv0 = s_pdv.n
    s_pdv.inc(8 * 8)
    pp_park = PPSem(P)

    def f_spill(eng, wx=(x_dep.h, x_dep.n)):
        eng.wait_ge(wx[0], wx[1])
        eng.dma_start(out=spill_ap, in_=xres[:]).then_inc(s_msp.h, 16)
    P.y(f_spill)

    ln_ctx = emit_ln_AB(P, bufs, x_dep)
    LN_A_BOX = ln_ctx["a_done"]

    def f_dma_pre(eng, wsp=v_msp, wx=(x_dep.h, x_dep.n), m0=pmm0,
                  wln=LN_A_BOX):
        eng.wait_ge(wx[0], wx[1])
        eng.wait_ge(s_msp.h, wsp)
        for h_, v_ in wln:
            eng.wait_ge(h_, v_)
        for s in range(16):
            e, half = divmod(s, 2)
            if s >= 2:
                eng.wait_ge(s_pmm.h, m0 + (s - 1) * 8)
            eng.dma_start(out=win[:, half * 16:(half + 1) * 16, :],
                          in_=waps["w_in"][e].rearrange("(a p) d -> p a d", p=128)
                          [:, half * 16:(half + 1) * 16, :]
                          ).then_inc(pp_winl.inc(s), 16)
    P.g(f_dma_pre)

    def f_trans(eng, wx=(x_dep.h, x_dep.n), m0=pmm0):
        eng.wait_ge(wx[0], wx[1])
        for e in range(8):
            eng.wait_ge(pp_wob[0].h, 16 * 4 * (e + 1))
            eng.wait_ge(pp_wob[1].h, 16 * 4 * (e + 1))
            for half in range(2):
                if e >= 1:
                    eng.wait_ge(s_pmm.h, m0 + (e - 1) * 16 + (half + 1) * 8)
                for fs in range(half * 16, half * 16 + 16):
                    if fs % 4 == 0 and fs != half * 16:
                        eng.wait_ge(s_wTh[half].h,
                                    wTh0[half] + 16 * (e * 16 + (fs - half * 16)))
                    eng.dma_start(out=woT[:, fs, :],
                                  in_=wo_bf[e][:, fs * 128:(fs + 1) * 128],
                                  transpose=True).then_inc(s_wTh[half].h, 16)
    P.y(f_trans)

    def f_restore(eng):
        eng.wait_ge(s_pmm.h, pmm0 + 128)
        eng.dma_start(out=xres[:], in_=spill_ap).then_inc(s_mre.h, 16)
    P.y(f_restore)

    def f_pre_pe(eng, a0=pac0, d0=pdv0, wx=(x_dep.h, x_dep.n)):
        eng.wait_ge(bufs["ld_const"].h, bufs["ld_const"].n)
        eng.wait_ge(wx[0], wx[1])
        for e in range(8):
            for half in range(2):
                sl = e * 2 + half
                eng.wait_ge(*pp_winl.tgt(sl))
                for dt in range(8):
                    v = sl * 8 + dt
                    if v >= 3:
                        pv = v - 3
                        pe_, pdt = pv // 16, pv % 8
                        if (pv % 16) // 8 == 0:
                            eng.wait_ge(s_pac.h, a0 + pe_ * 8 + pdt + 1)
                        else:
                            eng.wait_ge(s_pdv.h, d0 + pe_ * 8 + pdt + 1)
                    sel = v % 3
                    for fs in range(16):
                        if dt == 0 and fs % 4 == 0:
                            eng.wait_ge(s_wTh[half].h,
                                        wTh0[half] + 16 * (e * 16 + fs + 4))
                        fg = half * 16 + fs
                        for nh in range(2):
                            out = (pW[sel][:, nh * 512:(nh + 1) * 512]
                                   if sel < 2 else pSb[nh][:, :])
                            ins = eng.matmul(out,
                                             win[:, fg, dt * 128:(dt + 1) * 128],
                                             woT[:, fg, nh * 512:(nh + 1) * 512],
                                             start=(fs == 0), stop=(fs == 15))
                    ins.then_inc(s_pmm.h, 1)
    P.t(f_pre_pe)

    def f_pre_act(eng, m0=pmm0, d0=pdv0, wx=(x_dep.h, x_dep.n)):
        eng.wait_ge(wx[0], wx[1])
        for e in range(8):
            for dt in range(8):
                v = e * 16 + dt
                eng.wait_ge(s_pmm.h, m0 + v + 1)
                if e >= 1:
                    eng.wait_ge(s_pdv.h, d0 + (e - 1) * 8 + dt + 1)
                if v % 3 < 2:
                    ins = eng.activation(Wacc[:, dt, :], pW[v % 3][:, :],
                                         AF.Identity)
                else:
                    eng.activation(Wacc[:, dt, 0:512], pSb[0][:, :], AF.Identity)
                    ins = eng.activation(Wacc[:, dt, 512:1024], pSb[1][:, :],
                                         AF.Identity)
                ins.then_inc(s_pac.h, 1)
            if e >= 1:
                for dt in range(8):
                    u = (e - 1) * 8 + dt
                    eng.wait_ge(s_pdv.h, d0 + u + 1)
                    if u >= 2:
                        eng.wait_ge(pp_park.sems[u % 2].h, 16 * (u // 2))
                    eng.dma_start(out=weT_dram[e - 1, :, dt, :],
                                  in_=wev[:, dt, :]).then_inc(pp_park.inc(u), 16)
        for dt in range(8):
            u = 7 * 8 + dt
            eng.wait_ge(s_pdv.h, d0 + u + 1)
            eng.wait_ge(pp_park.sems[u % 2].h, 16 * (u // 2))
            eng.dma_start(out=weT_dram[7, :, dt, :],
                          in_=wev[:, dt, :]).then_inc(pp_park.inc(u), 16)
    P.s(f_pre_act)

    def f_pre_dve(eng, m0=pmm0, a0=pac0):
        for e in range(8):
            for dt in range(8):
                v = e * 16 + 8 + dt
                u = e * 8 + dt
                eng.wait_ge(s_pmm.h, m0 + v + 1)
                eng.wait_ge(s_pac.h, a0 + u + 1)
                if u >= 8:
                    eng.wait_ge(*pp_park.tgt(u - 8))
                if v % 3 < 2:
                    ins = eng.tensor_tensor(wev[:, dt, :], pW[v % 3][:, :],
                                            Wacc[:, dt, :], ALU.add)
                else:
                    eng.tensor_tensor(wev[:, dt, 0:512], pSb[0][:, :],
                                      Wacc[:, dt, 0:512], ALU.add)
                    ins = eng.tensor_tensor(wev[:, dt, 512:1024], pSb[1][:, :],
                                            Wacc[:, dt, 512:1024], ALU.add)
                ins.then_inc(s_pdv.h, 1)
    P.v(f_pre_dve)

    ln_dep = emit_ln_CD(P, bufs, ln_ctx, psum_dep=(x_dep.h, x_dep.n),
                        cx_dep=mre_sem)

    s_wg = P.sem(dma=True)
    v_wg = s_wg.inc(16)

    def f_ldwg(eng):
        eng.dma_start(out=bufs["mo_wgraw"][:], in_=waps["w_gate"]).then_inc(s_wg.h, 16)
    P.g(f_ldwg)

    s_gpe = P.sem()
    gpe0 = s_gpe.n
    s_gpe.inc(17)
    s_gev = P.sem()
    gev0 = s_gev.n
    s_gev.inc(17)

    def f_g_pe(eng, w=v_wg, wl=ln_dep.n, g0=gev0, a0=pac0, d0=pdv0):
        eng.wait_ge(s_wg.h, w)
        eng.wait_ge(ln_dep.h, wl)
        eng.wait_ge(s_pac.h, a0 + 64)
        eng.wait_ge(s_pdv.h, d0 + 64)
        for a in range(8):
            ins = eng.transpose(pT0[:, a * 8:(a + 1) * 8], bufs["mo_wgraw"][:, a * 128:(a + 1) * 128],
                                bufs["ident"][0:8, 0:8])
        ins.then_inc(s_gpe.h, 1)
        eng.wait_ge(s_gev.h, g0 + 1)
        for tt in range(NT):
            if tt >= 2 and tt % 2 == 0:
                eng.wait_ge(s_gev.h, g0 + min(tt + 1, 17))
            ps = pSb[tt % 2]
            for dt in range(ND):
                ins = eng.matmul(ps[:, 0:8], lnT[:, dt, tt * 128:(tt + 1) * 128],
                                 bufs["mo_wgT"][:, dt, :],
                                 start=(dt == 0), stop=(dt == ND - 1))
            ins.then_inc(s_gpe.h, 1)
    P.t(f_g_pe)

    def f_g_ev(eng, g0=gpe0):
        eng.wait_ge(s_gpe.h, g0 + 1)
        ins = eng.tensor_copy(bufs["mo_wgT"][:, :, :],
                              pT0[:, 0:64].rearrange("p (a b) -> p a b", a=8))
        ins.then_inc(s_gev.h, 1)
        for tt in range(NT):
            if tt % 2 == 0:
                eng.wait_ge(s_gpe.h, g0 + min(tt + 3, 17))
            ins = eng.tensor_copy(bufs["mo_lg"][:, tt, :], pSb[tt % 2][:, 0:8])
            ins.then_inc(s_gev.h, 1)
    P.v(f_g_ev)

    lg, eqm, tmp = bufs["mo_lg"], bufs["mo_g1"], bufs["mo_g2"]
    gs = bufs["mo_gs"]
    m1n, m2n, dd, qq, invd, glo = (gs[:, i, :] for i in range(6))
    s_gate = P.sem()
    v_gate = s_gate.inc()
    s_gq = P.sem()
    v_gq = s_gq.inc()

    def f_gates_a(eng, g0=gev0):
        eng.wait_ge(s_gev.h, g0 + 17)
        eng.reduce_max(m1n, lg[:], AX.X, negate=True)
        eng.drain()
        eng.tensor_tensor(eqm[:], lg[:], m1n[:, :, None].broadcast_to([128, 16, 8]), ALU.add)
        eng.drain()
        eng.tensor_single_scalar(eqm[:], eqm[:], 0.0, ALU.is_ge)
        eng.drain()
        eng.scalar_tensor_tensor(tmp[:], eqm[:], -1e30, lg[:], ALU.mult, ALU.add)
        eng.drain()
        eng.reduce_max(m2n, tmp[:], AX.X, negate=True)
        eng.drain()
        eng.tensor_tensor(tmp[:], tmp[:], m2n[:, :, None].broadcast_to([128, 16, 8]), ALU.add)
        eng.drain()
        eng.tensor_single_scalar(tmp[:], tmp[:], 0.0, ALU.is_ge)
        eng.drain()
        eng.tensor_tensor(dd, m1n, m2n, ALU.subtract)
        eng.drain().then_inc(s_gate.h, 1)
    P.v(f_gates_a)

    def f_gates_q(eng, w=v_gate):
        eng.wait_ge(s_gate.h, w)
        eng.activation(qq, dd, AF.Exp)
        eng.drain().then_inc(s_gq.h, 1)
    P.s(f_gates_q)

    s_gate2 = P.sem()
    v_gate2 = s_gate2.inc()

    def f_gates_b(eng, w=v_gq):
        eng.wait_ge(s_gq.h, w)
        eng.tensor_scalar_add(invd, qq, 1.0)
        eng.drain()
        eng.reciprocal(invd, invd)
        eng.drain()
        eng.tensor_tensor(glo, qq, invd, ALU.mult)
        eng.drain()
        eng.tensor_tensor(bufs["mo_gates"][:], eqm[:],
                          invd[:, :, None].broadcast_to([128, 16, 8]), ALU.mult)
        eng.drain()
        eng.tensor_tensor(tmp[:], tmp[:], glo[:, :, None].broadcast_to([128, 16, 8]),
                          ALU.mult)
        eng.drain()
        eng.tensor_tensor(bufs["mo_gates"][:], bufs["mo_gates"][:], tmp[:], ALU.add)
        eng.drain().then_inc(s_gate2.h, 1)
    P.v(f_gates_b)

    weT = bufs["mo_weT"]
    pp_awl = PPSem(P)
    s_amm = P.sem()
    amm0 = s_amm.n
    s_amm.inc(8 * NT)
    s_aev = P.sem()
    aev0 = s_aev.n
    s_aev.inc(8 * NT)

    def f_ald(eng, a0=amm0, pk=(pp_park.sems[0].n, pp_park.sems[1].n)):
        eng.wait_ge(pp_park.sems[0].h, pk[0] + 16 * 32)
        eng.wait_ge(pp_park.sems[1].h, pk[1] + 16 * 32)
        for e in range(8):
            if e >= 2:
                eng.wait_ge(s_amm.h, a0 + (e - 1) * NT)
            eng.dma_start(out=weT[:, e % 2], in_=weT_dram[e]).then_inc(pp_awl.inc(e), 16)
    P.y(f_ald)

    def f_apply_pe(eng, a0=aev0, wl=ln_dep.n, pa=pac0, pd=pdv0):
        eng.wait_ge(ln_dep.h, wl)
        eng.wait_ge(s_pac.h, pa + 64)
        eng.wait_ge(s_pdv.h, pd + 64)
        for e in range(8):
            eng.wait_ge(*pp_awl.tgt(e))
            for tt in range(NT):
                j = e * NT + tt
                if j == 2:
                    eng.wait_ge(s_gev.h, gev0 + 17)
                if j >= 3:
                    eng.wait_ge(s_aev.h, a0 + j - 2)
                sel = j % 3
                for dt in range(ND):
                    for half in range(2):
                        out = (pW[sel][:, half * 512:(half + 1) * 512]
                               if sel < 2 else pSb[half][:, :])
                        ins = eng.matmul(out,
                                         lnT[:, dt, tt * 128:(tt + 1) * 128],
                                         weT[:, e % 2, dt, half * 512:(half + 1) * 512],
                                         start=(dt == 0), stop=(dt == ND - 1))
                ins.then_inc(s_amm.h, 1)
    P.t(f_apply_pe)

    def f_apply_ev(eng, m0=amm0, g2=v_gate2):
        eng.wait_ge(s_gate2.h, g2)
        for e in range(8):
            for tt in range(NT):
                j = e * NT + tt
                eng.wait_ge(s_amm.h, m0 + j + 1)
                sel = j % 3
                if sel < 2:
                    ins = eng.scalar_tensor_tensor(
                        xres[:, tt, :], pW[sel][:, :],
                        bufs["mo_gates"][:, tt, e:e + 1],
                        xres[:, tt, :], ALU.mult, ALU.add)
                else:
                    eng.scalar_tensor_tensor(
                        xres[:, tt, 0:512], pSb[0][:, :],
                        bufs["mo_gates"][:, tt, e:e + 1],
                        xres[:, tt, 0:512], ALU.mult, ALU.add)
                    ins = eng.scalar_tensor_tensor(
                        xres[:, tt, 512:1024], pSb[1][:, :],
                        bufs["mo_gates"][:, tt, e:e + 1],
                        xres[:, tt, 512:1024], ALU.mult, ALU.add)
                ins.then_inc(s_aev.h, 1)
    P.v(f_apply_ev)
    return s_aev



def build_l1ffn_program():
    nc = bass.Bass("TRN2", target_bir_lowering=False)
    x_h = nc.dram_tensor("x", [T, D], F32, kind="ExternalInput").ap()
    out_h = nc.dram_tensor("out", [T, D], F32, kind="ExternalOutput").ap()
    haps = {"x": x_h}
    for name, (shape, dt) in CONST_SPECS.items():
        haps[name] = nc.dram_tensor(name, shape, dt, kind="ExternalInput").ap()
    waps = {}
    for nm, shape in [("U1", [1024, 256]), ("V1", [256, 4096]), ("b1", [4096]),
                      ("U2", [4096, 256]), ("V2", [256, 1024]), ("b2", [1024])]:
        waps[nm] = nc.dram_tensor(nm, shape, F32, kind="ExternalInput").ap()

    with contextlib.ExitStack() as stack:
        P = Prog(nc, stack)
        bufs = declare_common(P)
        declare_l1ffn(P, bufs)
        assert P.sbuf_off < 207 * 1024, P.sbuf_off
        emit_load_consts(P, bufs, haps)
        ln_dep = emit_ln(P, bufs, bufs["ld_const"])
        res = emit_l1ffn(P, bufs, waps, ln_dep)
        emit_store_x(P, bufs, out_h, res)
        P.finish()
    return nc


def build_l0_program():
    nc = bass.Bass("TRN2", target_bir_lowering=False)
    x_h = nc.dram_tensor("x", [T, D], F32, kind="ExternalInput").ap()
    out_h = nc.dram_tensor("out", [T, D], F32, kind="ExternalOutput").ap()
    haps = {"x": x_h}
    for name, (shape, dt) in CONST_SPECS.items():
        haps[name] = nc.dram_tensor(name, shape, dt, kind="ExternalInput").ap()
    waps = {
        "in_proj_w": nc.dram_tensor("in_proj_w", [3088, 1024], F32, kind="ExternalInput").ap(),
        "conv_w": nc.dram_tensor("conv_w", [3072, 1, 4], F32, kind="ExternalInput").ap(),
        "conv_b": nc.dram_tensor("conv_b", [3072], F32, kind="ExternalInput").ap(),
        "w_base": nc.dram_tensor("w_base", [16], F32, kind="ExternalInput").ap(),
        "out_proj_w": nc.dram_tensor("out_proj_w", [1024, 1024], F32, kind="ExternalInput").ap(),
    }
    ipT = nc.dram_tensor("ipT", [128, 8, 3088], BF16, kind="Internal").ap()
    opT = nc.dram_tensor("opT", [128, 8, 1024], BF16, kind="Internal").ap()

    with contextlib.ExitStack() as stack:
        P = Prog(nc, stack)
        bufs = declare_common(P)
        declare_l0(P, bufs)
        assert P.sbuf_off < Prog.ARENA_BYTES, P.sbuf_off
        emit_load_consts(P, bufs, haps)
        prep, cwl, l2d, evd = emit_l0_weightprep(P, bufs, waps, ipT, opT)
        ln_dep = emit_ln(P, bufs, bufs["ld_const"], psum_dep=(evd.h, evd.n))
        res = emit_l0attn(P, bufs, waps, ln_dep, bufs["ld_const"], ipT, opT, prep,
                          cwl, l2d)
        emit_store_x(P, bufs, out_h, res)
        P.finish()
    return nc


def build_moe_program():
    nc = bass.Bass("TRN2", target_bir_lowering=False)
    x_h = nc.dram_tensor("x", [T, D], F32, kind="ExternalInput").ap()
    out_h = nc.dram_tensor("out", [T, D], F32, kind="ExternalOutput").ap()
    haps = {"x": x_h}
    for name, (shape, dt) in CONST_SPECS.items():
        haps[name] = nc.dram_tensor(name, shape, dt, kind="ExternalInput").ap()
    waps = {
        "w_gate": nc.dram_tensor("w_gate", [8, 1024], F32, kind="ExternalInput").ap(),
        "w_in": nc.dram_tensor("w_in", [8, 4096, 1024], F32, kind="ExternalInput").ap(),
        "w_out": nc.dram_tensor("w_out", [8, 1024, 4096], F32, kind="ExternalInput").ap(),
    }
    weT_dram = nc.dram_tensor("weT", [8, 128, 8, 1024], BF16, kind="Internal").ap()
    wo_bf = nc.dram_tensor("wo_bf", [8, 1024, 4096], BF16, kind="Internal").ap()
    spill = nc.dram_tensor("spill", [128, NT, 1024], F32, kind="Internal").ap()

    with contextlib.ExitStack() as stack:
        P = Prog(nc, stack)
        bufs = declare_common(P)
        declare_moe(P, bufs)
        assert P.sbuf_off < Prog.ARENA_BYTES, P.sbuf_off
        emit_load_consts(P, bufs, haps)
        pp_wob = emit_moe_prelude(P, waps, wo_bf)
        res = emit_moe(P, bufs, waps, bufs["ld_const"], weT_dram, wo_bf, spill, pp_wob)
        emit_store_x(P, bufs, out_h, res)
        P.finish()
    return nc


def build_l1attn_program(dbg=False):
    nc = bass.Bass("TRN2", target_bir_lowering=False)
    x_h = nc.dram_tensor("x", [T, D], F32, kind="ExternalInput").ap()
    out_h = nc.dram_tensor("out", [T, D], F32, kind="ExternalOutput").ap()
    haps = {"x": x_h}
    for name, (shape, dt) in CONST_SPECS.items():
        haps[name] = nc.dram_tensor(name, shape, dt, kind="ExternalInput").ap()
    waps = {}
    for nm, shape in [("Wqkv", [1024, 3072]), ("bqkv", [3072]),
                      ("Wout", [1024, 1024]), ("bout", [1024])]:
        waps[nm] = nc.dram_tensor(nm, shape, F32, kind="ExternalInput").ap()
    spill = nc.dram_tensor("spill", [128, NT, 1024], F32, kind="Internal").ap()

    with contextlib.ExitStack() as stack:
        P = Prog(nc, stack)
        bufs = declare_common(P)
        declare_l1attn(P, bufs)
        assert P.sbuf_off < Prog.ARENA_BYTES, P.sbuf_off
        emit_load_consts(P, bufs, haps)
        ln_dep = emit_ln(P, bufs, bufs["ld_const"])
        res = emit_l1attn(P, bufs, waps, ln_dep, bufs["ld_const"], spill,
                          dbg_stop=dbg)
        if dbg:
            dsem = P.sem()
            names = [("at_QT", [128, 8, 2048]), ("at_KT", [128, 8, 2048]),
                     ("at_Vtm", [128, 16, 16, 65]), ("at_OT", [128, 8, 2048])]
            daps = {n: nc.dram_tensor("dbg_" + n, s, BF16, kind="ExternalOutput").ap()
                    for n, s in names}
            tgt = dsem.inc(16 * len(names))

            def f_dump(eng, w=res.n):
                eng.wait_ge(res.h, w)
                for n, s in names:
                    eng.dma_start(out=daps[n], in_=bufs[n][:]).then_inc(dsem.h, 16)
                eng.wait_ge(dsem.h, tgt)
            P.g(f_dump)
            res = dsem
        emit_store_x(P, bufs, out_h, res)
        P.finish()
    return nc


def build_fused_program(upto=4):
    nc = bass.Bass("TRN2", target_bir_lowering=False)
    x_h = nc.dram_tensor("x", [T, D], F32, kind="ExternalInput").ap()
    out_h = nc.dram_tensor("out", [T, D], F32, kind="ExternalOutput").ap()
    haps = {"x": x_h}
    for name, (shape, dt) in CONST_SPECS.items():
        haps[name] = nc.dram_tensor(name, shape, dt, kind="ExternalInput").ap()
    wl0 = {nm: nc.dram_tensor(nm, sh, F32, kind="ExternalInput").ap()
           for nm, sh in [("in_proj_w", [3088, 1024]), ("conv_w", [3072, 1, 4]),
                          ("conv_b", [3072]), ("w_base", [16]),
                          ("out_proj_w", [1024, 1024])]}
    wmo = {nm: nc.dram_tensor(nm, sh, F32, kind="ExternalInput").ap()
           for nm, sh in [("w_gate", [8, 1024]), ("w_in", [8, 4096, 1024]),
                          ("w_out", [8, 1024, 4096])]}
    wat = {nm: nc.dram_tensor(nm, sh, F32, kind="ExternalInput").ap()
           for nm, sh in [("Wqkv", [1024, 3072]), ("bqkv", [3072]),
                          ("Wout", [1024, 1024]), ("bout", [1024])]}
    wff = {nm: nc.dram_tensor(nm, sh, F32, kind="ExternalInput").ap()
           for nm, sh in [("U1", [1024, 256]), ("V1", [256, 4096]), ("b1", [4096]),
                          ("U2", [4096, 256]), ("V2", [256, 1024]), ("b2", [1024])]}
    ipT = nc.dram_tensor("ipT", [128, 8, 3088], BF16, kind="Internal").ap()
    opT = nc.dram_tensor("opT", [128, 8, 1024], BF16, kind="Internal").ap()
    weT_dram = nc.dram_tensor("weT", [8, 128, 8, 1024], BF16, kind="Internal").ap()
    wo_bf = nc.dram_tensor("wo_bf", [8, 1024, 4096], BF16, kind="Internal").ap()
    spill = nc.dram_tensor("spill", [128, NT, 1024], F32, kind="Internal").ap()

    with contextlib.ExitStack() as stack:
        P = Prog(nc, stack)
        bufs = declare_common(P)
        phase_base = P.mark()
        declare_l0(P, bufs)
        assert P.sbuf_off <= Prog.ARENA_BYTES
        P.reset_to(phase_base)
        declare_moe(P, bufs)
        assert P.sbuf_off <= Prog.ARENA_BYTES
        P.reset_to(phase_base)
        declare_l1attn(P, bufs)
        assert P.sbuf_off <= Prog.ARENA_BYTES
        P.reset_to(phase_base)
        declare_l1ffn(P, bufs)
        assert P.sbuf_off <= Prog.ARENA_BYTES

        emit_load_consts(P, bufs, haps)
        prep, cwl, l2d, evd = emit_l0_weightprep(P, bufs, wl0, ipT, opT)
        ln0 = emit_ln(P, bufs, bufs["ld_const"], psum_dep=(evd.h, evd.n))
        r = emit_l0attn(P, bufs, wl0, ln0, bufs["ld_const"], ipT, opT, prep,
                        cwl, l2d)
        pp_wob = emit_moe_prelude(P, wmo, wo_bf, pace=P.l0_pace) if upto >= 2 else None
        if upto >= 2:
            r = emit_moe(P, bufs, wmo, r, weT_dram, wo_bf, spill, pp_wob)
        if upto >= 3:
            ln2 = emit_ln(P, bufs, r, psum_dep=(r.h, r.n), tile_dep=True)
            r = emit_l1attn(P, bufs, wat, ln2, r, spill)
        if upto >= 4:
            ln3 = emit_ln(P, bufs, r, psum_dep=(r.h, r.n), tile_dep=True)
            r = emit_l1ffn(P, bufs, wff, ln3, ld_dep=(r.h, r.n))
        emit_store_x(P, bufs, out_h, r)
        P.finish()
    return nc


_PROG_CACHE = {}


def kernel(**inputs):
    consts = host_constants()
    if "fused" not in _PROG_CACHE:
        _PROG_CACHE["fused"] = build_fused_program()
    nc = _PROG_CACHE["fused"]
    x = np.ascontiguousarray(inputs["L0_attn_x"], dtype=np.float32)
    wmap = {
        "in_proj_w": inputs["L0_attn_in_proj_w"],
        "conv_w": inputs["L0_attn_conv_w"],
        "conv_b": inputs["L0_attn_conv_b"],
        "w_base": inputs["L0_attn_w_base"],
        "out_proj_w": inputs["L0_attn_out_proj_w"],
        "w_gate": inputs["L0_ffn_w_gate"],
        "w_in": inputs["L0_ffn_w_in"],
        "w_out": inputs["L0_ffn_w_out"],
        "Wqkv": inputs["L1_attn_Wqkv"],
        "bqkv": inputs["L1_attn_bqkv"],
        "Wout": inputs["L1_attn_Wout"],
        "bout": inputs["L1_attn_bout"],
        "U1": inputs["L1_ffn_U1"],
        "V1": inputs["L1_ffn_V1"],
        "b1": inputs["L1_ffn_b1"],
        "U2": inputs["L1_ffn_U2"],
        "V2": inputs["L1_ffn_V2"],
        "b2": inputs["L1_ffn_b2"],
    }
    wmap = {k: np.ascontiguousarray(v, dtype=np.float32) for k, v in wmap.items()}
    in_maps = []
    for c in range(NCORES):
        m = {"x": np.ascontiguousarray(x[c])}
        m.update(consts)
        m.update(wmap)
        in_maps.append(m)
    res = run_bass_kernel_spmd(nc, in_maps, core_ids=list(range(NCORES)))
    return np.stack([r["out"] for r in res.results], axis=0)
